# version 23
# baseline (speedup 1.0000x reference)
import numpy as np
import ml_dtypes
from contextlib import ExitStack

import concourse.bass as bass
import concourse.mybir as mybir
from concourse.bass_core_rust import BassInstruction
from concourse.bass_utils import run_bass_kernel_spmd

F32 = mybir.dt.float32
BF16 = mybir.dt.bfloat16
ACT = mybir.ActivationFunctionType
ALU = mybir.AluOpType
AXX = mybir.AxisListType.X

DEBUG_TAPS = False

B, T, D = 4, 4096, 1024
HEADS, DH, NPAIR = 16, 64, 8
DFF, RANK = 4096, 256
EPS = 1e-5
TC, NCH, CH = 2048, 16, 128
NSC, SCT = 4, 512
NKC = 8
BF = ml_dtypes.bfloat16


def build_program():
    nc = bass.Bass("TRN2", target_bir_lowering=False, detect_race_conditions=False)

    d_x = nc.dram_tensor("x", [TC, D], F32, kind="ExternalInput").ap()
    d_wqkv = nc.dram_tensor("wqkv", [D, 3 * D], F32, kind="ExternalInput").ap()
    d_wg = nc.dram_tensor("wg", [D, D], F32, kind="ExternalInput").ap()
    d_wp = nc.dram_tensor("wp", [D, D], F32, kind="ExternalInput").ap()
    d_u1 = nc.dram_tensor("u1", [D, RANK], F32, kind="ExternalInput").ap()
    d_v1 = nc.dram_tensor("v1", [RANK, DFF], F32, kind="ExternalInput").ap()
    d_u2 = nc.dram_tensor("u2", [DFF, RANK], F32, kind="ExternalInput").ap()
    d_v2 = nc.dram_tensor("v2", [RANK, D], F32, kind="ExternalInput").ap()
    d_lng = nc.dram_tensor("lng", [128, 8], F32, kind="ExternalInput").ap()
    d_lnb = nc.dram_tensor("lnb", [128, 8], F32, kind="ExternalInput").ap()
    d_bqkv = nc.dram_tensor("bqkv", [1, 3 * D], F32, kind="ExternalInput").ap()
    d_bg = nc.dram_tensor("bg", [1, D], F32, kind="ExternalInput").ap()
    d_bp = nc.dram_tensor("bp", [1, D], F32, kind="ExternalInput").ap()
    d_b1 = nc.dram_tensor("b1c", [128, 32], F32, kind="ExternalInput").ap()
    d_b2 = nc.dram_tensor("b2", [1, D], F32, kind="ExternalInput").ap()
    d_id16 = nc.dram_tensor("id16", [128, 128], BF16, kind="ExternalInput").ap()
    d_id32 = nc.dram_tensor("id32", [128, 128], F32, kind="ExternalInput").ap()
    d_mask = nc.dram_tensor("mask", [128, 128], F32, kind="ExternalInput").ap()
    d_wprev = nc.dram_tensor("wprev", [128, 1], F32, kind="ExternalInput").ap()
    d_out_dram = nc.dram_tensor("out", [TC, D], F32, kind="ExternalOutput").ap()

    d_q4spill = nc.dram_tensor("q4spill", [NCH, 128, D], BF16)
    d_ccin = nc.dram_tensor("ccin", [128, 520], F32)
    d_ccout = nc.dram_tensor("ccout", [2, 128, 520], F32)

    taps = {}
    if DEBUG_TAPS:
        for nm, shp, dt in [("t_z", [128, D], BF16), ("t_gate", [128, D], F32),
                            ("t_q4", [128, D], BF16), ("t_k4", [128, D], BF16),
                            ("t_v", [128, HEADS * 65], BF16), ("t_snap", [128, 520], BF16),
                            ("t_o", [128, D], BF16), ("t_x2", [128, D], F32),
                            ("t_ofm", [128, 256], BF16)]:
            taps[nm] = nc.dram_tensor(nm, shp, dt, kind="ExternalOutput").ap()

    last_inst = {}
    nc.push_inst_callback(lambda ins: last_inst.__setitem__(ins.engine, ins))
    cnt = {"DS": 0, "DG": 0, "TT": 0, "VV": 0, "SS": 0, "CC": 0, "ST": 0,
       "DSX0": 0, "DSX1": 0, "DSP": 0, "DSQ0": 0, "DSQ1": 0, "DSXD": 0, "DSO": 0, "GG": 0,
       "DGE": 0, "DG2": 0, "DSO1": 0}

    with ExitStack() as top:
        e = top.enter_context
        sems = {n: e(nc.semaphore(n)) for n in cnt}
        DS, DG, TT, VV, SS, CC, ST = (sems[n] for n in ["DS", "DG", "TT", "VV", "SS", "CC", "ST"])

        def bump(name, amt=1):
            cnt[name] += amt
            return cnt[name]

        def mark(eng, name):
            BassInstruction(last_inst[eng.engine]).then_inc(sems[name], 1)
            return bump(name)

        def wtok(eng, tok):
            eng.wait_ge(sems[tok[0]], tok[1])

        def chain(eng, name="VV"):
            BassInstruction(last_inst[eng.engine]).then_inc(sems[name], 1)
            bump(name)
            eng.wait_ge(sems[name], cnt[name])

        sb_k4fm = e(nc.sbuf_tensor("k4fm", [128, NKC * TC], BF16))
        sb_vtok = e(nc.sbuf_tensor("vtok", [128, NCH * HEADS * 65], BF16))
        sb_snaps = e(nc.sbuf_tensor("snaps", [128, NCH * 520], BF16))
        sb_ident = e(nc.sbuf_tensor("sident", [128, 128], BF16))
        sb_id32 = e(nc.sbuf_tensor("sid32", [128, 128], F32))
        sb_mask = e(nc.sbuf_tensor("smask", [128, 128], F32))
        sb_lng = e(nc.sbuf_tensor("slng", [128, 8], F32))
        sb_lnb = e(nc.sbuf_tensor("lnbv", [128, 8], BF16))
        sb_b1c = e(nc.sbuf_tensor("sb1c", [128, 32], F32))
        sb_eps = e(nc.sbuf_tensor("epsc", [128, 1], F32))
        sb_eps2 = e(nc.sbuf_tensor("eps2c", [128, 1], F32))
        sb_wprev = e(nc.sbuf_tensor("swprev", [128, 1], F32))
        sb_ones = e(nc.sbuf_tensor("ones_row", [1, 128], BF16))
        sb_rp = e(nc.sbuf_tensor("srp", [1, D], BF16))
        sb_rb2 = e(nc.sbuf_tensor("srb2", [1, D], BF16))
        sb_exch = e(nc.sbuf_tensor("sexch", [128, 520], F32))

        block = e(nc.Block())

        @block.sync
        def _(s):
            s.dma_start(out=sb_ident[:], in_=d_id16).then_inc(DS, 16)
            s.dma_start(out=sb_id32[:], in_=d_id32).then_inc(DS, 16)
            s.dma_start(out=sb_mask[:], in_=d_mask).then_inc(DS, 16)
            s.dma_start(out=sb_lng[:], in_=d_lng).then_inc(DS, 16)
            s.dma_start(out=sb_b1c[:], in_=d_b1).then_inc(DS, 16)
            s.dma_start(out=sb_wprev[:], in_=d_wprev).then_inc(DS, 16)
            bump("DS", 6 * 16)
        SETUP_DS = cnt["DS"]

        @block.gpsimd
        def _(g):
            g.dma_start(out=sb_lnb[:], in_=d_lnb).then_inc(DG, 16)
            g.dma_start(out=sb_rp[:], in_=d_bp).then_inc(DG, 16)
            g.dma_start(out=sb_rb2[:], in_=d_b2).then_inc(DG, 16)
            bump("DG", 3 * 16)
            g.memset(sb_ones[:], 1.0)
            g.memset(sb_vtok[:], 1.0).then_inc(sems["GG"], 1)
            bump("GG", 1)
        SETUP_GG = cnt["GG"]

        @block.vector
        def _(v):
            v.memset(sb_eps[:], EPS)
            v.memset(sb_eps2[:], EPS * EPS)
            mark(v, "VV")
        EPS_VV = cnt["VV"]

        with ExitStack() as aa:
            ea = aa.enter_context
            sb_wqkv = ea(nc.sbuf_tensor("wqkv_bf", [128, NKC * 3 * D], BF16))
            sb_wgate = ea(nc.sbuf_tensor("wg_bf", [128, NKC * D], BF16))
            sb_rq = ea(nc.sbuf_tensor("srq", [1, 3 * D], BF16))
            sb_rg = ea(nc.sbuf_tensor("srg", [1, D], BF16))
            sb_r2row = ea(nc.sbuf_tensor("r2row", [1, 256], BF16))
            sb_x = ea(nc.sbuf_tensor("xa", [128, 2 * D], F32))
            sb_ztok = ea(nc.sbuf_tensor("ztok", [128, 2 * D], BF16))
            sb_gate = ea(nc.sbuf_tensor("gate", [128, D], F32))
            sb_gaten = ea(nc.sbuf_tensor("gaten", [128, D], F32))
            sb_tmp1 = ea(nc.sbuf_tensor("tmp1", [128, D], F32))
            sb_tmp2 = ea(nc.sbuf_tensor("tmp2", [128, D], F32))
            sb_q4tok = ea(nc.sbuf_tensor("q4tok", [128, 2 * D], BF16))
            sb_k4tok = ea(nc.sbuf_tensor("k4tok", [128, 2 * D], BF16))
            sb_q4st = ea(nc.sbuf_tensor("q4stage", [128, D], BF16))
            sb_stats = ea(nc.sbuf_tensor("stats", [128, 12], F32))
            sb_sm = ea(nc.sbuf_tensor("smalls", [128, 36], F32))
            psA = ea(nc.psum_tensor("psA", [128, 2048], F32))
            psT = ea(nc.psum_tensor("psT", [128, 2 * 1024], BF16))
            psSA = ea(nc.psum_tensor("psSA", [128, 512], F32))
            psSB = ea(nc.psum_tensor("psSB", [128, 512], F32))

            atok = {0: ("VV", 0), 1: ("VV", 0), 2: ("VV", 0), 3: ("VV", 0),
                    "t0": ("VV", 0), "r2": ("VV", 0)}

            @block.gpsimd
            def _(g):
                for kc in range(NKC):
                    g.dma_start(out=sb_wgate[:, kc * D:(kc + 1) * D],
                                in_=d_wg[kc * 128:(kc + 1) * 128, :]).then_inc(DG, 16)
                g.dma_start(out=sb_rg[:], in_=d_bg).then_inc(DG, 16)
                bump("DG", 9 * 16)
                for kc in range(NKC):
                    g.dma_start(out=sb_wqkv[:, kc * 3 * D:(kc + 1) * 3 * D],
                                in_=d_wqkv[kc * 128:(kc + 1) * 128, :],
                                max_dma_last_dim=8000).then_inc(DG, 16)
                bump("DG", 8 * 16)
            WQKV_DG = cnt["DG"]

            if True:
                bq_rows = [sb_gate, sb_gaten, sb_tmp1]

                @block.sync
                def _(s):
                    for i in range(3):
                        s.dma_start(out=bq_rows[i][0:1, 0:1024],
                                    in_=d_bqkv[:, i * 1024:(i + 1) * 1024]).then_inc(DS, 16)
                    bump("DS", 48)
                BQ_DS = cnt["DS"]

                rq_tt = []
                rq_vv = []
                rq_vv_base = cnt["VV"]

                @block.tensor
                def _(t):
                    t.wait_ge(DG, WQKV_DG)
                    for nw in range(6):
                        if nw >= 3:
                            t.wait_ge(VV, rq_vv_base + (nw - 3) + 1)
                        dst = psA[:1, (nw % 3) * 512:(nw % 3) * 512 + 512]
                        for kc in range(NKC):
                            t.matmul(dst, sb_lnb[:, kc:kc + 1],
                                     sb_wqkv[:, kc * 3 * D + nw * 512: kc * 3 * D + (nw + 1) * 512],
                                     start=(kc == 0), stop=(kc == NKC - 1))
                        rq_tt.append(mark(t, "TT"))

                @block.vector
                def _(v):
                    v.wait_ge(DS, BQ_DS)
                    for nw in range(6):
                        v.wait_ge(TT, rq_tt[nw])
                        v.tensor_add(sb_rq[:, nw * 512:(nw + 1) * 512],
                                     psA[:1, (nw % 3) * 512:(nw % 3) * 512 + 512],
                                     bq_rows[nw // 2][0:1, (nw % 2) * 512:(nw % 2) * 512 + 512])
                        rq_vv.append(mark(v, "VV"))
                        assert rq_vv[-1] == rq_vv_base + len(rq_vv)
                RQ_VV = cnt["VV"]
                for b_ in range(4):
                    atok[b_] = ("VV", RQ_VV)

            @block.scalar
            def _(s):
                s.wait_ge(TT, rq_tt[-1])
                s.wait_ge(DS, BQ_DS)
                for kc in range(NKC):
                    w = sb_wqkv[:, kc * 3 * D:(kc + 1) * 3 * D]
                    s.activation(w, w, ACT.Copy, scale=sb_lng[:, kc:kc + 1])
                mark(s, "SS")
            WSC_SS = cnt["SS"]

            snap_vv = {}
            state_tt = {}
            spill_ds = {}
            z_vv = {}
            stats_vv = {}
            sm_ss = {}
            zt_tt = {}
            gn_vv = {}
            q4k4_vv = [0]
            q4t_tt_prev = [0]
            k4s_tt_prev = [0]

            def emit_s1(c):
                par = c % 2
                smo = (c % 3) * 12
                xb = sb_x[:, par * D:(par + 1) * D]
                zt = sb_ztok[:, par * D:(par + 1) * D]
                dsx = "DSX0" if par == 0 else "DSX1"

                @block.sync
                def _(s):
                    if c >= 2:
                        s.wait_ge(VV, z_vv[c - 2])
                    s.dma_start(out=xb, in_=d_x[c * 128:(c + 1) * 128, :]).then_inc(sems[dsx], 16)
                    bump(dsx, 16)
                X_DS = cnt[dsx]

                @block.vector
                def _(v):
                    v.wait_ge(sems[dsx], X_DS)
                    if c >= 3:
                        v.wait_ge(TT, zt_tt[c - 3])
                    v.bn_stats(sb_stats[:, 0:6], xb[:, 0:512])
                    v.bn_stats(sb_stats[:, 6:12], xb[:, 512:1024])
                    chain(v)
                    v.bn_aggr(sb_sm[:, smo + 0:smo + 2], sb_stats[:, 0:12])
                    mark(v, "VV")
                stats_vv[c] = cnt["VV"]

                @block.scalar
                def _(s):
                    s.wait_ge(VV, stats_vv[c])
                    s.wait_ge(VV, EPS_VV)
                    s.activation(sb_sm[:, smo + 2:smo + 3], sb_sm[:, smo + 1:smo + 2], ACT.Sqrt,
                                 scale=float(1.0 + EPS), bias=sb_eps2[:, 0:1])
                    s.activation(sb_sm[:, smo + 3:smo + 4], sb_sm[:, smo + 1:smo + 2], ACT.Sqrt,
                                 bias=sb_eps[:, 0:1])
                    mark(s, "SS")
                sm_ss[c] = cnt["SS"]

                @block.vector
                def _(v):
                    v.wait_ge(SS, sm_ss[c])
                    v.reciprocal(sb_sm[:, smo + 8:smo + 9], sb_sm[:, smo + 2:smo + 3])
                    v.reciprocal(sb_sm[:, smo + 4:smo + 5], sb_sm[:, smo + 3:smo + 4])
                    chain(v)
                    v.tensor_mul(sb_sm[:, smo + 10:smo + 11], sb_sm[:, smo + 3:smo + 4], sb_sm[:, smo + 8:smo + 9])
                    v.tensor_mul(sb_sm[:, smo + 9:smo + 10], sb_sm[:, smo + 2:smo + 3], sb_sm[:, smo + 4:smo + 5])
                    v.tensor_scalar(sb_sm[:, smo + 11:smo + 12], sb_sm[:, smo + 0:smo + 1],
                                    sb_sm[:, smo + 8:smo + 9], -1.0, ALU.mult, ALU.mult)
                    chain(v)
                    if c >= 2:
                        v.wait_ge(TT, vproj_tt[c - 2])
                    v.tensor_scalar(zt, xb, sb_sm[:, smo + 8:smo + 9],
                                    sb_sm[:, smo + 11:smo + 12], ALU.mult, ALU.add)
                    mark(v, "VV")
                z_vv[c] = cnt["VV"]

            def emit_s1b(c):
                par = c % 2
                smo = (c % 3) * 12
                zt = sb_ztok[:, par * D:(par + 1) * D]
                tp = psT[:, 0:1024]
                r2ps = psT[:1, 1024:1280].bitcast(F32)
                r2sl = sb_r2row[:, par * 128:(par + 1) * 128]

                @block.tensor
                def _(t):
                    t.wait_ge(VV, z_vv[c])
                    wtok(t, atok["t0"])
                    for kc in range(NKC):
                        t.transpose(tp[:, kc * 128:(kc + 1) * 128], zt[:, kc * 128:(kc + 1) * 128], sb_ident[:])
                    wtok(t, atok["r2"])
                    t.transpose(r2ps, sb_sm[:, smo + 10:smo + 11], sb_id32[:])
                    mark(t, "TT")
                zt_tt[c] = cnt["TT"]

                @block.vector
                def _(v):
                    v.wait_ge(TT, zt_tt[c])
                    v.tensor_copy(zt, tp[:, 0:1024])
                    v.tensor_copy(r2sl, r2ps)
                    mark(v, "VV")
                zf_vv[c] = cnt["VV"]
                atok["t0"] = ("VV", cnt["VV"])
                atok["r2"] = ("VV", cnt["VV"])

            def emit_s2a(c):
                par = c % 2
                smo = (c % 3) * 12
                zf = sb_ztok[:, par * D:(par + 1) * D]
                r2sl = sb_r2row[:, par * 128:(par + 1) * 128]
                ZF_VV = zf_vv[c]

                @block.tensor
                def _(t):
                    t.wait_ge(VV, ZF_VV)
                    wtok(t, atok[0]); wtok(t, atok[1])
                    for nw in range(2):
                        dst = psA[:, nw * 512:(nw + 1) * 512]
                        for kc in range(NKC):
                            t.matmul(dst, zf[:, kc * 128:(kc + 1) * 128],
                                     sb_wgate[:, kc * D + nw * 512: kc * D + (nw + 1) * 512],
                                     start=(kc == 0), stop=False)
                        t.matmul(dst, r2sl, sb_rg[:, nw * 512:(nw + 1) * 512], start=False, stop=True)
                    mark(t, "TT")
                GATE_TT = cnt["TT"]

                @block.scalar
                def _(s):
                    s.wait_ge(TT, GATE_TT)
                    if c > 0:
                        s.wait_ge(VV, gn_vv[c - 1])
                    s.activation(sb_gate[:], psA[:, 0:1024], ACT.Sigmoid, scale=sb_sm[:, smo + 9:smo + 10],
                                 accum_out=sb_sm[:, smo + 5:smo + 6])
                    mark(s, "SS")
                SIG_SS = cnt["SS"]
                atok[0] = ("SS", SIG_SS)
                atok[1] = ("SS", SIG_SS)

                @block.vector
                def _(v):
                    v.wait_ge(SS, SIG_SS)
                    v.tensor_scalar(sb_sm[:, smo + 6:smo + 7], sb_sm[:, smo + 5:smo + 6], 1.0 / D, EPS, ALU.mult, ALU.add)
                    chain(v)
                    v.reciprocal(sb_sm[:, smo + 7:smo + 8], sb_sm[:, smo + 6:smo + 7])
                    chain(v)
                    v.tensor_scalar_mul(sb_gaten[:], sb_gate[:], sb_sm[:, smo + 7:smo + 8])
                    mark(v, "VV")
                GN_VV = cnt["VV"]

                def emit_proj(t, nw0, dsts):
                    for j, dst in enumerate(dsts):
                        nw = nw0 + j
                        for kc in range(NKC):
                            t.matmul(dst, zf[:, kc * 128:(kc + 1) * 128],
                                     sb_wqkv[:, kc * 3 * D + nw * 512: kc * 3 * D + (nw + 1) * 512],
                                     start=(kc == 0), stop=False)
                        t.matmul(dst, sb_ones[:, 0:128], sb_rq[:, nw * 512:(nw + 1) * 512],
                                 start=False, stop=True)

                @block.tensor
                def _(t):
                    if c == 0:
                        t.wait_ge(SS, WSC_SS)
                    wtok(t, atok[2]); wtok(t, atok[3])
                    emit_proj(t, 0, [psA[:, 1024:1536], psA[:, 1536:2048]])
                    mark(t, "TT")
                Q_TT = cnt["TT"]

                @block.tensor
                def _(t):
                    wtok(t, atok[0]); wtok(t, atok[1])
                    emit_proj(t, 2, [psA[:, 0:512], psA[:, 512:1024]])
                    mark(t, "TT")
                K_TT = cnt["TT"]
                s2state[c] = (GN_VV, Q_TT, K_TT, emit_proj)

            def emit_s2b(c):
                par = c % 2
                zf = sb_ztok[:, par * D:(par + 1) * D]
                q4s = sb_q4st[:, 0:D]
                tp = psT[:, 0:1024]
                GN_VV, Q_TT, K_TT, emit_proj = s2state[c]

                @block.scalar
                def _(s):
                    s.wait_ge(TT, Q_TT)
                    if c > 0:
                        s.wait_ge(VV, q4k4_vv[0])
                    s.activation(sb_tmp1[:], psA[:, 1024:2048], ACT.Relu)
                    mark(s, "SS")
                QR_SS = cnt["SS"]
                atok[2] = ("SS", QR_SS)
                atok[3] = ("SS", QR_SS)

                @block.vector
                def _(v):
                    v.wait_ge(SS, QR_SS)
                    v.wait_ge(VV, GN_VV)
                    v.tensor_mul(sb_tmp1[:], sb_tmp1[:], sb_gaten[:])
                    mark(v, "VV")
                QG_VV = cnt["VV"]

                @block.scalar
                def _(s):
                    s.wait_ge(TT, K_TT)
                    s.activation(sb_tmp2[:], psA[:, 0:1024], ACT.Relu)
                    mark(s, "SS")
                KR_SS = cnt["SS"]
                atok[0] = ("SS", KR_SS)
                atok[1] = ("SS", KR_SS)

                @block.vector
                def _(v):
                    v.wait_ge(SS, KR_SS)
                    v.tensor_mul(sb_tmp2[:], sb_tmp2[:], sb_gaten[:])
                    mark(v, "VV")
                QKG_VV = cnt["VV"]
                gn_vv[c] = QKG_VV

                @block.scalar
                def _(s):
                    s.wait_ge(VV, QG_VV)
                    s.activation(sb_tmp1[:], sb_tmp1[:], ACT.Square)
                    mark(s, "SS")
                QS_SS = cnt["SS"]

                @block.vector
                def _(v):
                    v.wait_ge(SS, QS_SS)
                    if c >= 2:
                        v.wait_ge(TT, q4t_tt_by[c - 2])
                    v.tensor_scalar(sb_q4tok[:, (c % 2) * D:(c % 2 + 1) * D], sb_tmp1[:], EPS, None, ALU.add)
                    mark(v, "VV")
                Q4_VV = cnt["VV"]
                q4_vv_by[c] = Q4_VV

                @block.scalar
                def _(s):
                    s.wait_ge(VV, QKG_VV)
                    s.activation(sb_tmp2[:], sb_tmp2[:], ACT.Square)
                    mark(s, "SS")
                KS_SS = cnt["SS"]

                @block.vector
                def _(v):
                    v.wait_ge(SS, KS_SS)
                    if c >= 2:
                        v.wait_ge(TT, state_tt_by[c - 2])
                    v.tensor_scalar(sb_k4tok[:, (c % 2) * D:(c % 2 + 1) * D], sb_tmp2[:], EPS, None, ALU.add)
                    mark(v, "VV")
                K4_VV = cnt["VV"]
                k4_vv_by[c] = K4_VV
                q4k4_vv[0] = cnt["VV"]

                @block.tensor
                def _(t):
                    wtok(t, atok[2]); wtok(t, atok[3])
                    emit_proj(t, 4, [psA[:, 1024:1536], psA[:, 1536:2048]])
                    mark(t, "TT")
                V_TT = cnt["TT"]
                vproj_tt[c] = V_TT
                s2tail[c] = True

                @block.scalar
                def _(s):
                    s.wait_ge(TT, V_TT)
                    s.wait_ge(sems["GG"], SETUP_GG)
                    dst = sb_vtok[:, c * HEADS * 65:(c + 1) * HEADS * 65].rearrange("p (h w) -> p h w", w=65)[:, :, 0:64]
                    s.activation(dst[:, 0:8, :], psA[:, 1024:1536].rearrange("p (h w) -> p h w", w=64), ACT.Copy)
                    s.activation(dst[:, 8:16, :], psA[:, 1536:2048].rearrange("p (h w) -> p h w", w=64), ACT.Copy)
                    mark(s, "SS")
                VE_SS = cnt["SS"]
                ve_ss_by[c] = VE_SS
                atok[2] = ("SS", VE_SS)
                atok[3] = ("SS", VE_SS)

            def emit_s2c(c):
                par = c % 2
                q4s = sb_q4st[:, 0:D]
                tp = psT[:, 0:1024]
                q4slot = sb_q4tok[:, par * D:(par + 1) * D]
                k4slot = sb_k4tok[:, par * D:(par + 1) * D]

                @block.tensor
                def _(t):
                    t.wait_ge(VV, q4_vv_by[c])
                    wtok(t, atok["t0"])
                    for kc in range(NKC):
                        t.transpose(tp[:, kc * 128:(kc + 1) * 128], q4slot[:, kc * 128:(kc + 1) * 128], sb_ident[:])
                    q4t_tt_by[c] = mark(t, "TT")
                Q4T_TT = cnt["TT"]

                @block.vector
                def _(v):
                    v.wait_ge(TT, Q4T_TT)
                    if c >= 1:
                        v.wait_ge(sems["DSP"], spill_ds[c - 1])
                    v.tensor_copy(q4s, tp[:, 0:1024])
                    mark(v, "VV")
                Q4F_VV = cnt["VV"]
                atok["t0"] = ("VV", Q4F_VV)

                @block.tensor
                def _(t):
                    t.wait_ge(VV, k4_vv_by[c])
                    wtok(t, atok["t0"])
                    for kc in range(NKC):
                        t.transpose(tp[:, kc * 128:(kc + 1) * 128], k4slot[:, kc * 128:(kc + 1) * 128], sb_ident[:])
                    mark(t, "TT")
                K4T_TT = cnt["TT"]

                @block.vector
                def _(v):
                    v.wait_ge(TT, K4T_TT)
                    dst = sb_k4fm[:].rearrange("p (kc tok) -> p kc tok", tok=TC)[:, :, c * 128:(c + 1) * 128]
                    v.tensor_copy(dst, tp[:, 0:1024].rearrange("p (kc w) -> p kc w", w=128))
                    mark(v, "VV")
                K4F_VV = cnt["VV"]
                atok["t0"] = ("VV", K4F_VV)

                @block.sync
                def _(s):
                    s.wait_ge(VV, Q4F_VV)
                    s.dma_start(out=d_q4spill.ap()[c], in_=q4s).then_inc(sems["DSP"], 16)
                    bump("DSP", 16)
                spill_ds[c] = cnt["DSP"]

                @block.tensor
                def _(t):
                    t.wait_ge(SS, ve_ss_by[c])
                    if c > 0:
                        t.wait_ge(VV, snap_vv[c - 1])
                    for h in range(HEADS):
                        p, half = h // 2, h % 2
                        if p < 7:
                            dst = psSA[half * 64:(half + 1) * 64, p * 65:(p + 1) * 65]
                        else:
                            dst = psSB[half * 64:(half + 1) * 64, 0:65]
                        st = (c == 0) and (h in (0, 1, 14, 15))
                        t.matmul(dst, k4slot[:, h * 64:(h + 1) * 64],
                                 sb_vtok[:, c * HEADS * 65 + h * 65: c * HEADS * 65 + (h + 1) * 65],
                                 start=st, stop=(c == NCH - 1 and h >= 14), skip_group_check=True)
                    state_tt_by[c] = mark(t, "TT")
                state_tt[c] = cnt["TT"]

                @block.vector
                def _(v):
                    v.wait_ge(TT, state_tt[c])
                    if c < NCH - 1:
                        slot = sb_snaps[:, (c + 1) * 520:(c + 2) * 520]
                        v.tensor_copy(slot[:, 0:455], psSA[:, 0:455])
                        v.tensor_copy(slot[:, 455:520], psSB[:, 0:65])
                    else:
                        v.tensor_copy(sb_exch[:, 0:455], psSA[:, 0:455])
                        v.tensor_copy(sb_exch[:, 455:520], psSB[:, 0:65])
                    mark(v, "VV")
                snap_vv[c] = cnt["VV"]

            s2state = {}
            zf_vv = {}
            vproj_tt = {}
            s2tail = {}
            q4t_tt_by = {}
            state_tt_by = {}
            q4_vv_by = {}
            k4_vv_by = {}
            ve_ss_by = {}
            emit_s1(0)
            emit_s1b(0)
            for c in range(NCH):
                if c + 1 < NCH:
                    emit_s1(c + 1)
                emit_s2a(c)
                if c + 1 < NCH:
                    emit_s1b(c + 1)
                emit_s2b(c)
                if c >= 1:
                    emit_s2c(c - 1)
            emit_s2c(NCH - 1)

            A_SNAP = snap_vv[NCH - 1]
            ALL_SPILLS = cnt["DSP"]

        with ExitStack() as dd:
            ed = dd.enter_context
            sb_wp = ed(nc.sbuf_tensor("wp_bf", [128, NKC * D], BF16))
            sb_u1 = ed(nc.sbuf_tensor("u1_bf", [128, NKC * RANK], BF16))
            sb_v1 = ed(nc.sbuf_tensor("v1_bf", [128, 2 * DFF], BF16))
            sb_u2 = ed(nc.sbuf_tensor("u2_bf", [128, 32 * RANK], BF16))
            sb_v2 = ed(nc.sbuf_tensor("v2_bf", [128, 2 * D], BF16))
            sb_q4pf = ed(nc.sbuf_tensor("q4pf", [128, 2 * D], BF16))
            sb_am = ed(nc.sbuf_tensor("am", [128, HEADS * 128], BF16))
            sb_den = ed(nc.sbuf_tensor("den", [128, 16], F32))
            sb_rec = ed(nc.sbuf_tensor("recip", [128, 16], F32))
            sb_osb = ed(nc.sbuf_tensor("osb", [128, D], BF16))
            sb_ofm = ed(nc.sbuf_tensor("ofm", [128, 2 * NKC * 256], BF16))
            sb_x2 = ed(nc.sbuf_tensor("x2", [128, 2 * D], F32))
            sb_z3 = ed(nc.sbuf_tensor("z3", [128, D], BF16))
            sb_z3fm = ed(nc.sbuf_tensor("z3fm", [128, NKC * 256], BF16))
            sb_psb = ed(nc.sbuf_tensor("psb", [128, 512], BF16))
            sb_hsb = ed(nc.sbuf_tensor("hsb", [128, 32 * 256], BF16))
            sb_qsb = ed(nc.sbuf_tensor("qsb", [128, 512], BF16))
            sb_x3 = ed(nc.sbuf_tensor("x3", [128, 2 * D], F32))
            sb_st2 = ed(nc.sbuf_tensor("st2", [128, 12], F32))
            sb_sm2 = ed(nc.sbuf_tensor("sm2", [128, 6], F32))
            psC = ed(nc.psum_tensor("psC", [128, 4096], F32))

            btok = {b_: ("VV", 0) for b_ in range(8)}

            def bw(t, *banks):
                seen = set()
                for b_ in banks:
                    if btok[b_] not in seen:
                        seen.add(btok[b_])
                        wtok(t, btok[b_])

            def bs(tokval, *banks):
                for b_ in banks:
                    btok[b_] = tokval

            @block.scalar
            def _(s):
                s.wait_ge(VV, A_SNAP)
                s.dma_start(out=d_ccin.ap(), in_=sb_exch[:]).then_inc(sems["DGE"], 16)
                bump("DGE", 16)
            CCIN_DGE = cnt["DGE"]

            @block.gpsimd
            def _(g):
                g.wait_ge(sems["DGE"], CCIN_DGE)
                g.collective_compute("AllGather", ALU.bypass,
                                     replica_groups=[[0, 1], [2, 3], [4, 5], [6, 7]],
                                     ins=[d_ccin.ap().opt()],
                                     outs=[d_ccout.ap().opt()]).then_inc(CC, 1)
                bump("CC", 1)
            EXCH_CC = cnt["CC"]

            @block.scalar
            def _(s):
                s.wait_ge(CC, EXCH_CC)
                s.dma_start(out=sb_exch[:], in_=d_ccout.ap()[0]).then_inc(sems["DGE"], 16)
                bump("DGE", 16)
            EXCH_DG = cnt["DGE"]

            snapfix_vv = {}

            @block.vector
            def _(v):
                v.wait_ge(sems["DGE"], EXCH_DG)
                v.tensor_scalar_mul(sb_exch[:], sb_exch[:], sb_wprev[:])
                chain(v)
                v.tensor_copy(sb_snaps[:, 0:520], sb_exch[:])
                snapfix_vv[0] = mark(v, "VV")
                for c in range(1, NCH):
                    slot = sb_snaps[:, c * 520:(c + 1) * 520]
                    v.tensor_add(slot, slot, sb_exch[:])
                    snapfix_vv[c] = mark(v, "VV")
            SNAPFIX_VV = cnt["VV"]

            @block.gpsimd
            def _(g):
                g.wait_ge(VV, A_SNAP)
                g.wait_ge(sems["DGE"], 32)
                for kc in range(NKC):
                    g.dma_start(out=sb_wp[:, kc * D:(kc + 1) * D],
                                in_=d_wp[kc * 128:(kc + 1) * 128, :]).then_inc(DG, 16)
                    g.dma_start(out=sb_u1[:, kc * RANK:(kc + 1) * RANK],
                                in_=d_u1[kc * 128:(kc + 1) * 128, :]).then_inc(DG, 16)
                    bump("DG", 32)
                for rc in range(2):
                    g.dma_start(out=sb_v1[:, rc * DFF:(rc + 1) * DFF],
                                in_=d_v1[rc * 128:(rc + 1) * 128, :],
                                max_dma_last_dim=8000).then_inc(DG, 16)
                    g.dma_start(out=sb_v2[:, rc * D:(rc + 1) * D],
                                in_=d_v2[rc * 128:(rc + 1) * 128, :]).then_inc(DG, 16)
                    bump("DG", 32)
            DW_DG = cnt["DG"]

            @block.gpsimd
            def _(g):
                for fc in range(32):
                    g.dma_start(out=sb_u2[:, fc * RANK:(fc + 1) * RANK],
                                in_=d_u2[fc * 128:(fc + 1) * 128, :]).then_inc(sems["DG2"], 16)
                    bump("DG2", 16)
            DW_DG2 = cnt["DG2"]


            if DEBUG_TAPS:
                @block.sync
                def _(s):
                    s.wait_ge(VV, SNAPFIX_VV)
                    s.dma_start(out=taps["t_snap"], in_=sb_snaps[:, 520:1040]).then_inc(ST, 16)
                    bump("ST", 16)

            q4use_tt = {}
            ofm_ss = {}
            proj_done = [0]
            ot_tt = {}
            am_vv = {}
            div_vv = {}

            def c_a(c):
                par = c % 2
                q4p = sb_q4pf[:, par * D:(par + 1) * D]
                dsq = "DSQ0" if par == 0 else "DSQ1"

                @block.sync
                def _(s):
                    if c >= 2:
                        s.wait_ge(TT, q4use_tt[c - 2])
                    else:
                        s.wait_ge(sems["DSP"], ALL_SPILLS)
                    s.dma_start(out=q4p, in_=d_q4spill.ap()[c]).then_inc(sems[dsq], 16)
                    bump(dsq, 16)
                Q4P_DS = cnt[dsq]

                k4 = sb_k4fm[:].rearrange("p (kc tok) -> p kc tok", tok=TC)
                q4 = q4p.rearrange("p (kc w) -> p kc w", w=128)

                @block.tensor
                def _(t):
                    t.wait_ge(sems[dsq], Q4P_DS)
                    bw(t, 0, 1, 2, 3)
                    for p in range(NPAIR):
                        t.matmul(psC[:, p * 128:(p + 1) * 128],
                                 k4[0:64, p, c * 128:(c + 1) * 128], q4[0:64, p, :],
                                 start=True, stop=True, tile_position=(0, 0))
                        t.matmul(psC[:, 1024 + p * 128:1024 + (p + 1) * 128],
                                 k4[64:128, p, c * 128:(c + 1) * 128], q4[64:128, p, :],
                                 start=True, stop=True, tile_position=(64, 0))
                    mark(t, "TT")
                ATT_TT = cnt["TT"]

                @block.vector
                def _(v):
                    v.wait_ge(TT, ATT_TT)
                    amv = sb_am[:].rearrange("p (h w) -> p h w", w=128)
                    mb = sb_mask[:].unsqueeze(1).to_broadcast([128, 8, 128])
                    v.tensor_tensor(amv[:, 0::2, :], psC[:, 0:1024].rearrange("p (h w) -> p h w", w=128), mb, ALU.mult)
                    v.tensor_tensor(amv[:, 1::2, :], psC[:, 1024:2048].rearrange("p (h w) -> p h w", w=128), mb, ALU.mult)
                    mark(v, "VV")
                am_vv[c] = cnt["VV"]
                bs(("VV", am_vv[c]), 0, 1, 2, 3)

            def c_b(c):
                par = c % 2
                q4p = sb_q4pf[:, par * D:(par + 1) * D]
                q4 = q4p.rearrange("p (kc w) -> p kc w", w=128)

                @block.tensor
                def _(t):
                    t.wait_ge(VV, am_vv[c])
                    for h in range(HEADS):
                        i = h // 2
                        slot = (h % 2) * 1024 + i * 128
                        t.matmul(psC[:, slot:slot + 65], sb_am[:, h * 128:(h + 1) * 128],
                                 sb_vtok[:, c * HEADS * 65 + h * 65: c * HEADS * 65 + (h + 1) * 65],
                                 start=(h in (0, 1, 8, 9)), stop=False, skip_group_check=True)
                    t.wait_ge(VV, snapfix_vv[c])
                    for p in range(NPAIR):
                        off = c * 520 + (p * 65 if p < 7 else 455)
                        t.matmul(psC[:, p * 128:p * 128 + 65],
                                 q4[0:64, p, :], sb_snaps[0:64, off:off + 65],
                                 start=False, stop=True, tile_position=(0, 0), skip_group_check=True)
                        t.matmul(psC[:, 1024 + p * 128:1024 + p * 128 + 65],
                                 q4[64:128, p, :], sb_snaps[64:128, off:off + 65],
                                 start=False, stop=True, tile_position=(64, 0), skip_group_check=True)
                    mark(t, "TT")
                NUMM_TT = cnt["TT"]
                q4use_tt[c] = NUMM_TT

                @block.vector
                def _(v):
                    v.wait_ge(TT, NUMM_TT)
                    if c >= 1:
                        v.wait_ge(TT, ot_tt[c - 1])
                    dd_ = psC[:, 0:2048].rearrange("p (sl w) -> p sl w", w=128)[:, :, 64:65]
                    v.tensor_scalar(sb_den[:].unsqueeze(2), dd_, EPS, None, ALU.add)
                    chain(v)
                    v.reciprocal(sb_rec[:], sb_den[:])
                    chain(v)
                    ov = sb_osb[:].rearrange("p (h w) -> p h w", w=64)
                    ne = psC[:, 0:1024].rearrange("p (sl w) -> p sl w", w=128)[:, :, 0:64]
                    no = psC[:, 1024:2048].rearrange("p (sl w) -> p sl w", w=128)[:, :, 0:64]
                    v.tensor_tensor(ov[:, 0::2, :], ne,
                                    sb_rec[:, 0:8].unsqueeze(2).to_broadcast([128, 8, 64]), ALU.mult)
                    v.tensor_tensor(ov[:, 1::2, :], no,
                                    sb_rec[:, 8:16].unsqueeze(2).to_broadcast([128, 8, 64]), ALU.mult)
                    mark(v, "VV")
                div_vv[c] = cnt["VV"]
                bs(("VV", div_vv[c]), 0, 1, 2, 3)

            def c_c(c):
                @block.tensor
                def _(t):
                    t.wait_ge(VV, div_vv[c])
                    bw(t, 0)
                    dst = psC[:, 0:512].bitcast(BF16)
                    for kc in range(NKC):
                        t.transpose(dst[:, kc * 128:(kc + 1) * 128], sb_osb[:, kc * 128:(kc + 1) * 128], sb_ident[:])
                    mark(t, "TT")
                OT_TT = cnt["TT"]
                ot_tt[c] = OT_TT

                @block.scalar
                def _(s):
                    s.wait_ge(TT, OT_TT)
                    s.wait_ge(TT, proj_done[0])
                    src = psC[:, 0:512].bitcast(BF16).rearrange("p (kc w) -> p kc w", w=128)
                    hpar = (c // 2) % 2
                    dst = sb_ofm[:, hpar * NKC * 256:(hpar + 1) * NKC * 256]
                    dst = dst.rearrange("p (kc w) -> p kc w", w=256)[:, :, (c % 2) * 128:(c % 2) * 128 + 128]
                    s.activation(dst, src, ACT.Copy)
                    mark(s, "SS")
                bs(("SS", cnt["SS"]), 0)
                ofm_ss[c] = cnt["SS"]

            def d_proj(hs, j):
                c = hs * 2 + j
                hpar = hs % 2
                x2sl = sb_x2[:, j * D:(j + 1) * D]

                @block.sync
                def _(s):
                    if hs > 0:
                        s.wait_ge(VV, x2_free[j])
                    else:
                        s.wait_ge(sems["DSP"], ALL_SPILLS)
                    s.dma_start(out=x2sl, in_=d_x[c * 128:(c + 1) * 128, :]).then_inc(sems["DSXD"], 16)
                    bump("DSXD", 16)
                XD_DS = cnt["DSXD"]

                proj_nw = []

                @block.tensor
                def _(t):
                    t.wait_ge(SS, ofm_ss[hs * 2 + 1])
                    t.wait_ge(DG, DW_DG)
                    ofmv = sb_ofm[:, hpar * NKC * 256:(hpar + 1) * NKC * 256]
                    for nw in range(2):
                        bw(t, 4 + nw)
                        dst = psC[:, 2048 + nw * 512:2048 + (nw + 1) * 512]
                        for kc in range(NKC):
                            t.matmul(dst, ofmv[:, kc * 256 + j * 128: kc * 256 + (j + 1) * 128],
                                     sb_wp[:, kc * D + nw * 512:kc * D + (nw + 1) * 512],
                                     start=(kc == 0), stop=False)
                        t.matmul(dst, sb_ones[:, 0:128], sb_rp[:, nw * 512:(nw + 1) * 512],
                                 start=False, stop=True)
                        proj_nw.append(mark(t, "TT"))
                PROJ_TT = proj_nw[1]
                proj_done[0] = PROJ_TT

                add_m = []

                @block.vector
                def _(v):
                    v.wait_ge(TT, proj_nw[0])
                    v.wait_ge(sems["DSXD"], XD_DS)
                    v.tensor_add(x2sl[:, 0:512], psC[:, 2048:2560], x2sl[:, 0:512])
                    chain(v)
                    add_m.append(cnt["VV"])
                    v.bn_stats(sb_st2[:, 0:6], x2sl[:, 0:512])
                    v.wait_ge(TT, PROJ_TT)
                    v.tensor_add(x2sl[:, 512:1024], psC[:, 2560:3072], x2sl[:, 512:1024])
                    chain(v)
                    add_m.append(cnt["VV"])
                    v.bn_stats(sb_st2[:, 6:12], x2sl[:, 512:1024])
                    chain(v)
                    v.bn_aggr(sb_sm2[:, 0:2], sb_st2[:])
                    chain(v)
                    v.tensor_scalar(sb_sm2[:, 2:3], sb_sm2[:, 1:2], EPS, None, ALU.add)
                    chain(v)
                    v.reciprocal(sb_sm2[:, 3:4], sb_sm2[:, 2:3])
                    mark(v, "VV")
                LN3_VV = cnt["VV"]
                bs(("VV", add_m[0]), 4)
                bs(("VV", add_m[1]), 5)

                @block.scalar
                def _(s):
                    s.wait_ge(VV, LN3_VV)
                    s.activation(sb_sm2[:, 4:5], sb_sm2[:, 3:4], ACT.Sqrt)
                    mark(s, "SS")
                d_state[(hs, j)] = (cnt["SS"], x2sl)

            def d_z3(hs, j):
                R3_SS, x2sl = d_state[(hs, j)]

                @block.vector
                def _(v):
                    v.wait_ge(SS, R3_SS)
                    v.tensor_scalar(sb_sm2[:, 5:6], sb_sm2[:, 0:1], sb_sm2[:, 4:5], -1.0, ALU.mult, ALU.mult)
                    chain(v)
                    if hs + j > 0:
                        v.wait_ge(TT, z3t_used[0])
                    v.tensor_scalar(sb_z3[:], x2sl, sb_sm2[:, 4:5], sb_sm2[:, 5:6], ALU.mult, ALU.add)
                    mark(v, "VV")
                Z3_VV = cnt["VV"]

                @block.tensor
                def _(t):
                    t.wait_ge(VV, Z3_VV)
                    bw(t, 6)
                    dst = psC[:, 3072:3584].bitcast(BF16)
                    for kc in range(NKC):
                        t.transpose(dst[:, kc * 128:(kc + 1) * 128], sb_z3[:, kc * 128:(kc + 1) * 128], sb_ident[:])
                    z3t_used[0] = mark(t, "TT")
                Z3T_TT = cnt["TT"]

                @block.vector
                def _(v):
                    v.wait_ge(TT, Z3T_TT)
                    src = psC[:, 3072:3584].bitcast(BF16).rearrange("p (kc w) -> p kc w", w=128)
                    dst = sb_z3fm[:].rearrange("p (kc w) -> p kc w", w=256)[:, :, j * 128:(j + 1) * 128]
                    v.tensor_copy(dst, src)
                    mark(v, "VV")
                bs(("VV", cnt["VV"]), 6)
                z3f_vv[j] = cnt["VV"]

            def d_ffn(hs):
                TW = 256

                @block.tensor
                def _(t):
                    t.wait_ge(VV, z3f_vv[1])
                    bw(t, 7)
                    for mc in range(2):
                        dst = psC[:, 3584 + mc * TW:3584 + (mc + 1) * TW]
                        for kc in range(NKC):
                            t.matmul(dst, sb_u1[:, kc * RANK + mc * 128:kc * RANK + (mc + 1) * 128],
                                     sb_z3fm[:, kc * TW:(kc + 1) * TW],
                                     start=(mc == 0 and kc == 0), stop=(kc == NKC - 1), skip_group_check=True)
                    mark(t, "TT")
                U1_TT = cnt["TT"]

                @block.vector
                def _(v):
                    v.wait_ge(TT, U1_TT)
                    v.tensor_copy(sb_psb[:], psC[:, 3584:4096])
                    mark(v, "VV")
                P_VV = cnt["VV"]
                bs(("VV", P_VV), 7)

                ss_base = cnt["SS"]
                v1_tt = []

                @block.tensor
                def _(t):
                    t.wait_ge(VV, P_VV)
                    t.wait_ge(sems["DG2"], DW_DG2)
                    bw(t, 4, 5, 6)
                    for pr in range(16):
                        if pr >= 2:
                            t.wait_ge(SS, ss_base + pr - 1)
                        for half in range(2):
                            fc = pr * 2 + half
                            dst = psC[:, 2048 + (pr % 2) * 512 + half * TW:
                                      2048 + (pr % 2) * 512 + (half + 1) * TW]
                            for rc in range(2):
                                t.matmul(dst, sb_v1[:, rc * DFF + fc * 128:rc * DFF + (fc + 1) * 128],
                                         sb_psb[:, rc * TW:(rc + 1) * TW],
                                         start=(rc == 0), stop=(rc == 1))
                        v1_tt.append(mark(t, "TT"))
                        if pr >= 1:
                            t.wait_ge(SS, ss_base + pr)
                            for half in range(2):
                                fc = (pr - 1) * 2 + half
                                for mc in range(2):
                                    t.matmul(psC[:, 3072 + mc * TW:3072 + (mc + 1) * TW],
                                             sb_u2[:, fc * RANK + mc * 128:fc * RANK + (mc + 1) * 128],
                                             sb_hsb[:, fc * TW:(fc + 1) * TW],
                                             start=(fc == 0 and mc == 0), stop=False, skip_group_check=True)
                    t.wait_ge(SS, ss_base + 16)
                    for half in range(2):
                        fc = 30 + half
                        for mc in range(2):
                            t.matmul(psC[:, 3072 + mc * TW:3072 + (mc + 1) * TW],
                                     sb_u2[:, fc * RANK + mc * 128:fc * RANK + (mc + 1) * 128],
                                     sb_hsb[:, fc * TW:(fc + 1) * TW],
                                     start=False, stop=(fc == 31 and mc == 1), skip_group_check=True)
                    mark(t, "TT")
                U2_TT = cnt["TT"]

                @block.scalar
                def _(s):
                    for pr in range(16):
                        s.wait_ge(TT, v1_tt[pr])
                        src = psC[:, 2048 + (pr % 2) * 512:2048 + (pr % 2) * 512 + 512]
                        s.activation(sb_hsb[:, pr * 2 * TW:pr * 2 * TW + TW], src[:, 0:TW],
                                     ACT.Gelu_apprx_tanh, bias=sb_b1c[:, 2 * pr:2 * pr + 1])
                        s.activation(sb_hsb[:, pr * 2 * TW + TW:(pr + 1) * 2 * TW], src[:, TW:2 * TW],
                                     ACT.Gelu_apprx_tanh, bias=sb_b1c[:, 2 * pr + 1:2 * pr + 2])
                        assert mark(s, "SS") == ss_base + pr + 1
                bs(("SS", cnt["SS"]), 4, 5)
                return U2_TT

            def d_out(hs, U2_TT):
                TW = 256

                @block.vector
                def _(v):
                    v.wait_ge(TT, U2_TT)
                    v.tensor_copy(sb_qsb[:], psC[:, 3072:3584])
                    mark(v, "VV")
                QQ_VV = cnt["VV"]
                bs(("VV", QQ_VV), 6)

                for j in range(2):
                    c = hs * 2 + j
                    xpar = c % 2
                    x3b = sb_x3[:, xpar * D:(xpar + 1) * D]
                    dso = "DSO" if xpar == 0 else "DSO1"
                    _, x2sl = d_state[(hs, j)]

                    v2_nw = []

                    @block.tensor
                    def _(t):
                        t.wait_ge(VV, QQ_VV)
                        for nw in range(2):
                            bw(t, 4 + nw)
                            dst = psC[:, 2048 + nw * 512:2048 + (nw + 1) * 512]
                            for rc in range(2):
                                t.matmul(dst, sb_qsb[:, rc * TW + j * 128:rc * TW + (j + 1) * 128],
                                         sb_v2[:, rc * D + nw * 512:rc * D + (nw + 1) * 512],
                                         start=(rc == 0), stop=False)
                            t.matmul(dst, sb_ones[:, 0:128], sb_rb2[:, nw * 512:(nw + 1) * 512],
                                     start=False, stop=True)
                            v2_nw.append(mark(t, "TT"))

                    x3_m = []

                    @block.vector
                    def _(v):
                        v.wait_ge(TT, v2_nw[0])
                        if c >= 2:
                            v.wait_ge(sems[dso], x3_ds[xpar])
                        v.tensor_add(x3b[:, 0:512], psC[:, 2048:2560], x2sl[:, 0:512])
                        mark(v, "VV")
                        x3_m.append(cnt["VV"])
                        v.wait_ge(TT, v2_nw[1])
                        v.tensor_add(x3b[:, 512:1024], psC[:, 2560:3072], x2sl[:, 512:1024])
                        mark(v, "VV")
                        x3_m.append(cnt["VV"])
                    X3_VV = x3_m[1]
                    bs(("VV", x3_m[0]), 4)
                    bs(("VV", x3_m[1]), 5)
                    x2_free[j] = X3_VV

                    @block.scalar
                    def _(s):
                        s.wait_ge(VV, X3_VV)
                        s.dma_start(out=d_out_ap[c * 128:(c + 1) * 128, :], in_=x3b).then_inc(sems[dso], 16)
                        x3_ds[xpar] = bump(dso, 16)

            d_state = {}
            x2_free = {0: 0, 1: 0}
            z3t_used = [0]
            z3f_vv = {}
            x3_ds = {0: 0, 1: 0}
            d_out_ap = d_out_dram
            for c0 in (0, 1):
                c_a(c0); c_b(c0); c_c(c0)
            for hs in range(8):
                if hs < 7:
                    c_a(2 * hs + 2)
                d_proj(hs, 0)
                if hs < 7:
                    c_b(2 * hs + 2)
                    c_c(2 * hs + 2)
                d_z3(hs, 0)
                if hs < 7:
                    c_a(2 * hs + 3)
                d_proj(hs, 1)
                if hs < 7:
                    c_b(2 * hs + 3)
                    c_c(2 * hs + 3)
                d_z3(hs, 1)
                u2tt = d_ffn(hs)
                d_out(hs, u2tt)

            @block.sync
            def _(s):
                for nm in ["DS", "DSX0", "DSX1", "DSP", "DSQ0", "DSQ1", "DSXD", "DSO", "DSO1", "DG", "DGE", "DG2", "ST"]:
                    if cnt[nm]:
                        s.wait_ge(sems[nm], cnt[nm])

    return nc


def kernel(L0_attn_x, L0_attn_ln_g, L0_attn_ln_b, L0_attn_W_qkv, L0_attn_b_qkv,
           L0_attn_W_gate, L0_attn_b_gate, L0_attn_W_proj, L0_attn_b_proj,
           L0_ffn_U1, L0_ffn_V1, L0_ffn_b1, L0_ffn_U2, L0_ffn_V2, L0_ffn_b2):
    x = np.ascontiguousarray(np.asarray(L0_attn_x, dtype=np.float32))
    nc = build_program()
    ident16 = np.eye(128, dtype=BF)
    ident32 = np.eye(128, dtype=np.float32)
    mask = np.ascontiguousarray(np.triu(np.ones((128, 128), np.float32)))
    common = {
        "wqkv": np.ascontiguousarray(np.asarray(L0_attn_W_qkv, np.float32)),
        "wg": np.ascontiguousarray(np.asarray(L0_attn_W_gate, np.float32)),
        "wp": np.ascontiguousarray(np.asarray(L0_attn_W_proj, np.float32)),
        "u1": np.ascontiguousarray(np.asarray(L0_ffn_U1, np.float32)),
        "v1": np.ascontiguousarray(np.asarray(L0_ffn_V1, np.float32)),
        "u2": np.ascontiguousarray(np.asarray(L0_ffn_U2, np.float32)),
        "v2": np.ascontiguousarray(np.asarray(L0_ffn_V2, np.float32)),
        "lng": np.ascontiguousarray(np.asarray(L0_attn_ln_g, np.float32).reshape(8, 128).T),
        "lnb": np.ascontiguousarray(np.asarray(L0_attn_ln_b, np.float32).reshape(8, 128).T),
        "bqkv": np.ascontiguousarray(np.asarray(L0_attn_b_qkv, np.float32).reshape(1, 3 * D)),
        "bg": np.ascontiguousarray(np.asarray(L0_attn_b_gate, np.float32).reshape(1, D)),
        "bp": np.ascontiguousarray(np.asarray(L0_attn_b_proj, np.float32).reshape(1, D)),
        "b1c": np.ascontiguousarray(np.asarray(L0_ffn_b1, np.float32).reshape(32, 128).T),
        "b2": np.ascontiguousarray(np.asarray(L0_ffn_b2, np.float32).reshape(1, D)),
        "id16": ident16,
        "id32": ident32,
        "mask": mask,
    }
    in_maps = []
    for core in range(8):
        b, h = core // 2, core % 2
        m = dict(common)
        m["x"] = np.ascontiguousarray(x[b, h * TC:(h + 1) * TC, :])
        m["wprev"] = np.full((128, 1), float(h), np.float32)
        in_maps.append(m)
    res = run_bass_kernel_spmd(nc, in_maps, core_ids=list(range(8)))
    out = np.empty((B, T, D), np.float32)
    for core in range(8):
        b, h = core // 2, core % 2
        out[b, h * TC:(h + 1) * TC, :] = res.results[core]["out"]
    return out


# version 24
# speedup vs baseline: 1.0140x; 1.0140x over previous
import numpy as np
import ml_dtypes
from contextlib import ExitStack

import concourse.bass as bass
import concourse.mybir as mybir
from concourse.bass_core_rust import BassInstruction
from concourse.bass_utils import run_bass_kernel_spmd

F32 = mybir.dt.float32
BF16 = mybir.dt.bfloat16
ACT = mybir.ActivationFunctionType
ALU = mybir.AluOpType
AXX = mybir.AxisListType.X

DEBUG_TAPS = False

B, T, D = 4, 4096, 1024
HEADS, DH, NPAIR = 16, 64, 8
DFF, RANK = 4096, 256
EPS = 1e-5
TC, NCH, CH = 2048, 16, 128
NSC, SCT = 4, 512
NKC = 8
BF = ml_dtypes.bfloat16


def build_program():
    nc = bass.Bass("TRN2", target_bir_lowering=False, detect_race_conditions=False)

    d_x = nc.dram_tensor("x", [TC, D], F32, kind="ExternalInput").ap()
    d_wqkv = nc.dram_tensor("wqkv", [D, 3 * D], F32, kind="ExternalInput").ap()
    d_wg = nc.dram_tensor("wg", [D, D], F32, kind="ExternalInput").ap()
    d_wp = nc.dram_tensor("wp", [D, D], F32, kind="ExternalInput").ap()
    d_u1 = nc.dram_tensor("u1", [D, RANK], F32, kind="ExternalInput").ap()
    d_v1 = nc.dram_tensor("v1", [RANK, DFF], F32, kind="ExternalInput").ap()
    d_u2 = nc.dram_tensor("u2", [DFF, RANK], F32, kind="ExternalInput").ap()
    d_v2 = nc.dram_tensor("v2", [RANK, D], F32, kind="ExternalInput").ap()
    d_lng = nc.dram_tensor("lng", [128, 8], F32, kind="ExternalInput").ap()
    d_lnb = nc.dram_tensor("lnb", [128, 8], F32, kind="ExternalInput").ap()
    d_bqkv = nc.dram_tensor("bqkv", [1, 3 * D], F32, kind="ExternalInput").ap()
    d_bg = nc.dram_tensor("bg", [1, D], F32, kind="ExternalInput").ap()
    d_bp = nc.dram_tensor("bp", [1, D], F32, kind="ExternalInput").ap()
    d_b1 = nc.dram_tensor("b1c", [128, 32], F32, kind="ExternalInput").ap()
    d_b2 = nc.dram_tensor("b2", [1, D], F32, kind="ExternalInput").ap()
    d_id16 = nc.dram_tensor("id16", [128, 128], BF16, kind="ExternalInput").ap()
    d_id32 = nc.dram_tensor("id32", [128, 128], F32, kind="ExternalInput").ap()
    d_mask = nc.dram_tensor("mask", [128, 128], F32, kind="ExternalInput").ap()
    d_wprev = nc.dram_tensor("wprev", [128, 1], F32, kind="ExternalInput").ap()
    d_out_dram = nc.dram_tensor("out", [TC, D], F32, kind="ExternalOutput").ap()

    d_q4spill = nc.dram_tensor("q4spill", [NCH, 128, D], BF16)
    d_ccin = nc.dram_tensor("ccin", [128, 520], F32)
    d_ccout = nc.dram_tensor("ccout", [2, 128, 520], F32)

    taps = {}
    if DEBUG_TAPS:
        for nm, shp, dt in [("t_z", [128, D], BF16), ("t_gate", [128, D], F32),
                            ("t_q4", [128, D], BF16), ("t_k4", [128, D], BF16),
                            ("t_v", [128, HEADS * 65], BF16), ("t_snap", [128, 520], BF16),
                            ("t_o", [128, D], BF16), ("t_x2", [128, D], F32),
                            ("t_ofm", [128, 256], BF16)]:
            taps[nm] = nc.dram_tensor(nm, shp, dt, kind="ExternalOutput").ap()

    last_inst = {}
    nc.push_inst_callback(lambda ins: last_inst.__setitem__(ins.engine, ins))
    cnt = {"DS": 0, "DG": 0, "TT": 0, "VV": 0, "SS": 0, "CC": 0, "ST": 0,
       "DSX0": 0, "DSX1": 0, "DSP": 0, "DSQ0": 0, "DSQ1": 0, "DSXD": 0, "DSO": 0, "GG": 0,
       "DGE": 0, "DG2": 0, "DSO1": 0}

    with ExitStack() as top:
        e = top.enter_context
        sems = {n: e(nc.semaphore(n)) for n in cnt}
        DS, DG, TT, VV, SS, CC, ST = (sems[n] for n in ["DS", "DG", "TT", "VV", "SS", "CC", "ST"])

        def bump(name, amt=1):
            cnt[name] += amt
            return cnt[name]

        def mark(eng, name):
            BassInstruction(last_inst[eng.engine]).then_inc(sems[name], 1)
            return bump(name)

        def wtok(eng, tok):
            eng.wait_ge(sems[tok[0]], tok[1])

        def chain(eng, name="VV"):
            BassInstruction(last_inst[eng.engine]).then_inc(sems[name], 1)
            bump(name)
            eng.wait_ge(sems[name], cnt[name])

        sb_k4fm = e(nc.sbuf_tensor("k4fm", [128, NKC * TC], BF16))
        sb_vtok = e(nc.sbuf_tensor("vtok", [128, NCH * HEADS * 65], BF16))
        sb_snaps = e(nc.sbuf_tensor("snaps", [128, NCH * 520], BF16))
        sb_ident = e(nc.sbuf_tensor("sident", [128, 128], BF16))
        sb_id32 = e(nc.sbuf_tensor("sid32", [128, 128], F32))
        sb_mask = e(nc.sbuf_tensor("smask", [128, 128], F32))
        sb_lng = e(nc.sbuf_tensor("slng", [128, 8], F32))
        sb_lnb = e(nc.sbuf_tensor("lnbv", [128, 8], BF16))
        sb_b1c = e(nc.sbuf_tensor("sb1c", [128, 32], F32))
        sb_eps = e(nc.sbuf_tensor("epsc", [128, 1], F32))
        sb_eps2 = e(nc.sbuf_tensor("eps2c", [128, 1], F32))
        sb_wprev = e(nc.sbuf_tensor("swprev", [128, 1], F32))
        sb_ones = e(nc.sbuf_tensor("ones_row", [1, 128], BF16))
        sb_rp = e(nc.sbuf_tensor("srp", [1, D], BF16))
        sb_rb2 = e(nc.sbuf_tensor("srb2", [1, D], BF16))
        sb_exch = e(nc.sbuf_tensor("sexch", [128, 520], F32))

        block = e(nc.Block())

        @block.sync
        def _(s):
            s.dma_start(out=sb_ident[:], in_=d_id16).then_inc(DS, 16)
            s.dma_start(out=sb_id32[:], in_=d_id32).then_inc(DS, 16)
            s.dma_start(out=sb_mask[:], in_=d_mask).then_inc(DS, 16)
            s.dma_start(out=sb_lng[:], in_=d_lng).then_inc(DS, 16)
            s.dma_start(out=sb_b1c[:], in_=d_b1).then_inc(DS, 16)
            s.dma_start(out=sb_wprev[:], in_=d_wprev).then_inc(DS, 16)
            bump("DS", 6 * 16)
        SETUP_DS = cnt["DS"]

        @block.gpsimd
        def _(g):
            g.dma_start(out=sb_lnb[:], in_=d_lnb).then_inc(DG, 16)
            g.dma_start(out=sb_rp[:], in_=d_bp).then_inc(DG, 16)
            g.dma_start(out=sb_rb2[:], in_=d_b2).then_inc(DG, 16)
            bump("DG", 3 * 16)
            g.memset(sb_ones[:], 1.0)
            g.memset(sb_vtok[:], 1.0).then_inc(sems["GG"], 1)
            bump("GG", 1)
        SETUP_GG = cnt["GG"]

        @block.vector
        def _(v):
            v.memset(sb_eps[:], EPS)
            v.memset(sb_eps2[:], EPS * EPS)
            mark(v, "VV")
        EPS_VV = cnt["VV"]

        with ExitStack() as aa:
            ea = aa.enter_context
            sb_wqkv = ea(nc.sbuf_tensor("wqkv_bf", [128, NKC * 3 * D], BF16))
            sb_wgate = ea(nc.sbuf_tensor("wg_bf", [128, NKC * D], BF16))
            sb_rq = ea(nc.sbuf_tensor("srq", [1, 3 * D], BF16))
            sb_rg = ea(nc.sbuf_tensor("srg", [1, D], BF16))
            sb_r2row = ea(nc.sbuf_tensor("r2row", [1, 256], BF16))
            sb_x = ea(nc.sbuf_tensor("xa", [128, 2 * D], F32))
            sb_ztok = ea(nc.sbuf_tensor("ztok", [128, 2 * D], BF16))
            sb_gate = ea(nc.sbuf_tensor("gate", [128, D], F32))
            sb_gaten = ea(nc.sbuf_tensor("gaten", [128, D], F32))
            sb_tmp1 = ea(nc.sbuf_tensor("tmp1", [128, D], F32))
            sb_tmp2 = ea(nc.sbuf_tensor("tmp2", [128, D], F32))
            sb_q4tok = ea(nc.sbuf_tensor("q4tok", [128, 2 * D], BF16))
            sb_k4tok = ea(nc.sbuf_tensor("k4tok", [128, 2 * D], BF16))
            sb_q4st = ea(nc.sbuf_tensor("q4stage", [128, D], BF16))
            sb_stats = ea(nc.sbuf_tensor("stats", [128, 12], F32))
            sb_sm = ea(nc.sbuf_tensor("smalls", [128, 36], F32))
            psA = ea(nc.psum_tensor("psA", [128, 2048], F32))
            psT = ea(nc.psum_tensor("psT", [128, 2 * 1024], BF16))
            psSA = ea(nc.psum_tensor("psSA", [128, 512], F32))
            psSB = ea(nc.psum_tensor("psSB", [128, 512], F32))

            atok = {0: ("VV", 0), 1: ("VV", 0), 2: ("VV", 0), 3: ("VV", 0),
                    "t0": ("VV", 0), "r2": ("VV", 0)}

            @block.gpsimd
            def _(g):
                for kc in range(NKC):
                    g.dma_start(out=sb_wgate[:, kc * D:(kc + 1) * D],
                                in_=d_wg[kc * 128:(kc + 1) * 128, :]).then_inc(DG, 16)
                g.dma_start(out=sb_rg[:], in_=d_bg).then_inc(DG, 16)
                bump("DG", 9 * 16)
                for kc in range(NKC):
                    g.dma_start(out=sb_wqkv[:, kc * 3 * D:(kc + 1) * 3 * D],
                                in_=d_wqkv[kc * 128:(kc + 1) * 128, :],
                                max_dma_last_dim=8000).then_inc(DG, 16)
                bump("DG", 8 * 16)
            WQKV_DG = cnt["DG"]

            if True:
                bq_rows = [sb_gate, sb_gaten, sb_tmp1]

                @block.sync
                def _(s):
                    for i in range(3):
                        s.dma_start(out=bq_rows[i][0:1, 0:1024],
                                    in_=d_bqkv[:, i * 1024:(i + 1) * 1024]).then_inc(DS, 16)
                    bump("DS", 48)
                BQ_DS = cnt["DS"]

                rq_tt = []
                rq_vv = []
                rq_vv_base = cnt["VV"]

                @block.tensor
                def _(t):
                    t.wait_ge(DG, WQKV_DG)
                    for nw in range(6):
                        if nw >= 3:
                            t.wait_ge(VV, rq_vv_base + (nw - 3) + 1)
                        dst = psA[:1, (nw % 3) * 512:(nw % 3) * 512 + 512]
                        for kc in range(NKC):
                            t.matmul(dst, sb_lnb[:, kc:kc + 1],
                                     sb_wqkv[:, kc * 3 * D + nw * 512: kc * 3 * D + (nw + 1) * 512],
                                     start=(kc == 0), stop=(kc == NKC - 1))
                        rq_tt.append(mark(t, "TT"))

                @block.vector
                def _(v):
                    v.wait_ge(DS, BQ_DS)
                    for nw in range(6):
                        v.wait_ge(TT, rq_tt[nw])
                        v.tensor_add(sb_rq[:, nw * 512:(nw + 1) * 512],
                                     psA[:1, (nw % 3) * 512:(nw % 3) * 512 + 512],
                                     bq_rows[nw // 2][0:1, (nw % 2) * 512:(nw % 2) * 512 + 512])
                        rq_vv.append(mark(v, "VV"))
                        assert rq_vv[-1] == rq_vv_base + len(rq_vv)
                RQ_VV = cnt["VV"]
                for b_ in range(4):
                    atok[b_] = ("VV", RQ_VV)

            @block.scalar
            def _(s):
                s.wait_ge(TT, rq_tt[-1])
                s.wait_ge(DS, BQ_DS)
                for kc in range(NKC):
                    w = sb_wqkv[:, kc * 3 * D:(kc + 1) * 3 * D]
                    s.activation(w, w, ACT.Copy, scale=sb_lng[:, kc:kc + 1])
                mark(s, "SS")
            WSC_SS = cnt["SS"]

            snap_vv = {}
            state_tt = {}
            spill_ds = {}
            z_vv = {}
            stats_vv = {}
            sm_ss = {}
            zt_tt = {}
            gn_vv = {}
            q4k4_vv = [0]
            q4t_tt_prev = [0]
            k4s_tt_prev = [0]

            def emit_s1(c):
                par = c % 2
                smo = (c % 3) * 12
                xb = sb_x[:, par * D:(par + 1) * D]
                zt = sb_ztok[:, par * D:(par + 1) * D]
                dsx = "DSX0" if par == 0 else "DSX1"

                @block.sync
                def _(s):
                    if c >= 2:
                        s.wait_ge(VV, z_vv[c - 2])
                    s.dma_start(out=xb, in_=d_x[c * 128:(c + 1) * 128, :]).then_inc(sems[dsx], 16)
                    bump(dsx, 16)
                X_DS = cnt[dsx]

                @block.vector
                def _(v):
                    v.wait_ge(sems[dsx], X_DS)
                    if c >= 3:
                        v.wait_ge(TT, zt_tt[c - 3])
                    v.bn_stats(sb_stats[:, 0:6], xb[:, 0:512])
                    v.bn_stats(sb_stats[:, 6:12], xb[:, 512:1024])
                    chain(v)
                    v.bn_aggr(sb_sm[:, smo + 0:smo + 2], sb_stats[:, 0:12])
                    mark(v, "VV")
                stats_vv[c] = cnt["VV"]

                @block.scalar
                def _(s):
                    s.wait_ge(VV, stats_vv[c])
                    s.wait_ge(VV, EPS_VV)
                    s.activation(sb_sm[:, smo + 2:smo + 3], sb_sm[:, smo + 1:smo + 2], ACT.Sqrt,
                                 scale=float(1.0 + EPS), bias=sb_eps2[:, 0:1])
                    s.activation(sb_sm[:, smo + 3:smo + 4], sb_sm[:, smo + 1:smo + 2], ACT.Sqrt,
                                 bias=sb_eps[:, 0:1])
                    mark(s, "SS")
                sm_ss[c] = cnt["SS"]

                @block.vector
                def _(v):
                    v.wait_ge(SS, sm_ss[c])
                    v.reciprocal(sb_sm[:, smo + 8:smo + 9], sb_sm[:, smo + 2:smo + 3])
                    v.reciprocal(sb_sm[:, smo + 4:smo + 5], sb_sm[:, smo + 3:smo + 4])
                    chain(v)
                    v.tensor_mul(sb_sm[:, smo + 10:smo + 11], sb_sm[:, smo + 3:smo + 4], sb_sm[:, smo + 8:smo + 9])
                    v.tensor_mul(sb_sm[:, smo + 9:smo + 10], sb_sm[:, smo + 2:smo + 3], sb_sm[:, smo + 4:smo + 5])
                    v.tensor_scalar(sb_sm[:, smo + 11:smo + 12], sb_sm[:, smo + 0:smo + 1],
                                    sb_sm[:, smo + 8:smo + 9], -1.0, ALU.mult, ALU.mult)
                    chain(v)
                    if c >= 2:
                        v.wait_ge(TT, vproj_tt[c - 2])
                    v.tensor_scalar(zt, xb, sb_sm[:, smo + 8:smo + 9],
                                    sb_sm[:, smo + 11:smo + 12], ALU.mult, ALU.add)
                    mark(v, "VV")
                z_vv[c] = cnt["VV"]

            def emit_s1b(c):
                par = c % 2
                smo = (c % 3) * 12
                zt = sb_ztok[:, par * D:(par + 1) * D]
                tp = psT[:, 0:1024]
                r2ps = psT[:1, 1024:1280].bitcast(F32)
                r2sl = sb_r2row[:, par * 128:(par + 1) * 128]

                @block.tensor
                def _(t):
                    t.wait_ge(VV, z_vv[c])
                    wtok(t, atok["t0"])
                    for kc in range(NKC):
                        t.transpose(tp[:, kc * 128:(kc + 1) * 128], zt[:, kc * 128:(kc + 1) * 128], sb_ident[:])
                    wtok(t, atok["r2"])
                    t.transpose(r2ps, sb_sm[:, smo + 10:smo + 11], sb_id32[:])
                    mark(t, "TT")
                zt_tt[c] = cnt["TT"]

                @block.vector
                def _(v):
                    v.wait_ge(TT, zt_tt[c])
                    v.tensor_copy(zt, tp[:, 0:1024])
                    v.tensor_copy(r2sl, r2ps)
                    mark(v, "VV")
                zf_vv[c] = cnt["VV"]
                atok["t0"] = ("VV", cnt["VV"])
                atok["r2"] = ("VV", cnt["VV"])

            def emit_s2a(c):
                par = c % 2
                smo = (c % 3) * 12
                zf = sb_ztok[:, par * D:(par + 1) * D]
                r2sl = sb_r2row[:, par * 128:(par + 1) * 128]
                ZF_VV = zf_vv[c]

                @block.tensor
                def _(t):
                    t.wait_ge(VV, ZF_VV)
                    wtok(t, atok[0]); wtok(t, atok[1])
                    for nw in range(2):
                        dst = psA[:, nw * 512:(nw + 1) * 512]
                        for kc in range(NKC):
                            t.matmul(dst, zf[:, kc * 128:(kc + 1) * 128],
                                     sb_wgate[:, kc * D + nw * 512: kc * D + (nw + 1) * 512],
                                     start=(kc == 0), stop=False)
                        t.matmul(dst, r2sl, sb_rg[:, nw * 512:(nw + 1) * 512], start=False, stop=True)
                    mark(t, "TT")
                GATE_TT = cnt["TT"]

                @block.scalar
                def _(s):
                    s.wait_ge(TT, GATE_TT)
                    if c > 0:
                        s.wait_ge(VV, gn_vv[c - 1])
                    s.activation(sb_gate[:], psA[:, 0:1024], ACT.Sigmoid, scale=sb_sm[:, smo + 9:smo + 10],
                                 accum_out=sb_sm[:, smo + 5:smo + 6])
                    mark(s, "SS")
                SIG_SS = cnt["SS"]
                atok[0] = ("SS", SIG_SS)
                atok[1] = ("SS", SIG_SS)

                @block.vector
                def _(v):
                    v.wait_ge(SS, SIG_SS)
                    v.tensor_scalar(sb_sm[:, smo + 6:smo + 7], sb_sm[:, smo + 5:smo + 6], 1.0 / D, EPS, ALU.mult, ALU.add)
                    chain(v)
                    v.reciprocal(sb_sm[:, smo + 7:smo + 8], sb_sm[:, smo + 6:smo + 7])
                    chain(v)
                    v.tensor_scalar_mul(sb_gaten[:], sb_gate[:], sb_sm[:, smo + 7:smo + 8])
                    mark(v, "VV")
                GN_VV = cnt["VV"]

                def emit_proj(t, nw0, dsts):
                    for j, dst in enumerate(dsts):
                        nw = nw0 + j
                        for kc in range(NKC):
                            t.matmul(dst, zf[:, kc * 128:(kc + 1) * 128],
                                     sb_wqkv[:, kc * 3 * D + nw * 512: kc * 3 * D + (nw + 1) * 512],
                                     start=(kc == 0), stop=False)
                        t.matmul(dst, sb_ones[:, 0:128], sb_rq[:, nw * 512:(nw + 1) * 512],
                                 start=False, stop=True)

                @block.tensor
                def _(t):
                    if c == 0:
                        t.wait_ge(SS, WSC_SS)
                    wtok(t, atok[2]); wtok(t, atok[3])
                    emit_proj(t, 0, [psA[:, 1024:1536], psA[:, 1536:2048]])
                    mark(t, "TT")
                Q_TT = cnt["TT"]

                @block.tensor
                def _(t):
                    wtok(t, atok[0]); wtok(t, atok[1])
                    emit_proj(t, 2, [psA[:, 0:512], psA[:, 512:1024]])
                    mark(t, "TT")
                K_TT = cnt["TT"]
                s2state[c] = (GN_VV, Q_TT, K_TT, emit_proj)

            def emit_s2b(c):
                par = c % 2
                zf = sb_ztok[:, par * D:(par + 1) * D]
                q4s = sb_q4st[:, 0:D]
                tp = psT[:, 0:1024]
                GN_VV, Q_TT, K_TT, emit_proj = s2state[c]

                @block.scalar
                def _(s):
                    s.wait_ge(TT, Q_TT)
                    if c > 0:
                        s.wait_ge(VV, q4k4_vv[0])
                    s.activation(sb_tmp1[:], psA[:, 1024:2048], ACT.Relu)
                    mark(s, "SS")
                QR_SS = cnt["SS"]
                atok[2] = ("SS", QR_SS)
                atok[3] = ("SS", QR_SS)

                @block.vector
                def _(v):
                    v.wait_ge(SS, QR_SS)
                    v.wait_ge(VV, GN_VV)
                    v.tensor_mul(sb_tmp1[:], sb_tmp1[:], sb_gaten[:])
                    mark(v, "VV")
                QG_VV = cnt["VV"]

                @block.scalar
                def _(s):
                    s.wait_ge(TT, K_TT)
                    s.activation(sb_tmp2[:], psA[:, 0:1024], ACT.Relu)
                    mark(s, "SS")
                KR_SS = cnt["SS"]
                atok[0] = ("SS", KR_SS)
                atok[1] = ("SS", KR_SS)

                @block.vector
                def _(v):
                    v.wait_ge(SS, KR_SS)
                    v.tensor_mul(sb_tmp2[:], sb_tmp2[:], sb_gaten[:])
                    mark(v, "VV")
                QKG_VV = cnt["VV"]
                gn_vv[c] = QKG_VV

                @block.scalar
                def _(s):
                    s.wait_ge(VV, QG_VV)
                    s.activation(sb_tmp1[:], sb_tmp1[:], ACT.Square)
                    mark(s, "SS")
                QS_SS = cnt["SS"]

                @block.vector
                def _(v):
                    v.wait_ge(SS, QS_SS)
                    if c >= 2:
                        v.wait_ge(TT, q4t_tt_by[c - 2])
                    v.tensor_scalar(sb_q4tok[:, (c % 2) * D:(c % 2 + 1) * D], sb_tmp1[:], EPS, None, ALU.add)
                    mark(v, "VV")
                Q4_VV = cnt["VV"]
                q4_vv_by[c] = Q4_VV

                @block.scalar
                def _(s):
                    s.wait_ge(VV, QKG_VV)
                    s.activation(sb_tmp2[:], sb_tmp2[:], ACT.Square)
                    mark(s, "SS")
                KS_SS = cnt["SS"]

                @block.vector
                def _(v):
                    v.wait_ge(SS, KS_SS)
                    if c >= 2:
                        v.wait_ge(TT, state_tt_by[c - 2])
                    v.tensor_scalar(sb_k4tok[:, (c % 2) * D:(c % 2 + 1) * D], sb_tmp2[:], EPS, None, ALU.add)
                    mark(v, "VV")
                K4_VV = cnt["VV"]
                k4_vv_by[c] = K4_VV
                q4k4_vv[0] = cnt["VV"]

                @block.tensor
                def _(t):
                    wtok(t, atok[2]); wtok(t, atok[3])
                    emit_proj(t, 4, [psA[:, 1024:1536], psA[:, 1536:2048]])
                    mark(t, "TT")
                V_TT = cnt["TT"]
                vproj_tt[c] = V_TT
                s2tail[c] = True

                @block.scalar
                def _(s):
                    s.wait_ge(TT, V_TT)
                    s.wait_ge(sems["GG"], SETUP_GG)
                    dst = sb_vtok[:, c * HEADS * 65:(c + 1) * HEADS * 65].rearrange("p (h w) -> p h w", w=65)[:, :, 0:64]
                    s.activation(dst[:, 0:8, :], psA[:, 1024:1536].rearrange("p (h w) -> p h w", w=64), ACT.Copy)
                    s.activation(dst[:, 8:16, :], psA[:, 1536:2048].rearrange("p (h w) -> p h w", w=64), ACT.Copy)
                    mark(s, "SS")
                VE_SS = cnt["SS"]
                ve_ss_by[c] = VE_SS
                atok[2] = ("SS", VE_SS)
                atok[3] = ("SS", VE_SS)

            def emit_s2c(c):
                par = c % 2
                q4s = sb_q4st[:, 0:D]
                tp = psT[:, 0:1024]
                q4slot = sb_q4tok[:, par * D:(par + 1) * D]
                k4slot = sb_k4tok[:, par * D:(par + 1) * D]

                @block.tensor
                def _(t):
                    t.wait_ge(VV, q4_vv_by[c])
                    wtok(t, atok["t0"])
                    for kc in range(NKC):
                        t.transpose(tp[:, kc * 128:(kc + 1) * 128], q4slot[:, kc * 128:(kc + 1) * 128], sb_ident[:])
                    q4t_tt_by[c] = mark(t, "TT")
                Q4T_TT = cnt["TT"]

                @block.vector
                def _(v):
                    v.wait_ge(TT, Q4T_TT)
                    if c >= 1:
                        v.wait_ge(sems["DSP"], spill_ds[c - 1])
                    v.tensor_copy(q4s, tp[:, 0:1024])
                    mark(v, "VV")
                Q4F_VV = cnt["VV"]
                atok["t0"] = ("VV", Q4F_VV)

                @block.tensor
                def _(t):
                    t.wait_ge(VV, k4_vv_by[c])
                    wtok(t, atok["t0"])
                    for kc in range(NKC):
                        t.transpose(tp[:, kc * 128:(kc + 1) * 128], k4slot[:, kc * 128:(kc + 1) * 128], sb_ident[:])
                    mark(t, "TT")
                K4T_TT = cnt["TT"]

                @block.vector
                def _(v):
                    v.wait_ge(TT, K4T_TT)
                    dst = sb_k4fm[:].rearrange("p (kc tok) -> p kc tok", tok=TC)[:, :, c * 128:(c + 1) * 128]
                    v.tensor_copy(dst, tp[:, 0:1024].rearrange("p (kc w) -> p kc w", w=128))
                    mark(v, "VV")
                K4F_VV = cnt["VV"]
                atok["t0"] = ("VV", K4F_VV)

                @block.sync
                def _(s):
                    s.wait_ge(VV, Q4F_VV)
                    s.dma_start(out=d_q4spill.ap()[c], in_=q4s).then_inc(sems["DSP"], 16)
                    bump("DSP", 16)
                spill_ds[c] = cnt["DSP"]

                @block.tensor
                def _(t):
                    t.wait_ge(SS, ve_ss_by[c])
                    if c > 0:
                        t.wait_ge(VV, snap_vv[c - 1])
                    for h in range(HEADS):
                        p, half = h // 2, h % 2
                        if p < 7:
                            dst = psSA[half * 64:(half + 1) * 64, p * 65:(p + 1) * 65]
                        else:
                            dst = psSB[half * 64:(half + 1) * 64, 0:65]
                        st = (c == 0) and (h in (0, 1, 14, 15))
                        t.matmul(dst, k4slot[:, h * 64:(h + 1) * 64],
                                 sb_vtok[:, c * HEADS * 65 + h * 65: c * HEADS * 65 + (h + 1) * 65],
                                 start=st, stop=(c == NCH - 1 and h >= 14), skip_group_check=True)
                    state_tt_by[c] = mark(t, "TT")
                state_tt[c] = cnt["TT"]

                @block.vector
                def _(v):
                    v.wait_ge(TT, state_tt[c])
                    if c < NCH - 1:
                        slot = sb_snaps[:, (c + 1) * 520:(c + 2) * 520]
                        v.tensor_copy(slot[:, 0:455], psSA[:, 0:455])
                        v.tensor_copy(slot[:, 455:520], psSB[:, 0:65])
                    else:
                        v.tensor_copy(sb_exch[:, 0:455], psSA[:, 0:455])
                        v.tensor_copy(sb_exch[:, 455:520], psSB[:, 0:65])
                    mark(v, "VV")
                snap_vv[c] = cnt["VV"]

            s2state = {}
            zf_vv = {}
            vproj_tt = {}
            s2tail = {}
            q4t_tt_by = {}
            state_tt_by = {}
            q4_vv_by = {}
            k4_vv_by = {}
            ve_ss_by = {}
            emit_s1(0)
            emit_s1b(0)
            for c in range(NCH):
                if c + 1 < NCH:
                    emit_s1(c + 1)
                emit_s2a(c)
                if c + 1 < NCH:
                    emit_s1b(c + 1)
                emit_s2b(c)
                if c >= 1:
                    emit_s2c(c - 1)
            emit_s2c(NCH - 1)

            A_SNAP = snap_vv[NCH - 1]
            ALL_SPILLS = cnt["DSP"]

        with ExitStack() as dd:
            ed = dd.enter_context
            sb_wp = ed(nc.sbuf_tensor("wp_bf", [128, NKC * D], BF16))
            sb_u1 = ed(nc.sbuf_tensor("u1_bf", [128, NKC * RANK], BF16))
            sb_v1 = ed(nc.sbuf_tensor("v1_bf", [128, 2 * DFF], BF16))
            sb_u2 = ed(nc.sbuf_tensor("u2_bf", [128, 32 * RANK], BF16))
            sb_v2 = ed(nc.sbuf_tensor("v2_bf", [128, 2 * D], BF16))
            sb_q4pf = ed(nc.sbuf_tensor("q4pf", [128, 2 * D], BF16))
            sb_am = ed(nc.sbuf_tensor("am", [128, HEADS * 128], BF16))
            sb_den = ed(nc.sbuf_tensor("den", [128, 16], F32))
            sb_rec = ed(nc.sbuf_tensor("recip", [128, 16], F32))
            sb_osb = ed(nc.sbuf_tensor("osb", [128, D], BF16))
            sb_ofm = ed(nc.sbuf_tensor("ofm", [128, 2 * NKC * 256], BF16))
            sb_x2 = ed(nc.sbuf_tensor("x2", [128, 2 * D], F32))
            sb_z3 = ed(nc.sbuf_tensor("z3", [128, D], BF16))
            sb_z3fm = ed(nc.sbuf_tensor("z3fm", [128, NKC * 256], BF16))
            sb_psb = ed(nc.sbuf_tensor("psb", [128, 512], BF16))
            sb_hsb = ed(nc.sbuf_tensor("hsb", [128, 32 * 256], BF16))
            sb_qsb = ed(nc.sbuf_tensor("qsb", [128, 512], BF16))
            sb_x3 = ed(nc.sbuf_tensor("x3", [128, 2 * D], F32))
            sb_st2 = ed(nc.sbuf_tensor("st2", [128, 12], F32))
            sb_sm2 = ed(nc.sbuf_tensor("sm2", [128, 6], F32))
            psC = ed(nc.psum_tensor("psC", [128, 4096], F32))

            btok = {b_: ("VV", 0) for b_ in range(8)}

            def bw(t, *banks):
                seen = set()
                for b_ in banks:
                    if btok[b_] not in seen:
                        seen.add(btok[b_])
                        wtok(t, btok[b_])

            def bs(tokval, *banks):
                for b_ in banks:
                    btok[b_] = tokval

            @block.scalar
            def _(s):
                s.wait_ge(VV, A_SNAP)
                s.dma_start(out=d_ccin.ap(), in_=sb_exch[:]).then_inc(sems["DGE"], 16)
                bump("DGE", 16)
            CCIN_DGE = cnt["DGE"]

            @block.gpsimd
            def _(g):
                g.wait_ge(sems["DGE"], CCIN_DGE)
                g.collective_compute("AllGather", ALU.bypass,
                                     replica_groups=[[0, 1], [2, 3], [4, 5], [6, 7]],
                                     ins=[d_ccin.ap().opt()],
                                     outs=[d_ccout.ap().opt()]).then_inc(CC, 1)
                bump("CC", 1)
            EXCH_CC = cnt["CC"]

            @block.scalar
            def _(s):
                s.wait_ge(CC, EXCH_CC)
                s.dma_start(out=sb_exch[:], in_=d_ccout.ap()[0]).then_inc(sems["DGE"], 16)
                bump("DGE", 16)
            EXCH_DG = cnt["DGE"]

            snapfix_vv = {}

            @block.vector
            def _(v):
                v.wait_ge(sems["DGE"], EXCH_DG)
                v.tensor_scalar_mul(sb_exch[:], sb_exch[:], sb_wprev[:])
                chain(v)
                v.tensor_copy(sb_snaps[:, 0:520], sb_exch[:])
                snapfix_vv[0] = mark(v, "VV")
                for c in range(1, NCH):
                    slot = sb_snaps[:, c * 520:(c + 1) * 520]
                    v.tensor_add(slot, slot, sb_exch[:])
                    snapfix_vv[c] = mark(v, "VV")
            SNAPFIX_VV = cnt["VV"]

            @block.gpsimd
            def _(g):
                g.wait_ge(VV, A_SNAP)
                g.wait_ge(sems["DGE"], 32)
                for kc in range(NKC):
                    g.dma_start(out=sb_wp[:, kc * D:(kc + 1) * D],
                                in_=d_wp[kc * 128:(kc + 1) * 128, :]).then_inc(DG, 16)
                    g.dma_start(out=sb_u1[:, kc * RANK:(kc + 1) * RANK],
                                in_=d_u1[kc * 128:(kc + 1) * 128, :]).then_inc(DG, 16)
                    bump("DG", 32)
                for rc in range(2):
                    g.dma_start(out=sb_v1[:, rc * DFF:(rc + 1) * DFF],
                                in_=d_v1[rc * 128:(rc + 1) * 128, :],
                                max_dma_last_dim=8000).then_inc(DG, 16)
                    g.dma_start(out=sb_v2[:, rc * D:(rc + 1) * D],
                                in_=d_v2[rc * 128:(rc + 1) * 128, :]).then_inc(DG, 16)
                    bump("DG", 32)
            DW_DG = cnt["DG"]

            @block.gpsimd
            def _(g):
                for fc in range(32):
                    g.dma_start(out=sb_u2[:, fc * RANK:(fc + 1) * RANK],
                                in_=d_u2[fc * 128:(fc + 1) * 128, :]).then_inc(sems["DG2"], 16)
                    bump("DG2", 16)
            DW_DG2 = cnt["DG2"]


            if DEBUG_TAPS:
                @block.sync
                def _(s):
                    s.wait_ge(VV, SNAPFIX_VV)
                    s.dma_start(out=taps["t_snap"], in_=sb_snaps[:, 520:1040]).then_inc(ST, 16)
                    bump("ST", 16)

            q4use_tt = {}
            ofm_ss = {}
            proj_done = [0]
            ot_tt = {}
            am_vv = {}
            div_vv = {}

            def c_a(c):
                par = c % 2
                q4p = sb_q4pf[:, par * D:(par + 1) * D]
                dsq = "DSQ0" if par == 0 else "DSQ1"

                @block.sync
                def _(s):
                    if c >= 2:
                        s.wait_ge(TT, q4use_tt[c - 2])
                    else:
                        s.wait_ge(sems["DSP"], ALL_SPILLS)
                    s.dma_start(out=q4p, in_=d_q4spill.ap()[c]).then_inc(sems[dsq], 16)
                    bump(dsq, 16)
                Q4P_DS = cnt[dsq]

                k4 = sb_k4fm[:].rearrange("p (kc tok) -> p kc tok", tok=TC)
                q4 = q4p.rearrange("p (kc w) -> p kc w", w=128)

                @block.tensor
                def _(t):
                    t.wait_ge(sems[dsq], Q4P_DS)
                    bw(t, 0, 1, 2, 3)
                    for p in range(NPAIR):
                        t.matmul(psC[:, p * 128:(p + 1) * 128],
                                 k4[0:64, p, c * 128:(c + 1) * 128], q4[0:64, p, :],
                                 start=True, stop=True, tile_position=(0, 0))
                        t.matmul(psC[:, 1024 + p * 128:1024 + (p + 1) * 128],
                                 k4[64:128, p, c * 128:(c + 1) * 128], q4[64:128, p, :],
                                 start=True, stop=True, tile_position=(64, 0))
                    mark(t, "TT")
                ATT_TT = cnt["TT"]

                @block.vector
                def _(v):
                    v.wait_ge(TT, ATT_TT)
                    amv = sb_am[:].rearrange("p (h w) -> p h w", w=128)
                    mb = sb_mask[:].unsqueeze(1).to_broadcast([128, 8, 128])
                    v.tensor_tensor(amv[:, 0::2, :], psC[:, 0:1024].rearrange("p (h w) -> p h w", w=128), mb, ALU.mult)
                    v.tensor_tensor(amv[:, 1::2, :], psC[:, 1024:2048].rearrange("p (h w) -> p h w", w=128), mb, ALU.mult)
                    mark(v, "VV")
                am_vv[c] = cnt["VV"]
                bs(("VV", am_vv[c]), 0, 1, 2, 3)

            def c_b(c):
                par = c % 2
                q4p = sb_q4pf[:, par * D:(par + 1) * D]
                q4 = q4p.rearrange("p (kc w) -> p kc w", w=128)

                @block.tensor
                def _(t):
                    t.wait_ge(VV, am_vv[c])
                    for h in range(HEADS):
                        i = h // 2
                        slot = (h % 2) * 1024 + i * 128
                        t.matmul(psC[:, slot:slot + 65], sb_am[:, h * 128:(h + 1) * 128],
                                 sb_vtok[:, c * HEADS * 65 + h * 65: c * HEADS * 65 + (h + 1) * 65],
                                 start=(h in (0, 1, 8, 9)), stop=False, skip_group_check=True)
                    t.wait_ge(VV, snapfix_vv[c])
                    for p in range(NPAIR):
                        off = c * 520 + (p * 65 if p < 7 else 455)
                        t.matmul(psC[:, p * 128:p * 128 + 65],
                                 q4[0:64, p, :], sb_snaps[0:64, off:off + 65],
                                 start=False, stop=True, tile_position=(0, 0), skip_group_check=True)
                        t.matmul(psC[:, 1024 + p * 128:1024 + p * 128 + 65],
                                 q4[64:128, p, :], sb_snaps[64:128, off:off + 65],
                                 start=False, stop=True, tile_position=(64, 0), skip_group_check=True)
                    mark(t, "TT")
                NUMM_TT = cnt["TT"]
                q4use_tt[c] = NUMM_TT

                @block.vector
                def _(v):
                    v.wait_ge(TT, NUMM_TT)
                    if c >= 1:
                        v.wait_ge(TT, ot_tt[c - 1])
                    dd_ = psC[:, 0:2048].rearrange("p (sl w) -> p sl w", w=128)[:, :, 64:65]
                    v.tensor_scalar(sb_den[:].unsqueeze(2), dd_, EPS, None, ALU.add)
                    chain(v)
                    v.reciprocal(sb_rec[:], sb_den[:])
                    chain(v)
                    ov = sb_osb[:].rearrange("p (h w) -> p h w", w=64)
                    ne = psC[:, 0:1024].rearrange("p (sl w) -> p sl w", w=128)[:, :, 0:64]
                    no = psC[:, 1024:2048].rearrange("p (sl w) -> p sl w", w=128)[:, :, 0:64]
                    v.tensor_tensor(ov[:, 0::2, :], ne,
                                    sb_rec[:, 0:8].unsqueeze(2).to_broadcast([128, 8, 64]), ALU.mult)
                    v.tensor_tensor(ov[:, 1::2, :], no,
                                    sb_rec[:, 8:16].unsqueeze(2).to_broadcast([128, 8, 64]), ALU.mult)
                    mark(v, "VV")
                div_vv[c] = cnt["VV"]
                bs(("VV", div_vv[c]), 0, 1, 2, 3)

            def c_c(c):
                @block.tensor
                def _(t):
                    t.wait_ge(VV, div_vv[c])
                    bw(t, 0)
                    dst = psC[:, 0:512].bitcast(BF16)
                    for kc in range(NKC):
                        t.transpose(dst[:, kc * 128:(kc + 1) * 128], sb_osb[:, kc * 128:(kc + 1) * 128], sb_ident[:])
                    mark(t, "TT")
                OT_TT = cnt["TT"]
                ot_tt[c] = OT_TT

            def c_c_v(c):
                OT_TT = ot_tt[c]

                @block.vector
                def _(v):
                    v.wait_ge(TT, OT_TT)
                    v.wait_ge(TT, proj_done[0])
                    src = psC[:, 0:512].bitcast(BF16).rearrange("p (kc w) -> p kc w", w=128)
                    hpar = (c // 2) % 2
                    dst = sb_ofm[:, hpar * NKC * 256:(hpar + 1) * NKC * 256]
                    dst = dst.rearrange("p (kc w) -> p kc w", w=256)[:, :, (c % 2) * 128:(c % 2) * 128 + 128]
                    v.tensor_copy(dst, src)
                    mark(v, "VV")
                bs(("VV", cnt["VV"]), 0)
                ofm_ss[c] = cnt["VV"]

            def d_proj(hs, j):
                c = hs * 2 + j
                hpar = hs % 2
                x2sl = sb_x2[:, j * D:(j + 1) * D]

                @block.sync
                def _(s):
                    if hs > 0:
                        s.wait_ge(VV, x2_free[j])
                    else:
                        s.wait_ge(sems["DSP"], ALL_SPILLS)
                    s.dma_start(out=x2sl, in_=d_x[c * 128:(c + 1) * 128, :]).then_inc(sems["DSXD"], 16)
                    bump("DSXD", 16)
                XD_DS = cnt["DSXD"]

                proj_nw = []

                @block.tensor
                def _(t):
                    t.wait_ge(VV, ofm_ss[hs * 2 + 1])
                    t.wait_ge(DG, DW_DG)
                    ofmv = sb_ofm[:, hpar * NKC * 256:(hpar + 1) * NKC * 256]
                    for nw in range(2):
                        bw(t, 4 + nw)
                        dst = psC[:, 2048 + nw * 512:2048 + (nw + 1) * 512]
                        for kc in range(NKC):
                            t.matmul(dst, ofmv[:, kc * 256 + j * 128: kc * 256 + (j + 1) * 128],
                                     sb_wp[:, kc * D + nw * 512:kc * D + (nw + 1) * 512],
                                     start=(kc == 0), stop=False)
                        t.matmul(dst, sb_ones[:, 0:128], sb_rp[:, nw * 512:(nw + 1) * 512],
                                 start=False, stop=True)
                        proj_nw.append(mark(t, "TT"))
                PROJ_TT = proj_nw[1]
                proj_done[0] = PROJ_TT

                add_m = []

                @block.vector
                def _(v):
                    v.wait_ge(TT, proj_nw[0])
                    v.wait_ge(sems["DSXD"], XD_DS)
                    v.tensor_add(x2sl[:, 0:512], psC[:, 2048:2560], x2sl[:, 0:512])
                    chain(v)
                    add_m.append(cnt["VV"])
                    v.bn_stats(sb_st2[:, 0:6], x2sl[:, 0:512])
                    v.wait_ge(TT, PROJ_TT)
                    v.tensor_add(x2sl[:, 512:1024], psC[:, 2560:3072], x2sl[:, 512:1024])
                    chain(v)
                    add_m.append(cnt["VV"])
                    v.bn_stats(sb_st2[:, 6:12], x2sl[:, 512:1024])
                    chain(v)
                    v.bn_aggr(sb_sm2[:, 0:2], sb_st2[:])
                    chain(v)
                    v.tensor_scalar(sb_sm2[:, 2:3], sb_sm2[:, 1:2], EPS, None, ALU.add)
                    chain(v)
                    v.reciprocal(sb_sm2[:, 3:4], sb_sm2[:, 2:3])
                    mark(v, "VV")
                LN3_VV = cnt["VV"]
                bs(("VV", add_m[0]), 4)
                bs(("VV", add_m[1]), 5)

                @block.scalar
                def _(s):
                    s.wait_ge(VV, LN3_VV)
                    s.activation(sb_sm2[:, 4:5], sb_sm2[:, 3:4], ACT.Sqrt)
                    mark(s, "SS")
                d_state[(hs, j)] = (cnt["SS"], x2sl)

            def d_z3(hs, j):
                R3_SS, x2sl = d_state[(hs, j)]

                @block.vector
                def _(v):
                    v.wait_ge(SS, R3_SS)
                    v.tensor_scalar(sb_sm2[:, 5:6], sb_sm2[:, 0:1], sb_sm2[:, 4:5], -1.0, ALU.mult, ALU.mult)
                    chain(v)
                    if hs + j > 0:
                        v.wait_ge(TT, z3t_used[0])
                    v.tensor_scalar(sb_z3[:], x2sl, sb_sm2[:, 4:5], sb_sm2[:, 5:6], ALU.mult, ALU.add)
                    mark(v, "VV")
                Z3_VV = cnt["VV"]

                @block.tensor
                def _(t):
                    t.wait_ge(VV, Z3_VV)
                    bw(t, 6)
                    dst = psC[:, 3072:3584].bitcast(BF16)
                    for kc in range(NKC):
                        t.transpose(dst[:, kc * 128:(kc + 1) * 128], sb_z3[:, kc * 128:(kc + 1) * 128], sb_ident[:])
                    z3t_used[0] = mark(t, "TT")
                Z3T_TT = cnt["TT"]

                @block.vector
                def _(v):
                    v.wait_ge(TT, Z3T_TT)
                    src = psC[:, 3072:3584].bitcast(BF16).rearrange("p (kc w) -> p kc w", w=128)
                    dst = sb_z3fm[:].rearrange("p (kc w) -> p kc w", w=256)[:, :, j * 128:(j + 1) * 128]
                    v.tensor_copy(dst, src)
                    mark(v, "VV")
                bs(("VV", cnt["VV"]), 6)
                z3f_vv[j] = cnt["VV"]

            def d_ffn(hs):
                TW = 256

                @block.tensor
                def _(t):
                    t.wait_ge(VV, z3f_vv[1])
                    bw(t, 7)
                    for mc in range(2):
                        dst = psC[:, 3584 + mc * TW:3584 + (mc + 1) * TW]
                        for kc in range(NKC):
                            t.matmul(dst, sb_u1[:, kc * RANK + mc * 128:kc * RANK + (mc + 1) * 128],
                                     sb_z3fm[:, kc * TW:(kc + 1) * TW],
                                     start=(mc == 0 and kc == 0), stop=(kc == NKC - 1), skip_group_check=True)
                    mark(t, "TT")
                U1_TT = cnt["TT"]

                @block.vector
                def _(v):
                    v.wait_ge(TT, U1_TT)
                    v.tensor_copy(sb_psb[:], psC[:, 3584:4096])
                    mark(v, "VV")
                P_VV = cnt["VV"]
                bs(("VV", P_VV), 7)

                ss_base = cnt["SS"]
                v1_tt = []

                @block.tensor
                def _(t):
                    t.wait_ge(VV, P_VV)
                    t.wait_ge(sems["DG2"], DW_DG2)
                    bw(t, 4, 5, 6)
                    for pr in range(16):
                        if pr >= 2:
                            t.wait_ge(SS, ss_base + pr - 1)
                        for half in range(2):
                            fc = pr * 2 + half
                            dst = psC[:, 2048 + (pr % 2) * 512 + half * TW:
                                      2048 + (pr % 2) * 512 + (half + 1) * TW]
                            for rc in range(2):
                                t.matmul(dst, sb_v1[:, rc * DFF + fc * 128:rc * DFF + (fc + 1) * 128],
                                         sb_psb[:, rc * TW:(rc + 1) * TW],
                                         start=(rc == 0), stop=(rc == 1))
                        v1_tt.append(mark(t, "TT"))
                        if pr >= 1:
                            t.wait_ge(SS, ss_base + pr)
                            for half in range(2):
                                fc = (pr - 1) * 2 + half
                                for mc in range(2):
                                    t.matmul(psC[:, 3072 + mc * TW:3072 + (mc + 1) * TW],
                                             sb_u2[:, fc * RANK + mc * 128:fc * RANK + (mc + 1) * 128],
                                             sb_hsb[:, fc * TW:(fc + 1) * TW],
                                             start=(fc == 0 and mc == 0), stop=False, skip_group_check=True)
                    t.wait_ge(SS, ss_base + 16)
                    for half in range(2):
                        fc = 30 + half
                        for mc in range(2):
                            t.matmul(psC[:, 3072 + mc * TW:3072 + (mc + 1) * TW],
                                     sb_u2[:, fc * RANK + mc * 128:fc * RANK + (mc + 1) * 128],
                                     sb_hsb[:, fc * TW:(fc + 1) * TW],
                                     start=False, stop=(fc == 31 and mc == 1), skip_group_check=True)
                    mark(t, "TT")
                U2_TT = cnt["TT"]

                @block.scalar
                def _(s):
                    for pr in range(16):
                        s.wait_ge(TT, v1_tt[pr])
                        src = psC[:, 2048 + (pr % 2) * 512:2048 + (pr % 2) * 512 + 512]
                        s.activation(sb_hsb[:, pr * 2 * TW:pr * 2 * TW + TW], src[:, 0:TW],
                                     ACT.Gelu_apprx_tanh, bias=sb_b1c[:, 2 * pr:2 * pr + 1])
                        s.activation(sb_hsb[:, pr * 2 * TW + TW:(pr + 1) * 2 * TW], src[:, TW:2 * TW],
                                     ACT.Gelu_apprx_tanh, bias=sb_b1c[:, 2 * pr + 1:2 * pr + 2])
                        assert mark(s, "SS") == ss_base + pr + 1
                bs(("SS", cnt["SS"]), 4, 5)
                return U2_TT

            def d_out(hs, U2_TT):
                TW = 256

                @block.vector
                def _(v):
                    v.wait_ge(TT, U2_TT)
                    v.tensor_copy(sb_qsb[:], psC[:, 3072:3584])
                    mark(v, "VV")
                QQ_VV = cnt["VV"]
                bs(("VV", QQ_VV), 6)

                for j in range(2):
                    c = hs * 2 + j
                    xpar = c % 2
                    x3b = sb_x3[:, xpar * D:(xpar + 1) * D]
                    dso = "DSO" if xpar == 0 else "DSO1"
                    _, x2sl = d_state[(hs, j)]

                    v2_nw = []

                    @block.tensor
                    def _(t):
                        t.wait_ge(VV, QQ_VV)
                        for nw in range(2):
                            bw(t, 4 + nw)
                            dst = psC[:, 2048 + nw * 512:2048 + (nw + 1) * 512]
                            for rc in range(2):
                                t.matmul(dst, sb_qsb[:, rc * TW + j * 128:rc * TW + (j + 1) * 128],
                                         sb_v2[:, rc * D + nw * 512:rc * D + (nw + 1) * 512],
                                         start=(rc == 0), stop=False)
                            t.matmul(dst, sb_ones[:, 0:128], sb_rb2[:, nw * 512:(nw + 1) * 512],
                                     start=False, stop=True)
                            v2_nw.append(mark(t, "TT"))

                    x3_m = []

                    @block.vector
                    def _(v):
                        v.wait_ge(TT, v2_nw[0])
                        if c >= 2:
                            v.wait_ge(sems[dso], x3_ds[xpar])
                        v.tensor_add(x3b[:, 0:512], psC[:, 2048:2560], x2sl[:, 0:512])
                        mark(v, "VV")
                        x3_m.append(cnt["VV"])
                        v.wait_ge(TT, v2_nw[1])
                        v.tensor_add(x3b[:, 512:1024], psC[:, 2560:3072], x2sl[:, 512:1024])
                        mark(v, "VV")
                        x3_m.append(cnt["VV"])
                    X3_VV = x3_m[1]
                    bs(("VV", x3_m[0]), 4)
                    bs(("VV", x3_m[1]), 5)
                    x2_free[j] = X3_VV

                    @block.scalar
                    def _(s):
                        s.wait_ge(VV, X3_VV)
                        s.dma_start(out=d_out_ap[c * 128:(c + 1) * 128, :], in_=x3b).then_inc(sems[dso], 16)
                        x3_ds[xpar] = bump(dso, 16)

            d_state = {}
            x2_free = {0: 0, 1: 0}
            z3t_used = [0]
            z3f_vv = {}
            x3_ds = {0: 0, 1: 0}
            d_out_ap = d_out_dram
            for c0 in (0, 1):
                c_a(c0); c_b(c0); c_c(c0); c_c_v(c0)
            for hs in range(8):
                if hs < 7:
                    c_a(2 * hs + 2)
                d_proj(hs, 0)
                if hs < 7:
                    c_b(2 * hs + 2)
                    c_c(2 * hs + 2)
                d_z3(hs, 0)
                if hs < 7:
                    c_c_v(2 * hs + 2)
                    c_a(2 * hs + 3)
                d_proj(hs, 1)
                if hs < 7:
                    c_b(2 * hs + 3)
                    c_c(2 * hs + 3)
                d_z3(hs, 1)
                if hs < 7:
                    c_c_v(2 * hs + 3)
                u2tt = d_ffn(hs)
                d_out(hs, u2tt)

            @block.sync
            def _(s):
                for nm in ["DS", "DSX0", "DSX1", "DSP", "DSQ0", "DSQ1", "DSXD", "DSO", "DSO1", "DG", "DGE", "DG2", "ST"]:
                    if cnt[nm]:
                        s.wait_ge(sems[nm], cnt[nm])

    return nc


def kernel(L0_attn_x, L0_attn_ln_g, L0_attn_ln_b, L0_attn_W_qkv, L0_attn_b_qkv,
           L0_attn_W_gate, L0_attn_b_gate, L0_attn_W_proj, L0_attn_b_proj,
           L0_ffn_U1, L0_ffn_V1, L0_ffn_b1, L0_ffn_U2, L0_ffn_V2, L0_ffn_b2):
    x = np.ascontiguousarray(np.asarray(L0_attn_x, dtype=np.float32))
    nc = build_program()
    ident16 = np.eye(128, dtype=BF)
    ident32 = np.eye(128, dtype=np.float32)
    mask = np.ascontiguousarray(np.triu(np.ones((128, 128), np.float32)))
    common = {
        "wqkv": np.ascontiguousarray(np.asarray(L0_attn_W_qkv, np.float32)),
        "wg": np.ascontiguousarray(np.asarray(L0_attn_W_gate, np.float32)),
        "wp": np.ascontiguousarray(np.asarray(L0_attn_W_proj, np.float32)),
        "u1": np.ascontiguousarray(np.asarray(L0_ffn_U1, np.float32)),
        "v1": np.ascontiguousarray(np.asarray(L0_ffn_V1, np.float32)),
        "u2": np.ascontiguousarray(np.asarray(L0_ffn_U2, np.float32)),
        "v2": np.ascontiguousarray(np.asarray(L0_ffn_V2, np.float32)),
        "lng": np.ascontiguousarray(np.asarray(L0_attn_ln_g, np.float32).reshape(8, 128).T),
        "lnb": np.ascontiguousarray(np.asarray(L0_attn_ln_b, np.float32).reshape(8, 128).T),
        "bqkv": np.ascontiguousarray(np.asarray(L0_attn_b_qkv, np.float32).reshape(1, 3 * D)),
        "bg": np.ascontiguousarray(np.asarray(L0_attn_b_gate, np.float32).reshape(1, D)),
        "bp": np.ascontiguousarray(np.asarray(L0_attn_b_proj, np.float32).reshape(1, D)),
        "b1c": np.ascontiguousarray(np.asarray(L0_ffn_b1, np.float32).reshape(32, 128).T),
        "b2": np.ascontiguousarray(np.asarray(L0_ffn_b2, np.float32).reshape(1, D)),
        "id16": ident16,
        "id32": ident32,
        "mask": mask,
    }
    in_maps = []
    for core in range(8):
        b, h = core // 2, core % 2
        m = dict(common)
        m["x"] = np.ascontiguousarray(x[b, h * TC:(h + 1) * TC, :])
        m["wprev"] = np.full((128, 1), float(h), np.float32)
        in_maps.append(m)
    res = run_bass_kernel_spmd(nc, in_maps, core_ids=list(range(8)))
    out = np.empty((B, T, D), np.float32)
    for core in range(8):
        b, h = core // 2, core % 2
        out[b, h * TC:(h + 1) * TC, :] = res.results[core]["out"]
    return out


# version 25
# speedup vs baseline: 1.0315x; 1.0172x over previous
import numpy as np
import ml_dtypes
from contextlib import ExitStack

import concourse.bass as bass
import concourse.mybir as mybir
from concourse.bass_core_rust import BassInstruction
from concourse.bass_utils import run_bass_kernel_spmd

F32 = mybir.dt.float32
BF16 = mybir.dt.bfloat16
ACT = mybir.ActivationFunctionType
ALU = mybir.AluOpType
AXX = mybir.AxisListType.X

DEBUG_TAPS = False

B, T, D = 4, 4096, 1024
HEADS, DH, NPAIR = 16, 64, 8
DFF, RANK = 4096, 256
EPS = 1e-5
TC, NCH, CH = 2048, 16, 128
NSC, SCT = 4, 512
NKC = 8
BF = ml_dtypes.bfloat16


def build_program():
    nc = bass.Bass("TRN2", target_bir_lowering=False, detect_race_conditions=False)

    d_x = nc.dram_tensor("x", [TC, D], F32, kind="ExternalInput").ap()
    d_wqkv = nc.dram_tensor("wqkv", [D, 3 * D], F32, kind="ExternalInput").ap()
    d_wg = nc.dram_tensor("wg", [D, D], F32, kind="ExternalInput").ap()
    d_wp = nc.dram_tensor("wp", [D, D], F32, kind="ExternalInput").ap()
    d_u1 = nc.dram_tensor("u1", [D, RANK], F32, kind="ExternalInput").ap()
    d_v1 = nc.dram_tensor("v1", [RANK, DFF], F32, kind="ExternalInput").ap()
    d_u2 = nc.dram_tensor("u2", [DFF, RANK], F32, kind="ExternalInput").ap()
    d_v2 = nc.dram_tensor("v2", [RANK, D], F32, kind="ExternalInput").ap()
    d_lng = nc.dram_tensor("lng", [128, 8], F32, kind="ExternalInput").ap()
    d_lnb = nc.dram_tensor("lnb", [128, 8], F32, kind="ExternalInput").ap()
    d_bqkv = nc.dram_tensor("bqkv", [1, 3 * D], F32, kind="ExternalInput").ap()
    d_bg = nc.dram_tensor("bg", [1, D], F32, kind="ExternalInput").ap()
    d_bp = nc.dram_tensor("bp", [1, D], F32, kind="ExternalInput").ap()
    d_b1 = nc.dram_tensor("b1c", [128, 32], F32, kind="ExternalInput").ap()
    d_b2 = nc.dram_tensor("b2", [1, D], F32, kind="ExternalInput").ap()
    d_id16 = nc.dram_tensor("id16", [128, 128], BF16, kind="ExternalInput").ap()
    d_id32 = nc.dram_tensor("id32", [128, 128], F32, kind="ExternalInput").ap()
    d_mask = nc.dram_tensor("mask", [128, 128], F32, kind="ExternalInput").ap()
    d_wprev = nc.dram_tensor("wprev", [128, 1], F32, kind="ExternalInput").ap()
    d_out_dram = nc.dram_tensor("out", [TC, D], F32, kind="ExternalOutput").ap()

    d_q4spill = nc.dram_tensor("q4spill", [NCH, 128, D], BF16)
    d_ccin = nc.dram_tensor("ccin", [128, 520], F32)
    d_ccout = nc.dram_tensor("ccout", [2, 128, 520], F32)

    taps = {}
    if DEBUG_TAPS:
        for nm, shp, dt in [("t_z", [128, D], BF16), ("t_gate", [128, D], F32),
                            ("t_q4", [128, D], BF16), ("t_k4", [128, D], BF16),
                            ("t_v", [128, HEADS * 65], BF16), ("t_snap", [128, 520], BF16),
                            ("t_o", [128, D], BF16), ("t_x2", [128, D], F32),
                            ("t_ofm", [128, 256], BF16)]:
            taps[nm] = nc.dram_tensor(nm, shp, dt, kind="ExternalOutput").ap()

    last_inst = {}
    nc.push_inst_callback(lambda ins: last_inst.__setitem__(ins.engine, ins))
    cnt = {"DS": 0, "DG": 0, "TT": 0, "VV": 0, "SS": 0, "CC": 0, "ST": 0,
       "DSX0": 0, "DSX1": 0, "DSP": 0, "DSQ0": 0, "DSQ1": 0, "DSXD": 0, "DSO": 0, "GG": 0,
       "DGE": 0, "DG2": 0, "DSO1": 0}

    with ExitStack() as top:
        e = top.enter_context
        sems = {n: e(nc.semaphore(n)) for n in cnt}
        DS, DG, TT, VV, SS, CC, ST = (sems[n] for n in ["DS", "DG", "TT", "VV", "SS", "CC", "ST"])

        def bump(name, amt=1):
            cnt[name] += amt
            return cnt[name]

        def mark(eng, name):
            BassInstruction(last_inst[eng.engine]).then_inc(sems[name], 1)
            return bump(name)

        def wtok(eng, tok):
            eng.wait_ge(sems[tok[0]], tok[1])

        def chain(eng, name="VV"):
            BassInstruction(last_inst[eng.engine]).then_inc(sems[name], 1)
            bump(name)
            eng.wait_ge(sems[name], cnt[name])

        sb_k4fm = e(nc.sbuf_tensor("k4fm", [128, NKC * TC], BF16))
        sb_vtok = e(nc.sbuf_tensor("vtok", [128, NCH * HEADS * 65], BF16))
        sb_snaps = e(nc.sbuf_tensor("snaps", [128, NCH * 520], BF16))
        sb_ident = e(nc.sbuf_tensor("sident", [128, 128], BF16))
        sb_id32 = e(nc.sbuf_tensor("sid32", [128, 128], F32))
        sb_mask = e(nc.sbuf_tensor("smask", [128, 128], F32))
        sb_lng = e(nc.sbuf_tensor("slng", [128, 8], F32))
        sb_lnb = e(nc.sbuf_tensor("lnbv", [128, 8], BF16))
        sb_b1c = e(nc.sbuf_tensor("sb1c", [128, 32], F32))
        sb_eps = e(nc.sbuf_tensor("epsc", [128, 1], F32))
        sb_eps2 = e(nc.sbuf_tensor("eps2c", [128, 1], F32))
        sb_wprev = e(nc.sbuf_tensor("swprev", [128, 1], F32))
        sb_ones = e(nc.sbuf_tensor("ones_row", [1, 128], BF16))
        sb_rp = e(nc.sbuf_tensor("srp", [1, D], BF16))
        sb_rb2 = e(nc.sbuf_tensor("srb2", [1, D], BF16))
        sb_exch = e(nc.sbuf_tensor("sexch", [128, 520], F32))

        block = e(nc.Block())

        @block.sync
        def _(s):
            s.dma_start(out=sb_ident[:], in_=d_id16).then_inc(DS, 16)
            s.dma_start(out=sb_id32[:], in_=d_id32).then_inc(DS, 16)
            s.dma_start(out=sb_mask[:], in_=d_mask).then_inc(DS, 16)
            s.dma_start(out=sb_lng[:], in_=d_lng).then_inc(DS, 16)
            s.dma_start(out=sb_b1c[:], in_=d_b1).then_inc(DS, 16)
            s.dma_start(out=sb_wprev[:], in_=d_wprev).then_inc(DS, 16)
            bump("DS", 6 * 16)
        SETUP_DS = cnt["DS"]

        @block.gpsimd
        def _(g):
            g.dma_start(out=sb_lnb[:], in_=d_lnb).then_inc(DG, 16)
            g.dma_start(out=sb_rp[:], in_=d_bp).then_inc(DG, 16)
            g.dma_start(out=sb_rb2[:], in_=d_b2).then_inc(DG, 16)
            bump("DG", 3 * 16)
            g.memset(sb_ones[:], 1.0)
            g.memset(sb_vtok[:], 1.0).then_inc(sems["GG"], 1)
            bump("GG", 1)
        SETUP_GG = cnt["GG"]

        @block.vector
        def _(v):
            v.memset(sb_eps[:], EPS)
            v.memset(sb_eps2[:], EPS * EPS)
            mark(v, "VV")
        EPS_VV = cnt["VV"]

        with ExitStack() as aa:
            ea = aa.enter_context
            sb_wqkv = ea(nc.sbuf_tensor("wqkv_bf", [128, NKC * 3 * D], BF16))
            sb_wgate = ea(nc.sbuf_tensor("wg_bf", [128, NKC * D], BF16))
            sb_rq = ea(nc.sbuf_tensor("srq", [1, 3 * D], BF16))
            sb_rg = ea(nc.sbuf_tensor("srg", [1, D], BF16))
            sb_r2row = ea(nc.sbuf_tensor("r2row", [1, 256], BF16))
            sb_x = ea(nc.sbuf_tensor("xa", [128, 2 * D], F32))
            sb_ztok = ea(nc.sbuf_tensor("ztok", [128, 2 * D], BF16))
            sb_gate = ea(nc.sbuf_tensor("gate", [128, D], F32))
            sb_gaten = ea(nc.sbuf_tensor("gaten", [128, D], F32))
            sb_tmp1 = ea(nc.sbuf_tensor("tmp1", [128, D], F32))
            sb_tmp2 = ea(nc.sbuf_tensor("tmp2", [128, D], F32))
            sb_q4tok = ea(nc.sbuf_tensor("q4tok", [128, 2 * D], BF16))
            sb_k4tok = ea(nc.sbuf_tensor("k4tok", [128, 2 * D], BF16))
            sb_q4st = ea(nc.sbuf_tensor("q4stage", [128, D], BF16))
            sb_stats = ea(nc.sbuf_tensor("stats", [128, 12], F32))
            sb_sm = ea(nc.sbuf_tensor("smalls", [128, 36], F32))
            psA = ea(nc.psum_tensor("psA", [128, 2048], F32))
            psT = ea(nc.psum_tensor("psT", [128, 2 * 1024], BF16))
            psSA = ea(nc.psum_tensor("psSA", [128, 512], F32))
            psSB = ea(nc.psum_tensor("psSB", [128, 512], F32))

            atok = {0: ("VV", 0), 1: ("VV", 0), 2: ("VV", 0), 3: ("VV", 0),
                    "t0": ("VV", 0), "r2": ("VV", 0)}

            @block.gpsimd
            def _(g):
                for kc in range(NKC):
                    g.dma_start(out=sb_wgate[:, kc * D:(kc + 1) * D],
                                in_=d_wg[kc * 128:(kc + 1) * 128, :]).then_inc(DG, 16)
                g.dma_start(out=sb_rg[:], in_=d_bg).then_inc(DG, 16)
                bump("DG", 9 * 16)
                for kc in range(NKC):
                    g.dma_start(out=sb_wqkv[:, kc * 3 * D:(kc + 1) * 3 * D],
                                in_=d_wqkv[kc * 128:(kc + 1) * 128, :],
                                max_dma_last_dim=8000).then_inc(DG, 16)
                bump("DG", 8 * 16)
            WQKV_DG = cnt["DG"]

            if True:
                bq_rows = [sb_gate, sb_gaten, sb_tmp1]

                @block.sync
                def _(s):
                    for i in range(3):
                        s.dma_start(out=bq_rows[i][0:1, 0:1024],
                                    in_=d_bqkv[:, i * 1024:(i + 1) * 1024]).then_inc(DS, 16)
                    bump("DS", 48)
                BQ_DS = cnt["DS"]

                rq_tt = []
                rq_vv = []
                rq_vv_base = cnt["VV"]

                @block.tensor
                def _(t):
                    t.wait_ge(DG, WQKV_DG)
                    for nw in range(6):
                        if nw >= 3:
                            t.wait_ge(VV, rq_vv_base + (nw - 3) + 1)
                        dst = psA[:1, (nw % 3) * 512:(nw % 3) * 512 + 512]
                        for kc in range(NKC):
                            t.matmul(dst, sb_lnb[:, kc:kc + 1],
                                     sb_wqkv[:, kc * 3 * D + nw * 512: kc * 3 * D + (nw + 1) * 512],
                                     start=(kc == 0), stop=(kc == NKC - 1))
                        rq_tt.append(mark(t, "TT"))

                @block.vector
                def _(v):
                    v.wait_ge(DS, BQ_DS)
                    for nw in range(6):
                        v.wait_ge(TT, rq_tt[nw])
                        v.tensor_add(sb_rq[:, nw * 512:(nw + 1) * 512],
                                     psA[:1, (nw % 3) * 512:(nw % 3) * 512 + 512],
                                     bq_rows[nw // 2][0:1, (nw % 2) * 512:(nw % 2) * 512 + 512])
                        rq_vv.append(mark(v, "VV"))
                        assert rq_vv[-1] == rq_vv_base + len(rq_vv)
                RQ_VV = cnt["VV"]
                for b_ in range(4):
                    atok[b_] = ("VV", RQ_VV)

            @block.scalar
            def _(s):
                s.wait_ge(TT, rq_tt[-1])
                s.wait_ge(DS, BQ_DS)
                for kc in range(NKC):
                    w = sb_wqkv[:, kc * 3 * D:(kc + 1) * 3 * D]
                    s.activation(w, w, ACT.Copy, scale=sb_lng[:, kc:kc + 1])
                mark(s, "SS")
            WSC_SS = cnt["SS"]

            snap_vv = {}
            state_tt = {}
            spill_ds = {}
            z_vv = {}
            stats_vv = {}
            sm_ss = {}
            zt_tt = {}
            gn_vv = {}
            q4k4_vv = [0]
            q4t_tt_prev = [0]
            k4s_tt_prev = [0]

            def emit_s1(c):
                par = c % 2
                smo = (c % 3) * 12
                xb = sb_x[:, par * D:(par + 1) * D]
                zt = sb_ztok[:, par * D:(par + 1) * D]
                dsx = "DSX0" if par == 0 else "DSX1"

                @block.sync
                def _(s):
                    if c >= 2:
                        s.wait_ge(VV, z_vv[c - 2])
                    s.dma_start(out=xb, in_=d_x[c * 128:(c + 1) * 128, :]).then_inc(sems[dsx], 16)
                    bump(dsx, 16)
                X_DS = cnt[dsx]

                @block.vector
                def _(v):
                    v.wait_ge(sems[dsx], X_DS)
                    if c >= 3:
                        v.wait_ge(TT, zt_tt[c - 3])
                    v.bn_stats(sb_stats[:, 0:6], xb[:, 0:512])
                    v.bn_stats(sb_stats[:, 6:12], xb[:, 512:1024])
                    chain(v)
                    v.bn_aggr(sb_sm[:, smo + 0:smo + 2], sb_stats[:, 0:12])
                    mark(v, "VV")
                stats_vv[c] = cnt["VV"]

                @block.scalar
                def _(s):
                    s.wait_ge(VV, stats_vv[c])
                    s.wait_ge(VV, EPS_VV)
                    s.activation(sb_sm[:, smo + 2:smo + 3], sb_sm[:, smo + 1:smo + 2], ACT.Sqrt,
                                 scale=float(1.0 + EPS), bias=sb_eps2[:, 0:1])
                    s.activation(sb_sm[:, smo + 3:smo + 4], sb_sm[:, smo + 1:smo + 2], ACT.Sqrt,
                                 bias=sb_eps[:, 0:1])
                    mark(s, "SS")
                sm_ss[c] = cnt["SS"]

                @block.vector
                def _(v):
                    v.wait_ge(SS, sm_ss[c])
                    v.reciprocal(sb_sm[:, smo + 8:smo + 9], sb_sm[:, smo + 2:smo + 3])
                    v.reciprocal(sb_sm[:, smo + 4:smo + 5], sb_sm[:, smo + 3:smo + 4])
                    chain(v)
                    v.tensor_mul(sb_sm[:, smo + 10:smo + 11], sb_sm[:, smo + 3:smo + 4], sb_sm[:, smo + 8:smo + 9])
                    v.tensor_mul(sb_sm[:, smo + 9:smo + 10], sb_sm[:, smo + 2:smo + 3], sb_sm[:, smo + 4:smo + 5])
                    v.tensor_scalar(sb_sm[:, smo + 11:smo + 12], sb_sm[:, smo + 0:smo + 1],
                                    sb_sm[:, smo + 8:smo + 9], -1.0, ALU.mult, ALU.mult)
                    chain(v)
                    if c >= 2:
                        v.wait_ge(TT, vproj_tt[c - 2])
                    v.tensor_scalar(zt, xb, sb_sm[:, smo + 8:smo + 9],
                                    sb_sm[:, smo + 11:smo + 12], ALU.mult, ALU.add)
                    mark(v, "VV")
                z_vv[c] = cnt["VV"]

            def emit_s1b(c):
                par = c % 2
                smo = (c % 3) * 12
                zt = sb_ztok[:, par * D:(par + 1) * D]
                tp = psT[:, 0:1024]
                r2ps = psT[:1, 1024:1280].bitcast(F32)
                r2sl = sb_r2row[:, par * 128:(par + 1) * 128]

                @block.tensor
                def _(t):
                    t.wait_ge(VV, z_vv[c])
                    wtok(t, atok["t0"])
                    for kc in range(NKC):
                        t.transpose(tp[:, kc * 128:(kc + 1) * 128], zt[:, kc * 128:(kc + 1) * 128], sb_ident[:])
                    wtok(t, atok["r2"])
                    t.transpose(r2ps, sb_sm[:, smo + 10:smo + 11], sb_id32[:])
                    mark(t, "TT")
                zt_tt[c] = cnt["TT"]

                @block.vector
                def _(v):
                    v.wait_ge(TT, zt_tt[c])
                    v.tensor_copy(zt, tp[:, 0:1024])
                    v.tensor_copy(r2sl, r2ps)
                    mark(v, "VV")
                zf_vv[c] = cnt["VV"]
                atok["t0"] = ("VV", cnt["VV"])
                atok["r2"] = ("VV", cnt["VV"])

            def emit_s2a(c):
                par = c % 2
                smo = (c % 3) * 12
                zf = sb_ztok[:, par * D:(par + 1) * D]
                r2sl = sb_r2row[:, par * 128:(par + 1) * 128]
                ZF_VV = zf_vv[c]

                @block.tensor
                def _(t):
                    t.wait_ge(VV, ZF_VV)
                    wtok(t, atok[0]); wtok(t, atok[1])
                    for nw in range(2):
                        dst = psA[:, nw * 512:(nw + 1) * 512]
                        for kc in range(NKC):
                            t.matmul(dst, zf[:, kc * 128:(kc + 1) * 128],
                                     sb_wgate[:, kc * D + nw * 512: kc * D + (nw + 1) * 512],
                                     start=(kc == 0), stop=False)
                        t.matmul(dst, r2sl, sb_rg[:, nw * 512:(nw + 1) * 512], start=False, stop=True)
                    mark(t, "TT")
                GATE_TT = cnt["TT"]

                @block.scalar
                def _(s):
                    s.wait_ge(TT, GATE_TT)
                    if c > 0:
                        s.wait_ge(VV, gn_vv[c - 1])
                    s.activation(sb_gate[:], psA[:, 0:1024], ACT.Sigmoid, scale=sb_sm[:, smo + 9:smo + 10],
                                 accum_out=sb_sm[:, smo + 5:smo + 6])
                    mark(s, "SS")
                SIG_SS = cnt["SS"]
                atok[0] = ("SS", SIG_SS)
                atok[1] = ("SS", SIG_SS)

                @block.vector
                def _(v):
                    v.wait_ge(SS, SIG_SS)
                    v.tensor_scalar(sb_sm[:, smo + 6:smo + 7], sb_sm[:, smo + 5:smo + 6], 1.0 / D, EPS, ALU.mult, ALU.add)
                    chain(v)
                    v.reciprocal(sb_sm[:, smo + 7:smo + 8], sb_sm[:, smo + 6:smo + 7])
                    chain(v)
                    v.tensor_scalar_mul(sb_gaten[:], sb_gate[:], sb_sm[:, smo + 7:smo + 8])
                    mark(v, "VV")
                GN_VV = cnt["VV"]

                def emit_proj(t, nw0, dsts):
                    for j, dst in enumerate(dsts):
                        nw = nw0 + j
                        for kc in range(NKC):
                            t.matmul(dst, zf[:, kc * 128:(kc + 1) * 128],
                                     sb_wqkv[:, kc * 3 * D + nw * 512: kc * 3 * D + (nw + 1) * 512],
                                     start=(kc == 0), stop=False)
                        t.matmul(dst, sb_ones[:, 0:128], sb_rq[:, nw * 512:(nw + 1) * 512],
                                 start=False, stop=True)

                @block.tensor
                def _(t):
                    if c == 0:
                        t.wait_ge(SS, WSC_SS)
                    wtok(t, atok[2]); wtok(t, atok[3])
                    emit_proj(t, 0, [psA[:, 1024:1536], psA[:, 1536:2048]])
                    mark(t, "TT")
                Q_TT = cnt["TT"]

                @block.tensor
                def _(t):
                    wtok(t, atok[0]); wtok(t, atok[1])
                    emit_proj(t, 2, [psA[:, 0:512], psA[:, 512:1024]])
                    mark(t, "TT")
                K_TT = cnt["TT"]
                s2state[c] = (GN_VV, Q_TT, K_TT, emit_proj)

            def emit_s2b(c):
                par = c % 2
                zf = sb_ztok[:, par * D:(par + 1) * D]
                q4s = sb_q4st[:, 0:D]
                tp = psT[:, 0:1024]
                GN_VV, Q_TT, K_TT, emit_proj = s2state[c]

                @block.scalar
                def _(s):
                    s.wait_ge(TT, Q_TT)
                    if c > 0:
                        s.wait_ge(VV, q4k4_vv[0])
                    s.activation(sb_tmp1[:], psA[:, 1024:2048], ACT.Relu)
                    mark(s, "SS")
                QR_SS = cnt["SS"]
                atok[2] = ("SS", QR_SS)
                atok[3] = ("SS", QR_SS)

                @block.vector
                def _(v):
                    v.wait_ge(SS, QR_SS)
                    v.wait_ge(VV, GN_VV)
                    v.tensor_mul(sb_tmp1[:], sb_tmp1[:], sb_gaten[:])
                    mark(v, "VV")
                QG_VV = cnt["VV"]

                @block.scalar
                def _(s):
                    s.wait_ge(TT, K_TT)
                    s.activation(sb_tmp2[:], psA[:, 0:1024], ACT.Relu)
                    mark(s, "SS")
                KR_SS = cnt["SS"]
                atok[0] = ("SS", KR_SS)
                atok[1] = ("SS", KR_SS)

                @block.vector
                def _(v):
                    v.wait_ge(SS, KR_SS)
                    v.tensor_mul(sb_tmp2[:], sb_tmp2[:], sb_gaten[:])
                    mark(v, "VV")
                QKG_VV = cnt["VV"]
                gn_vv[c] = QKG_VV

                @block.scalar
                def _(s):
                    s.wait_ge(VV, QG_VV)
                    s.activation(sb_tmp1[:], sb_tmp1[:], ACT.Square)
                    mark(s, "SS")
                QS_SS = cnt["SS"]

                @block.vector
                def _(v):
                    v.wait_ge(SS, QS_SS)
                    if c >= 2:
                        v.wait_ge(TT, q4t_tt_by[c - 2])
                    v.tensor_scalar(sb_q4tok[:, (c % 2) * D:(c % 2 + 1) * D], sb_tmp1[:], EPS, None, ALU.add)
                    mark(v, "VV")
                Q4_VV = cnt["VV"]
                q4_vv_by[c] = Q4_VV

                @block.scalar
                def _(s):
                    s.wait_ge(VV, QKG_VV)
                    s.activation(sb_tmp2[:], sb_tmp2[:], ACT.Square)
                    mark(s, "SS")
                KS_SS = cnt["SS"]

                @block.vector
                def _(v):
                    v.wait_ge(SS, KS_SS)
                    if c >= 2:
                        v.wait_ge(TT, state_tt_by[c - 2])
                    v.tensor_scalar(sb_k4tok[:, (c % 2) * D:(c % 2 + 1) * D], sb_tmp2[:], EPS, None, ALU.add)
                    mark(v, "VV")
                K4_VV = cnt["VV"]
                k4_vv_by[c] = K4_VV
                q4k4_vv[0] = cnt["VV"]

                @block.tensor
                def _(t):
                    wtok(t, atok[2]); wtok(t, atok[3])
                    emit_proj(t, 4, [psA[:, 1024:1536], psA[:, 1536:2048]])
                    mark(t, "TT")
                V_TT = cnt["TT"]
                vproj_tt[c] = V_TT
                s2tail[c] = True

                @block.scalar
                def _(s):
                    s.wait_ge(TT, V_TT)
                    s.wait_ge(sems["GG"], SETUP_GG)
                    dst = sb_vtok[:, c * HEADS * 65:(c + 1) * HEADS * 65].rearrange("p (h w) -> p h w", w=65)[:, :, 0:64]
                    s.activation(dst[:, 0:8, :], psA[:, 1024:1536].rearrange("p (h w) -> p h w", w=64), ACT.Copy)
                    s.activation(dst[:, 8:16, :], psA[:, 1536:2048].rearrange("p (h w) -> p h w", w=64), ACT.Copy)
                    mark(s, "SS")
                VE_SS = cnt["SS"]
                ve_ss_by[c] = VE_SS
                atok[2] = ("SS", VE_SS)
                atok[3] = ("SS", VE_SS)

            def emit_s2c(c):
                par = c % 2
                q4s = sb_q4st[:, 0:D]
                tp = psT[:, 0:1024]
                q4slot = sb_q4tok[:, par * D:(par + 1) * D]
                k4slot = sb_k4tok[:, par * D:(par + 1) * D]

                @block.tensor
                def _(t):
                    t.wait_ge(VV, q4_vv_by[c])
                    wtok(t, atok["t0"])
                    for kc in range(NKC):
                        t.transpose(tp[:, kc * 128:(kc + 1) * 128], q4slot[:, kc * 128:(kc + 1) * 128], sb_ident[:])
                    q4t_tt_by[c] = mark(t, "TT")
                Q4T_TT = cnt["TT"]

                @block.vector
                def _(v):
                    v.wait_ge(TT, Q4T_TT)
                    if c >= 1:
                        v.wait_ge(sems["DSP"], spill_ds[c - 1])
                    v.tensor_copy(q4s, tp[:, 0:1024])
                    mark(v, "VV")
                Q4F_VV = cnt["VV"]
                atok["t0"] = ("VV", Q4F_VV)

                @block.tensor
                def _(t):
                    t.wait_ge(VV, k4_vv_by[c])
                    wtok(t, atok["t0"])
                    for kc in range(NKC):
                        t.transpose(tp[:, kc * 128:(kc + 1) * 128], k4slot[:, kc * 128:(kc + 1) * 128], sb_ident[:])
                    mark(t, "TT")
                K4T_TT = cnt["TT"]

                @block.vector
                def _(v):
                    v.wait_ge(TT, K4T_TT)
                    dst = sb_k4fm[:].rearrange("p (kc tok) -> p kc tok", tok=TC)[:, :, c * 128:(c + 1) * 128]
                    v.tensor_copy(dst, tp[:, 0:1024].rearrange("p (kc w) -> p kc w", w=128))
                    mark(v, "VV")
                K4F_VV = cnt["VV"]
                atok["t0"] = ("VV", K4F_VV)

                @block.sync
                def _(s):
                    s.wait_ge(VV, Q4F_VV)
                    s.dma_start(out=d_q4spill.ap()[c], in_=q4s).then_inc(sems["DSP"], 16)
                    bump("DSP", 16)
                spill_ds[c] = cnt["DSP"]

                @block.tensor
                def _(t):
                    t.wait_ge(SS, ve_ss_by[c])
                    if c > 0:
                        t.wait_ge(VV, snap_vv[c - 1])
                    for h in range(HEADS):
                        p, half = h // 2, h % 2
                        if p < 7:
                            dst = psSA[half * 64:(half + 1) * 64, p * 65:(p + 1) * 65]
                        else:
                            dst = psSB[half * 64:(half + 1) * 64, 0:65]
                        st = (c == 0) and (h in (0, 1, 14, 15))
                        t.matmul(dst, k4slot[:, h * 64:(h + 1) * 64],
                                 sb_vtok[:, c * HEADS * 65 + h * 65: c * HEADS * 65 + (h + 1) * 65],
                                 start=st, stop=(c == NCH - 1 and h >= 14), skip_group_check=True)
                    state_tt_by[c] = mark(t, "TT")
                state_tt[c] = cnt["TT"]

                @block.vector
                def _(v):
                    v.wait_ge(TT, state_tt[c])
                    if c < NCH - 1:
                        slot = sb_snaps[:, (c + 1) * 520:(c + 2) * 520]
                        v.tensor_copy(slot[:, 0:455], psSA[:, 0:455])
                        v.tensor_copy(slot[:, 455:520], psSB[:, 0:65])
                    else:
                        v.tensor_copy(sb_exch[:, 0:455], psSA[:, 0:455])
                        v.tensor_copy(sb_exch[:, 455:520], psSB[:, 0:65])
                    mark(v, "VV")
                snap_vv[c] = cnt["VV"]

            s2state = {}
            zf_vv = {}
            vproj_tt = {}
            s2tail = {}
            q4t_tt_by = {}
            state_tt_by = {}
            q4_vv_by = {}
            k4_vv_by = {}
            ve_ss_by = {}
            emit_s1(0)
            emit_s1b(0)
            for c in range(NCH):
                if c + 1 < NCH:
                    emit_s1(c + 1)
                emit_s2a(c)
                if c + 1 < NCH:
                    emit_s1b(c + 1)
                emit_s2b(c)
                if c >= 1:
                    emit_s2c(c - 1)
            emit_s2c(NCH - 1)

            A_SNAP = snap_vv[NCH - 1]
            ALL_SPILLS = cnt["DSP"]

        with ExitStack() as dd:
            ed = dd.enter_context
            sb_wp = ed(nc.sbuf_tensor("wp_bf", [128, NKC * D], BF16))
            sb_u1 = ed(nc.sbuf_tensor("u1_bf", [128, NKC * RANK], BF16))
            sb_v1 = ed(nc.sbuf_tensor("v1_bf", [128, 2 * DFF], BF16))
            sb_u2 = ed(nc.sbuf_tensor("u2_bf", [128, 32 * RANK], BF16))
            sb_v2 = ed(nc.sbuf_tensor("v2_bf", [128, 2 * D], BF16))
            sb_q4pf = ed(nc.sbuf_tensor("q4pf", [128, 2 * D], BF16))
            sb_am = ed(nc.sbuf_tensor("am", [128, HEADS * 128], BF16))
            sb_den = ed(nc.sbuf_tensor("den", [128, 16], F32))
            sb_rec = ed(nc.sbuf_tensor("recip", [128, 16], F32))
            sb_osb = ed(nc.sbuf_tensor("osb", [128, D], BF16))
            sb_ofm = ed(nc.sbuf_tensor("ofm", [128, 2 * NKC * 256], BF16))
            sb_x2 = ed(nc.sbuf_tensor("x2", [128, 2 * D], F32))
            sb_z3 = ed(nc.sbuf_tensor("z3", [128, D], BF16))
            sb_z3fm = ed(nc.sbuf_tensor("z3fm", [128, NKC * 256], BF16))
            sb_psb = ed(nc.sbuf_tensor("psb", [128, 512], BF16))
            sb_hsb = ed(nc.sbuf_tensor("hsb", [128, 32 * 256], BF16))
            sb_qsb = ed(nc.sbuf_tensor("qsb", [128, 512], BF16))
            sb_x3 = ed(nc.sbuf_tensor("x3", [128, 2 * D], F32))
            sb_st2 = ed(nc.sbuf_tensor("st2", [128, 12], F32))
            sb_sm2 = ed(nc.sbuf_tensor("sm2", [128, 6], F32))
            psC = ed(nc.psum_tensor("psC", [128, 4096], F32))

            btok = {b_: ("VV", 0) for b_ in range(8)}

            def bw(t, *banks):
                seen = set()
                for b_ in banks:
                    if btok[b_] not in seen:
                        seen.add(btok[b_])
                        wtok(t, btok[b_])

            def bs(tokval, *banks):
                for b_ in banks:
                    btok[b_] = tokval

            @block.scalar
            def _(s):
                s.wait_ge(VV, A_SNAP)
                s.dma_start(out=d_ccin.ap(), in_=sb_exch[:]).then_inc(sems["DGE"], 16)
                bump("DGE", 16)
            CCIN_DGE = cnt["DGE"]

            @block.gpsimd
            def _(g):
                g.wait_ge(sems["DGE"], CCIN_DGE)
                g.collective_compute("AllGather", ALU.bypass,
                                     replica_groups=[[0, 1], [2, 3], [4, 5], [6, 7]],
                                     ins=[d_ccin.ap().opt()],
                                     outs=[d_ccout.ap().opt()]).then_inc(CC, 1)
                bump("CC", 1)
            EXCH_CC = cnt["CC"]

            @block.scalar
            def _(s):
                s.wait_ge(CC, EXCH_CC)
                s.dma_start(out=sb_exch[:], in_=d_ccout.ap()[0]).then_inc(sems["DGE"], 16)
                bump("DGE", 16)
            EXCH_DG = cnt["DGE"]

            snapfix_vv = {}

            @block.vector
            def _(v):
                v.wait_ge(sems["DGE"], EXCH_DG)
                v.tensor_scalar_mul(sb_exch[:], sb_exch[:], sb_wprev[:])
                chain(v)
                v.tensor_copy(sb_snaps[:, 0:520], sb_exch[:])
                snapfix_vv[0] = mark(v, "VV")
                for c in range(1, NCH):
                    slot = sb_snaps[:, c * 520:(c + 1) * 520]
                    v.tensor_add(slot, slot, sb_exch[:])
                    snapfix_vv[c] = mark(v, "VV")
            SNAPFIX_VV = cnt["VV"]

            @block.gpsimd
            def _(g):
                g.wait_ge(VV, A_SNAP)
                g.wait_ge(sems["DGE"], 32)
                for kc in range(NKC):
                    g.dma_start(out=sb_wp[:, kc * D:(kc + 1) * D],
                                in_=d_wp[kc * 128:(kc + 1) * 128, :]).then_inc(DG, 16)
                    g.dma_start(out=sb_u1[:, kc * RANK:(kc + 1) * RANK],
                                in_=d_u1[kc * 128:(kc + 1) * 128, :]).then_inc(DG, 16)
                    bump("DG", 32)
                for rc in range(2):
                    g.dma_start(out=sb_v1[:, rc * DFF:(rc + 1) * DFF],
                                in_=d_v1[rc * 128:(rc + 1) * 128, :],
                                max_dma_last_dim=8000).then_inc(DG, 16)
                    g.dma_start(out=sb_v2[:, rc * D:(rc + 1) * D],
                                in_=d_v2[rc * 128:(rc + 1) * 128, :]).then_inc(DG, 16)
                    bump("DG", 32)
            DW_DG = cnt["DG"]

            @block.gpsimd
            def _(g):
                for fc in range(32):
                    g.dma_start(out=sb_u2[:, fc * RANK:(fc + 1) * RANK],
                                in_=d_u2[fc * 128:(fc + 1) * 128, :]).then_inc(sems["DG2"], 16)
                    bump("DG2", 16)
            DW_DG2 = cnt["DG2"]


            if DEBUG_TAPS:
                @block.sync
                def _(s):
                    s.wait_ge(VV, SNAPFIX_VV)
                    s.dma_start(out=taps["t_snap"], in_=sb_snaps[:, 520:1040]).then_inc(ST, 16)
                    bump("ST", 16)

            q4use_tt = {}
            ofm_ss = {}
            proj_done = [0]
            ot_tt = {}
            am_vv = {}
            div_vv = {}

            def c_a(c):
                par = c % 2
                q4p = sb_q4pf[:, par * D:(par + 1) * D]
                dsq = "DSQ0" if par == 0 else "DSQ1"

                @block.sync
                def _(s):
                    if c >= 2:
                        s.wait_ge(TT, q4use_tt[c - 2])
                    else:
                        s.wait_ge(sems["DSP"], ALL_SPILLS)
                    s.dma_start(out=q4p, in_=d_q4spill.ap()[c]).then_inc(sems[dsq], 16)
                    bump(dsq, 16)
                Q4P_DS = cnt[dsq]

                k4 = sb_k4fm[:].rearrange("p (kc tok) -> p kc tok", tok=TC)
                q4 = q4p.rearrange("p (kc w) -> p kc w", w=128)

                @block.tensor
                def _(t):
                    t.wait_ge(sems[dsq], Q4P_DS)
                    bw(t, 0, 1, 2, 3)
                    for p in range(NPAIR):
                        t.matmul(psC[:, p * 128:(p + 1) * 128],
                                 k4[0:64, p, c * 128:(c + 1) * 128], q4[0:64, p, :],
                                 start=True, stop=True, tile_position=(0, 0))
                        t.matmul(psC[:, 1024 + p * 128:1024 + (p + 1) * 128],
                                 k4[64:128, p, c * 128:(c + 1) * 128], q4[64:128, p, :],
                                 start=True, stop=True, tile_position=(64, 0))
                    mark(t, "TT")
                ATT_TT = cnt["TT"]

                @block.vector
                def _(v):
                    v.wait_ge(TT, ATT_TT)
                    amv = sb_am[:].rearrange("p (h w) -> p h w", w=128)
                    mb = sb_mask[:].unsqueeze(1).to_broadcast([128, 8, 128])
                    v.tensor_tensor(amv[:, 0::2, :], psC[:, 0:1024].rearrange("p (h w) -> p h w", w=128), mb, ALU.mult)
                    v.tensor_tensor(amv[:, 1::2, :], psC[:, 1024:2048].rearrange("p (h w) -> p h w", w=128), mb, ALU.mult)
                    mark(v, "VV")
                am_vv[c] = cnt["VV"]
                bs(("VV", am_vv[c]), 0, 1, 2, 3)

            def c_b(c):
                par = c % 2
                q4p = sb_q4pf[:, par * D:(par + 1) * D]
                q4 = q4p.rearrange("p (kc w) -> p kc w", w=128)

                @block.tensor
                def _(t):
                    t.wait_ge(VV, am_vv[c])
                    for h in range(HEADS):
                        i = h // 2
                        slot = (h % 2) * 1024 + i * 128
                        t.matmul(psC[:, slot:slot + 65], sb_am[:, h * 128:(h + 1) * 128],
                                 sb_vtok[:, c * HEADS * 65 + h * 65: c * HEADS * 65 + (h + 1) * 65],
                                 start=(h in (0, 1, 8, 9)), stop=False, skip_group_check=True)
                    t.wait_ge(VV, snapfix_vv[c])
                    for p in range(NPAIR):
                        off = c * 520 + (p * 65 if p < 7 else 455)
                        t.matmul(psC[:, p * 128:p * 128 + 65],
                                 q4[0:64, p, :], sb_snaps[0:64, off:off + 65],
                                 start=False, stop=True, tile_position=(0, 0), skip_group_check=True)
                        t.matmul(psC[:, 1024 + p * 128:1024 + p * 128 + 65],
                                 q4[64:128, p, :], sb_snaps[64:128, off:off + 65],
                                 start=False, stop=True, tile_position=(64, 0), skip_group_check=True)
                    mark(t, "TT")
                NUMM_TT = cnt["TT"]
                q4use_tt[c] = NUMM_TT

                @block.vector
                def _(v):
                    v.wait_ge(TT, NUMM_TT)
                    if c >= 1:
                        v.wait_ge(TT, ot_tt[c - 1])
                    dd_ = psC[:, 0:2048].rearrange("p (sl w) -> p sl w", w=128)[:, :, 64:65]
                    v.tensor_scalar(sb_den[:].unsqueeze(2), dd_, EPS, None, ALU.add)
                    chain(v)
                    v.reciprocal(sb_rec[:], sb_den[:])
                    chain(v)
                    ov = sb_osb[:].rearrange("p (h w) -> p h w", w=64)
                    ne = psC[:, 0:1024].rearrange("p (sl w) -> p sl w", w=128)[:, :, 0:64]
                    no = psC[:, 1024:2048].rearrange("p (sl w) -> p sl w", w=128)[:, :, 0:64]
                    v.tensor_tensor(ov[:, 0::2, :], ne,
                                    sb_rec[:, 0:8].unsqueeze(2).to_broadcast([128, 8, 64]), ALU.mult)
                    v.tensor_tensor(ov[:, 1::2, :], no,
                                    sb_rec[:, 8:16].unsqueeze(2).to_broadcast([128, 8, 64]), ALU.mult)
                    mark(v, "VV")
                div_vv[c] = cnt["VV"]
                bs(("VV", div_vv[c]), 0, 1, 2, 3)

            def c_c(c):
                @block.tensor
                def _(t):
                    t.wait_ge(VV, div_vv[c])
                    bw(t, 0)
                    dst = psC[:, 0:512].bitcast(BF16)
                    for kc in range(NKC):
                        t.transpose(dst[:, kc * 128:(kc + 1) * 128], sb_osb[:, kc * 128:(kc + 1) * 128], sb_ident[:])
                    mark(t, "TT")
                OT_TT = cnt["TT"]
                ot_tt[c] = OT_TT

            def c_c_v(c):
                OT_TT = ot_tt[c]

                @block.vector
                def _(v):
                    v.wait_ge(TT, OT_TT)
                    v.wait_ge(TT, proj_done[0])
                    src = psC[:, 0:512].bitcast(BF16).rearrange("p (kc w) -> p kc w", w=128)
                    hpar = (c // 2) % 2
                    dst = sb_ofm[:, hpar * NKC * 256:(hpar + 1) * NKC * 256]
                    dst = dst.rearrange("p (kc w) -> p kc w", w=256)[:, :, (c % 2) * 128:(c % 2) * 128 + 128]
                    v.tensor_copy(dst, src)
                    mark(v, "VV")
                bs(("VV", cnt["VV"]), 0)
                ofm_ss[c] = cnt["VV"]

            def d_proj(hs, j):
                c = hs * 2 + j
                hpar = hs % 2
                x2sl = sb_x2[:, j * D:(j + 1) * D]

                @block.sync
                def _(s):
                    if hs > 0:
                        s.wait_ge(VV, x2_free[j])
                    else:
                        s.wait_ge(sems["DSP"], ALL_SPILLS)
                    s.dma_start(out=x2sl, in_=d_x[c * 128:(c + 1) * 128, :]).then_inc(sems["DSXD"], 16)
                    bump("DSXD", 16)
                XD_DS = cnt["DSXD"]

                proj_nw = []

                @block.tensor
                def _(t):
                    t.wait_ge(VV, ofm_ss[hs * 2 + 1])
                    t.wait_ge(DG, DW_DG)
                    ofmv = sb_ofm[:, hpar * NKC * 256:(hpar + 1) * NKC * 256]
                    for nw in range(2):
                        bw(t, 4 + nw)
                        dst = psC[:, 2048 + nw * 512:2048 + (nw + 1) * 512]
                        for kc in range(NKC):
                            t.matmul(dst, ofmv[:, kc * 256 + j * 128: kc * 256 + (j + 1) * 128],
                                     sb_wp[:, kc * D + nw * 512:kc * D + (nw + 1) * 512],
                                     start=(kc == 0), stop=False)
                        t.matmul(dst, sb_ones[:, 0:128], sb_rp[:, nw * 512:(nw + 1) * 512],
                                 start=False, stop=True)
                        proj_nw.append(mark(t, "TT"))
                PROJ_TT = proj_nw[1]
                proj_done[0] = PROJ_TT

                add_m = []

                @block.vector
                def _(v):
                    v.wait_ge(TT, proj_nw[0])
                    v.wait_ge(sems["DSXD"], XD_DS)
                    v.tensor_add(x2sl[:, 0:512], psC[:, 2048:2560], x2sl[:, 0:512])
                    chain(v)
                    add_m.append(cnt["VV"])
                    v.bn_stats(sb_st2[:, 0:6], x2sl[:, 0:512])
                    v.wait_ge(TT, PROJ_TT)
                    v.tensor_add(x2sl[:, 512:1024], psC[:, 2560:3072], x2sl[:, 512:1024])
                    chain(v)
                    add_m.append(cnt["VV"])
                    v.bn_stats(sb_st2[:, 6:12], x2sl[:, 512:1024])
                    chain(v)
                    v.bn_aggr(sb_sm2[:, 0:2], sb_st2[:])
                    chain(v)
                    v.tensor_scalar(sb_sm2[:, 2:3], sb_sm2[:, 1:2], EPS, None, ALU.add)
                    chain(v)
                    v.reciprocal(sb_sm2[:, 3:4], sb_sm2[:, 2:3])
                    mark(v, "VV")
                LN3_VV = cnt["VV"]
                bs(("VV", add_m[0]), 4)
                bs(("VV", add_m[1]), 5)

                @block.scalar
                def _(s):
                    s.wait_ge(VV, LN3_VV)
                    s.activation(sb_sm2[:, 4:5], sb_sm2[:, 3:4], ACT.Sqrt)
                    mark(s, "SS")
                d_state[(hs, j)] = (cnt["SS"], x2sl)

            def d_z3(hs, j):
                R3_SS, x2sl = d_state[(hs, j)]

                @block.vector
                def _(v):
                    v.wait_ge(SS, R3_SS)
                    v.tensor_scalar(sb_sm2[:, 5:6], sb_sm2[:, 0:1], sb_sm2[:, 4:5], -1.0, ALU.mult, ALU.mult)
                    chain(v)
                    if hs + j > 0:
                        v.wait_ge(TT, z3t_used[0])
                    v.tensor_scalar(sb_z3[:], x2sl, sb_sm2[:, 4:5], sb_sm2[:, 5:6], ALU.mult, ALU.add)
                    mark(v, "VV")
                Z3_VV = cnt["VV"]

                @block.tensor
                def _(t):
                    t.wait_ge(VV, Z3_VV)
                    bw(t, 6)
                    dst = psC[:, 3072:3584].bitcast(BF16)
                    for kc in range(NKC):
                        t.transpose(dst[:, kc * 128:(kc + 1) * 128], sb_z3[:, kc * 128:(kc + 1) * 128], sb_ident[:])
                    z3t_used[0] = mark(t, "TT")
                Z3T_TT = cnt["TT"]

                @block.vector
                def _(v):
                    v.wait_ge(TT, Z3T_TT)
                    src = psC[:, 3072:3584].bitcast(BF16).rearrange("p (kc w) -> p kc w", w=128)
                    dst = sb_z3fm[:].rearrange("p (kc w) -> p kc w", w=256)[:, :, j * 128:(j + 1) * 128]
                    v.tensor_copy(dst, src)
                    mark(v, "VV")
                bs(("VV", cnt["VV"]), 6)
                z3f_vv[j] = cnt["VV"]

            def d_ffn(hs):
                TW = 256

                @block.tensor
                def _(t):
                    t.wait_ge(VV, z3f_vv[1])
                    bw(t, 7)
                    for mc in range(2):
                        dst = psC[:, 3584 + mc * TW:3584 + (mc + 1) * TW]
                        for kc in range(NKC):
                            t.matmul(dst, sb_u1[:, kc * RANK + mc * 128:kc * RANK + (mc + 1) * 128],
                                     sb_z3fm[:, kc * TW:(kc + 1) * TW],
                                     start=(mc == 0 and kc == 0), stop=(kc == NKC - 1), skip_group_check=True)
                    mark(t, "TT")
                U1_TT = cnt["TT"]

                @block.vector
                def _(v):
                    v.wait_ge(TT, U1_TT)
                    v.tensor_copy(sb_psb[:], psC[:, 3584:4096])
                    mark(v, "VV")
                P_VV = cnt["VV"]
                bs(("VV", P_VV), 7)

                ss_base = cnt["SS"]
                v1_tt = []

                @block.tensor
                def _(t):
                    t.wait_ge(VV, P_VV)
                    t.wait_ge(sems["DG2"], DW_DG2)
                    bw(t, 4, 5, 6)
                    for pr in range(16):
                        if pr >= 2:
                            t.wait_ge(SS, ss_base + pr - 1)
                        for half in range(2):
                            fc = pr * 2 + half
                            dst = psC[:, 2048 + (pr % 2) * 512 + half * TW:
                                      2048 + (pr % 2) * 512 + (half + 1) * TW]
                            for rc in range(2):
                                t.matmul(dst, sb_v1[:, rc * DFF + fc * 128:rc * DFF + (fc + 1) * 128],
                                         sb_psb[:, rc * TW:(rc + 1) * TW],
                                         start=(rc == 0), stop=(rc == 1))
                        v1_tt.append(mark(t, "TT"))
                        if pr >= 2:
                            for half in range(2):
                                fc = (pr - 2) * 2 + half
                                for mc in range(2):
                                    t.matmul(psC[:, 3072 + mc * TW:3072 + (mc + 1) * TW],
                                             sb_u2[:, fc * RANK + mc * 128:fc * RANK + (mc + 1) * 128],
                                             sb_hsb[:, fc * TW:(fc + 1) * TW],
                                             start=(fc == 0 and mc == 0), stop=False, skip_group_check=True)
                    for prl in (14, 15):
                        t.wait_ge(SS, ss_base + prl + 1)
                        for half in range(2):
                            fc = prl * 2 + half
                            for mc in range(2):
                                t.matmul(psC[:, 3072 + mc * TW:3072 + (mc + 1) * TW],
                                         sb_u2[:, fc * RANK + mc * 128:fc * RANK + (mc + 1) * 128],
                                         sb_hsb[:, fc * TW:(fc + 1) * TW],
                                         start=False, stop=(fc == 31 and mc == 1), skip_group_check=True)
                    mark(t, "TT")
                U2_TT = cnt["TT"]

                @block.scalar
                def _(s):
                    for pr in range(16):
                        s.wait_ge(TT, v1_tt[pr])
                        src = psC[:, 2048 + (pr % 2) * 512:2048 + (pr % 2) * 512 + 512]
                        s.activation(sb_hsb[:, pr * 2 * TW:pr * 2 * TW + TW], src[:, 0:TW],
                                     ACT.Gelu_apprx_tanh, bias=sb_b1c[:, 2 * pr:2 * pr + 1])
                        s.activation(sb_hsb[:, pr * 2 * TW + TW:(pr + 1) * 2 * TW], src[:, TW:2 * TW],
                                     ACT.Gelu_apprx_tanh, bias=sb_b1c[:, 2 * pr + 1:2 * pr + 2])
                        assert mark(s, "SS") == ss_base + pr + 1
                bs(("SS", cnt["SS"]), 4, 5)
                return U2_TT

            def d_out(hs, U2_TT):
                TW = 256

                @block.vector
                def _(v):
                    v.wait_ge(TT, U2_TT)
                    v.tensor_copy(sb_qsb[:], psC[:, 3072:3584])
                    mark(v, "VV")
                QQ_VV = cnt["VV"]
                bs(("VV", QQ_VV), 6)

                for j in range(2):
                    c = hs * 2 + j
                    xpar = c % 2
                    x3b = sb_x3[:, xpar * D:(xpar + 1) * D]
                    dso = "DSO" if xpar == 0 else "DSO1"
                    _, x2sl = d_state[(hs, j)]

                    v2_nw = []

                    @block.tensor
                    def _(t):
                        t.wait_ge(VV, QQ_VV)
                        for nw in range(2):
                            bw(t, 4 + nw)
                            dst = psC[:, 2048 + nw * 512:2048 + (nw + 1) * 512]
                            for rc in range(2):
                                t.matmul(dst, sb_qsb[:, rc * TW + j * 128:rc * TW + (j + 1) * 128],
                                         sb_v2[:, rc * D + nw * 512:rc * D + (nw + 1) * 512],
                                         start=(rc == 0), stop=False)
                            t.matmul(dst, sb_ones[:, 0:128], sb_rb2[:, nw * 512:(nw + 1) * 512],
                                     start=False, stop=True)
                            v2_nw.append(mark(t, "TT"))

                    x3_m = []

                    @block.vector
                    def _(v):
                        v.wait_ge(TT, v2_nw[0])
                        if c >= 2:
                            v.wait_ge(sems[dso], x3_ds[xpar])
                        v.tensor_add(x3b[:, 0:512], psC[:, 2048:2560], x2sl[:, 0:512])
                        mark(v, "VV")
                        x3_m.append(cnt["VV"])
                        v.wait_ge(TT, v2_nw[1])
                        v.tensor_add(x3b[:, 512:1024], psC[:, 2560:3072], x2sl[:, 512:1024])
                        mark(v, "VV")
                        x3_m.append(cnt["VV"])
                    X3_VV = x3_m[1]
                    bs(("VV", x3_m[0]), 4)
                    bs(("VV", x3_m[1]), 5)
                    x2_free[j] = X3_VV

                    @block.scalar
                    def _(s):
                        s.wait_ge(VV, X3_VV)
                        s.dma_start(out=d_out_ap[c * 128:(c + 1) * 128, :], in_=x3b).then_inc(sems[dso], 16)
                        x3_ds[xpar] = bump(dso, 16)

            d_state = {}
            x2_free = {0: 0, 1: 0}
            z3t_used = [0]
            z3f_vv = {}
            x3_ds = {0: 0, 1: 0}
            d_out_ap = d_out_dram
            for c0 in (0, 1):
                c_a(c0); c_b(c0); c_c(c0); c_c_v(c0)
            for hs in range(8):
                if hs < 7:
                    c_a(2 * hs + 2)
                d_proj(hs, 0)
                if hs < 7:
                    c_b(2 * hs + 2)
                    c_c(2 * hs + 2)
                d_z3(hs, 0)
                if hs < 7:
                    c_c_v(2 * hs + 2)
                    c_a(2 * hs + 3)
                d_proj(hs, 1)
                if hs < 7:
                    c_b(2 * hs + 3)
                    c_c(2 * hs + 3)
                d_z3(hs, 1)
                if hs < 7:
                    c_c_v(2 * hs + 3)
                u2tt = d_ffn(hs)
                d_out(hs, u2tt)

            @block.sync
            def _(s):
                for nm in ["DS", "DSX0", "DSX1", "DSP", "DSQ0", "DSQ1", "DSXD", "DSO", "DSO1", "DG", "DGE", "DG2", "ST"]:
                    if cnt[nm]:
                        s.wait_ge(sems[nm], cnt[nm])

    return nc


def kernel(L0_attn_x, L0_attn_ln_g, L0_attn_ln_b, L0_attn_W_qkv, L0_attn_b_qkv,
           L0_attn_W_gate, L0_attn_b_gate, L0_attn_W_proj, L0_attn_b_proj,
           L0_ffn_U1, L0_ffn_V1, L0_ffn_b1, L0_ffn_U2, L0_ffn_V2, L0_ffn_b2):
    x = np.ascontiguousarray(np.asarray(L0_attn_x, dtype=np.float32))
    nc = build_program()
    ident16 = np.eye(128, dtype=BF)
    ident32 = np.eye(128, dtype=np.float32)
    mask = np.ascontiguousarray(np.triu(np.ones((128, 128), np.float32)))
    common = {
        "wqkv": np.ascontiguousarray(np.asarray(L0_attn_W_qkv, np.float32)),
        "wg": np.ascontiguousarray(np.asarray(L0_attn_W_gate, np.float32)),
        "wp": np.ascontiguousarray(np.asarray(L0_attn_W_proj, np.float32)),
        "u1": np.ascontiguousarray(np.asarray(L0_ffn_U1, np.float32)),
        "v1": np.ascontiguousarray(np.asarray(L0_ffn_V1, np.float32)),
        "u2": np.ascontiguousarray(np.asarray(L0_ffn_U2, np.float32)),
        "v2": np.ascontiguousarray(np.asarray(L0_ffn_V2, np.float32)),
        "lng": np.ascontiguousarray(np.asarray(L0_attn_ln_g, np.float32).reshape(8, 128).T),
        "lnb": np.ascontiguousarray(np.asarray(L0_attn_ln_b, np.float32).reshape(8, 128).T),
        "bqkv": np.ascontiguousarray(np.asarray(L0_attn_b_qkv, np.float32).reshape(1, 3 * D)),
        "bg": np.ascontiguousarray(np.asarray(L0_attn_b_gate, np.float32).reshape(1, D)),
        "bp": np.ascontiguousarray(np.asarray(L0_attn_b_proj, np.float32).reshape(1, D)),
        "b1c": np.ascontiguousarray(np.asarray(L0_ffn_b1, np.float32).reshape(32, 128).T),
        "b2": np.ascontiguousarray(np.asarray(L0_ffn_b2, np.float32).reshape(1, D)),
        "id16": ident16,
        "id32": ident32,
        "mask": mask,
    }
    in_maps = []
    for core in range(8):
        b, h = core // 2, core % 2
        m = dict(common)
        m["x"] = np.ascontiguousarray(x[b, h * TC:(h + 1) * TC, :])
        m["wprev"] = np.full((128, 1), float(h), np.float32)
        in_maps.append(m)
    res = run_bass_kernel_spmd(nc, in_maps, core_ids=list(range(8)))
    out = np.empty((B, T, D), np.float32)
    for core in range(8):
        b, h = core // 2, core % 2
        out[b, h * TC:(h + 1) * TC, :] = res.results[core]["out"]
    return out


# version 27
# speedup vs baseline: 1.0382x; 1.0065x over previous
import numpy as np
import ml_dtypes
from contextlib import ExitStack

import concourse.bass as bass
import concourse.mybir as mybir
from concourse.bass_core_rust import BassInstruction
from concourse.bass_utils import run_bass_kernel_spmd

F32 = mybir.dt.float32
BF16 = mybir.dt.bfloat16
ACT = mybir.ActivationFunctionType
ALU = mybir.AluOpType
AXX = mybir.AxisListType.X

DEBUG_TAPS = False

B, T, D = 4, 4096, 1024
HEADS, DH, NPAIR = 16, 64, 8
DFF, RANK = 4096, 256
EPS = 1e-5
TC, NCH, CH = 2048, 16, 128
NSC, SCT = 4, 512
NKC = 8
BF = ml_dtypes.bfloat16


def build_program():
    nc = bass.Bass("TRN2", target_bir_lowering=False, detect_race_conditions=False)

    d_x = nc.dram_tensor("x", [TC, D], F32, kind="ExternalInput").ap()
    d_wqkv = nc.dram_tensor("wqkv", [D, 3 * D], F32, kind="ExternalInput").ap()
    d_wg = nc.dram_tensor("wg", [D, D], F32, kind="ExternalInput").ap()
    d_wp = nc.dram_tensor("wp", [D, D], F32, kind="ExternalInput").ap()
    d_u1 = nc.dram_tensor("u1", [D, RANK], F32, kind="ExternalInput").ap()
    d_v1 = nc.dram_tensor("v1", [RANK, DFF], F32, kind="ExternalInput").ap()
    d_u2 = nc.dram_tensor("u2", [DFF, RANK], F32, kind="ExternalInput").ap()
    d_v2 = nc.dram_tensor("v2", [RANK, D], F32, kind="ExternalInput").ap()
    d_lng = nc.dram_tensor("lng", [128, 8], F32, kind="ExternalInput").ap()
    d_lnb = nc.dram_tensor("lnb", [128, 8], F32, kind="ExternalInput").ap()
    d_bqkv = nc.dram_tensor("bqkv", [1, 3 * D], F32, kind="ExternalInput").ap()
    d_bg = nc.dram_tensor("bg", [1, D], F32, kind="ExternalInput").ap()
    d_bp = nc.dram_tensor("bp", [1, D], F32, kind="ExternalInput").ap()
    d_b1 = nc.dram_tensor("b1c", [128, 32], F32, kind="ExternalInput").ap()
    d_b2 = nc.dram_tensor("b2", [1, D], F32, kind="ExternalInput").ap()
    d_id16 = nc.dram_tensor("id16", [128, 128], BF16, kind="ExternalInput").ap()
    d_id32 = nc.dram_tensor("id32", [128, 128], F32, kind="ExternalInput").ap()
    d_mask = nc.dram_tensor("mask", [128, 128], F32, kind="ExternalInput").ap()
    d_wprev = nc.dram_tensor("wprev", [128, 1], F32, kind="ExternalInput").ap()
    d_out_dram = nc.dram_tensor("out", [TC, D], F32, kind="ExternalOutput").ap()

    d_q4spill = nc.dram_tensor("q4spill", [NCH, 128, D], BF16)
    d_ccin = nc.dram_tensor("ccin", [128, 520], F32)
    d_ccout = nc.dram_tensor("ccout", [2, 128, 520], F32)

    taps = {}
    if DEBUG_TAPS:
        for nm, shp, dt in [("t_z", [128, D], BF16), ("t_gate", [128, D], F32),
                            ("t_q4", [128, D], BF16), ("t_k4", [128, D], BF16),
                            ("t_v", [128, HEADS * 65], BF16), ("t_snap", [128, 520], BF16),
                            ("t_o", [128, D], BF16), ("t_x2", [128, D], F32),
                            ("t_ofm", [128, 256], BF16)]:
            taps[nm] = nc.dram_tensor(nm, shp, dt, kind="ExternalOutput").ap()

    last_inst = {}
    nc.push_inst_callback(lambda ins: last_inst.__setitem__(ins.engine, ins))
    cnt = {"DS": 0, "DG": 0, "TT": 0, "VV": 0, "SS": 0, "CC": 0, "ST": 0,
       "DSX0": 0, "DSX1": 0, "DSP": 0, "DSQ0": 0, "DSQ1": 0, "DSXD": 0, "DSO": 0, "GG": 0,
       "DGE": 0, "DG2": 0, "DSO1": 0}

    with ExitStack() as top:
        e = top.enter_context
        sems = {n: e(nc.semaphore(n)) for n in cnt}
        DS, DG, TT, VV, SS, CC, ST = (sems[n] for n in ["DS", "DG", "TT", "VV", "SS", "CC", "ST"])

        def bump(name, amt=1):
            cnt[name] += amt
            return cnt[name]

        def mark(eng, name):
            BassInstruction(last_inst[eng.engine]).then_inc(sems[name], 1)
            return bump(name)

        def wtok(eng, tok):
            eng.wait_ge(sems[tok[0]], tok[1])

        def chain(eng, name="VV"):
            BassInstruction(last_inst[eng.engine]).then_inc(sems[name], 1)
            bump(name)
            eng.wait_ge(sems[name], cnt[name])

        sb_k4fm = e(nc.sbuf_tensor("k4fm", [128, NKC * TC], BF16))
        sb_vtok = e(nc.sbuf_tensor("vtok", [128, NCH * HEADS * 65], BF16))
        sb_snaps = e(nc.sbuf_tensor("snaps", [128, NCH * 520], BF16))
        sb_ident = e(nc.sbuf_tensor("sident", [128, 128], BF16))
        sb_id32 = e(nc.sbuf_tensor("sid32", [128, 128], F32))
        sb_mask = e(nc.sbuf_tensor("smask", [128, 128], F32))
        sb_lng = e(nc.sbuf_tensor("slng", [128, 8], F32))
        sb_lnb = e(nc.sbuf_tensor("lnbv", [128, 8], BF16))
        sb_b1c = e(nc.sbuf_tensor("sb1c", [128, 32], F32))
        sb_eps = e(nc.sbuf_tensor("epsc", [128, 1], F32))
        sb_eps2 = e(nc.sbuf_tensor("eps2c", [128, 1], F32))
        sb_wprev = e(nc.sbuf_tensor("swprev", [128, 1], F32))
        sb_ones = e(nc.sbuf_tensor("ones_row", [1, 128], BF16))
        sb_rp = e(nc.sbuf_tensor("srp", [1, D], BF16))
        sb_rb2 = e(nc.sbuf_tensor("srb2", [1, D], BF16))
        sb_exch = e(nc.sbuf_tensor("sexch", [128, 520], F32))

        block = e(nc.Block())

        @block.sync
        def _(s):
            s.dma_start(out=sb_ident[:], in_=d_id16).then_inc(DS, 16)
            s.dma_start(out=sb_id32[:], in_=d_id32).then_inc(DS, 16)
            s.dma_start(out=sb_mask[:], in_=d_mask).then_inc(DS, 16)
            s.dma_start(out=sb_lng[:], in_=d_lng).then_inc(DS, 16)
            s.dma_start(out=sb_b1c[:], in_=d_b1).then_inc(DS, 16)
            s.dma_start(out=sb_wprev[:], in_=d_wprev).then_inc(DS, 16)
            bump("DS", 6 * 16)
        SETUP_DS = cnt["DS"]

        @block.gpsimd
        def _(g):
            g.dma_start(out=sb_lnb[:], in_=d_lnb).then_inc(DG, 16)
            g.dma_start(out=sb_rp[:], in_=d_bp).then_inc(DG, 16)
            g.dma_start(out=sb_rb2[:], in_=d_b2).then_inc(DG, 16)
            bump("DG", 3 * 16)
            g.memset(sb_ones[:], 1.0)
            g.memset(sb_vtok[:], 1.0).then_inc(sems["GG"], 1)
            bump("GG", 1)
        SETUP_GG = cnt["GG"]

        @block.vector
        def _(v):
            v.memset(sb_eps[:], EPS)
            v.memset(sb_eps2[:], EPS * EPS)
            mark(v, "VV")
        EPS_VV = cnt["VV"]

        with ExitStack() as aa:
            ea = aa.enter_context
            sb_wqkv = ea(nc.sbuf_tensor("wqkv_bf", [128, NKC * 3 * D], BF16))
            sb_wgate = ea(nc.sbuf_tensor("wg_bf", [128, NKC * D], BF16))
            sb_rq = ea(nc.sbuf_tensor("srq", [1, 3 * D], BF16))
            sb_rg = ea(nc.sbuf_tensor("srg", [1, D], BF16))
            sb_r2row = ea(nc.sbuf_tensor("r2row", [1, 256], BF16))
            sb_x = ea(nc.sbuf_tensor("xa", [128, 2 * D], F32))
            sb_ztok = ea(nc.sbuf_tensor("ztok", [128, 2 * D], BF16))
            sb_gate = ea(nc.sbuf_tensor("gate", [128, D], F32))
            sb_gaten = ea(nc.sbuf_tensor("gaten", [128, D], F32))
            sb_tmp1 = ea(nc.sbuf_tensor("tmp1", [128, D], F32))
            sb_tmp2 = ea(nc.sbuf_tensor("tmp2", [128, D], F32))
            sb_q4tok = ea(nc.sbuf_tensor("q4tok", [128, 2 * D], BF16))
            sb_k4tok = ea(nc.sbuf_tensor("k4tok", [128, 2 * D], BF16))
            sb_q4st = ea(nc.sbuf_tensor("q4stage", [128, D], BF16))
            sb_stats = ea(nc.sbuf_tensor("stats", [128, 12], F32))
            sb_sm = ea(nc.sbuf_tensor("smalls", [128, 36], F32))
            psA = ea(nc.psum_tensor("psA", [128, 2048], F32))
            psT = ea(nc.psum_tensor("psT", [128, 2 * 1024], BF16))
            psSA = ea(nc.psum_tensor("psSA", [128, 512], F32))
            psSB = ea(nc.psum_tensor("psSB", [128, 512], F32))

            atok = {0: ("VV", 0), 1: ("VV", 0), 2: ("VV", 0), 3: ("VV", 0),
                    "t0": ("VV", 0), "r2": ("VV", 0)}

            @block.gpsimd
            def _(g):
                for kc in range(NKC):
                    g.dma_start(out=sb_wgate[:, kc * D:(kc + 1) * D],
                                in_=d_wg[kc * 128:(kc + 1) * 128, :]).then_inc(DG, 16)
                g.dma_start(out=sb_rg[:], in_=d_bg).then_inc(DG, 16)
                bump("DG", 9 * 16)
                for kc in range(NKC):
                    g.dma_start(out=sb_wqkv[:, kc * 3 * D:(kc + 1) * 3 * D],
                                in_=d_wqkv[kc * 128:(kc + 1) * 128, :],
                                max_dma_last_dim=8000).then_inc(DG, 16)
                bump("DG", 8 * 16)
            WQKV_DG = cnt["DG"]

            if True:
                bq_rows = [sb_gate, sb_gaten, sb_tmp1]

                @block.sync
                def _(s):
                    for i in range(3):
                        s.dma_start(out=bq_rows[i][0:1, 0:1024],
                                    in_=d_bqkv[:, i * 1024:(i + 1) * 1024]).then_inc(DS, 16)
                    bump("DS", 48)
                BQ_DS = cnt["DS"]

                rq_tt = []
                rq_vv = []
                rq_vv_base = cnt["VV"]

                @block.tensor
                def _(t):
                    t.wait_ge(DG, WQKV_DG)
                    for nw in range(6):
                        if nw >= 3:
                            t.wait_ge(VV, rq_vv_base + (nw - 3) + 1)
                        dst = psA[:1, (nw % 3) * 512:(nw % 3) * 512 + 512]
                        for kc in range(NKC):
                            t.matmul(dst, sb_lnb[:, kc:kc + 1],
                                     sb_wqkv[:, kc * 3 * D + nw * 512: kc * 3 * D + (nw + 1) * 512],
                                     start=(kc == 0), stop=(kc == NKC - 1))
                        rq_tt.append(mark(t, "TT"))

                @block.vector
                def _(v):
                    v.wait_ge(DS, BQ_DS)
                    for nw in range(6):
                        v.wait_ge(TT, rq_tt[nw])
                        v.tensor_add(sb_rq[:, nw * 512:(nw + 1) * 512],
                                     psA[:1, (nw % 3) * 512:(nw % 3) * 512 + 512],
                                     bq_rows[nw // 2][0:1, (nw % 2) * 512:(nw % 2) * 512 + 512])
                        rq_vv.append(mark(v, "VV"))
                        assert rq_vv[-1] == rq_vv_base + len(rq_vv)
                RQ_VV = cnt["VV"]
                for b_ in range(4):
                    atok[b_] = ("VV", RQ_VV)

            @block.scalar
            def _(s):
                s.wait_ge(TT, rq_tt[-1])
                s.wait_ge(DS, BQ_DS)
                for kc in range(NKC):
                    w = sb_wqkv[:, kc * 3 * D:(kc + 1) * 3 * D]
                    s.activation(w, w, ACT.Copy, scale=sb_lng[:, kc:kc + 1])
                mark(s, "SS")
            WSC_SS = cnt["SS"]

            snap_vv = {}
            state_tt = {}
            spill_ds = {}
            z_vv = {}
            stats_vv = {}
            sm_ss = {}
            zt_tt = {}
            gn_vv = {}
            q4k4_vv = [0]
            q4t_tt_prev = [0]
            k4s_tt_prev = [0]

            def emit_s1(c):
                par = c % 2
                smo = (c % 3) * 12
                xb = sb_x[:, par * D:(par + 1) * D]
                zt = sb_ztok[:, par * D:(par + 1) * D]
                dsx = "DSX0" if par == 0 else "DSX1"

                @block.sync
                def _(s):
                    if c >= 2:
                        s.wait_ge(VV, z_vv[c - 2])
                    s.dma_start(out=xb, in_=d_x[c * 128:(c + 1) * 128, :]).then_inc(sems[dsx], 16)
                    bump(dsx, 16)
                X_DS = cnt[dsx]

                @block.vector
                def _(v):
                    v.wait_ge(sems[dsx], X_DS)
                    if c >= 3:
                        v.wait_ge(TT, zt_tt[c - 3])
                    v.bn_stats(sb_stats[:, 0:6], xb[:, 0:512])
                    v.bn_stats(sb_stats[:, 6:12], xb[:, 512:1024])
                    chain(v)
                    v.bn_aggr(sb_sm[:, smo + 0:smo + 2], sb_stats[:, 0:12])
                    mark(v, "VV")
                stats_vv[c] = cnt["VV"]

                @block.scalar
                def _(s):
                    s.wait_ge(VV, stats_vv[c])
                    s.wait_ge(VV, EPS_VV)
                    s.activation(sb_sm[:, smo + 2:smo + 3], sb_sm[:, smo + 1:smo + 2], ACT.Sqrt,
                                 scale=float(1.0 + EPS), bias=sb_eps2[:, 0:1])
                    s.activation(sb_sm[:, smo + 3:smo + 4], sb_sm[:, smo + 1:smo + 2], ACT.Sqrt,
                                 bias=sb_eps[:, 0:1])
                    mark(s, "SS")
                sm_ss[c] = cnt["SS"]

                @block.vector
                def _(v):
                    v.wait_ge(SS, sm_ss[c])
                    v.reciprocal(sb_sm[:, smo + 8:smo + 9], sb_sm[:, smo + 2:smo + 3])
                    v.reciprocal(sb_sm[:, smo + 4:smo + 5], sb_sm[:, smo + 3:smo + 4])
                    chain(v)
                    v.tensor_mul(sb_sm[:, smo + 10:smo + 11], sb_sm[:, smo + 3:smo + 4], sb_sm[:, smo + 8:smo + 9])
                    v.tensor_mul(sb_sm[:, smo + 9:smo + 10], sb_sm[:, smo + 2:smo + 3], sb_sm[:, smo + 4:smo + 5])
                    v.tensor_scalar(sb_sm[:, smo + 11:smo + 12], sb_sm[:, smo + 0:smo + 1],
                                    sb_sm[:, smo + 8:smo + 9], -1.0, ALU.mult, ALU.mult)
                    chain(v)
                    if c >= 2:
                        v.wait_ge(TT, vproj_tt[c - 2])
                    v.tensor_scalar(zt, xb, sb_sm[:, smo + 8:smo + 9],
                                    sb_sm[:, smo + 11:smo + 12], ALU.mult, ALU.add)
                    mark(v, "VV")
                z_vv[c] = cnt["VV"]

            def emit_s1b(c):
                par = c % 2
                smo = (c % 3) * 12
                zt = sb_ztok[:, par * D:(par + 1) * D]
                tp = psT[:, 0:1024]
                r2ps = psT[:1, 1024:1280].bitcast(F32)
                r2sl = sb_r2row[:, par * 128:(par + 1) * 128]

                @block.tensor
                def _(t):
                    t.wait_ge(VV, z_vv[c])
                    wtok(t, atok["t0"])
                    for kc in range(NKC):
                        t.transpose(tp[:, kc * 128:(kc + 1) * 128], zt[:, kc * 128:(kc + 1) * 128], sb_ident[:])
                    wtok(t, atok["r2"])
                    t.transpose(r2ps, sb_sm[:, smo + 10:smo + 11], sb_id32[:])
                    mark(t, "TT")
                zt_tt[c] = cnt["TT"]

                @block.vector
                def _(v):
                    v.wait_ge(TT, zt_tt[c])
                    v.tensor_copy(zt, tp[:, 0:1024])
                    v.tensor_copy(r2sl, r2ps)
                    mark(v, "VV")
                zf_vv[c] = cnt["VV"]
                atok["t0"] = ("VV", cnt["VV"])
                atok["r2"] = ("VV", cnt["VV"])

            def emit_s2a(c):
                par = c % 2
                smo = (c % 3) * 12
                zf = sb_ztok[:, par * D:(par + 1) * D]
                r2sl = sb_r2row[:, par * 128:(par + 1) * 128]
                ZF_VV = zf_vv[c]

                @block.tensor
                def _(t):
                    t.wait_ge(VV, ZF_VV)
                    wtok(t, atok[0]); wtok(t, atok[1])
                    for nw in range(2):
                        dst = psA[:, nw * 512:(nw + 1) * 512]
                        for kc in range(NKC):
                            t.matmul(dst, zf[:, kc * 128:(kc + 1) * 128],
                                     sb_wgate[:, kc * D + nw * 512: kc * D + (nw + 1) * 512],
                                     start=(kc == 0), stop=False)
                        t.matmul(dst, r2sl, sb_rg[:, nw * 512:(nw + 1) * 512], start=False, stop=True)
                    mark(t, "TT")
                GATE_TT = cnt["TT"]

                @block.scalar
                def _(s):
                    s.wait_ge(TT, GATE_TT)
                    if c > 0:
                        s.wait_ge(VV, gn_vv[c - 1])
                    s.activation(sb_gate[:], psA[:, 0:1024], ACT.Sigmoid, scale=sb_sm[:, smo + 9:smo + 10],
                                 accum_out=sb_sm[:, smo + 5:smo + 6])
                    mark(s, "SS")
                SIG_SS = cnt["SS"]
                atok[0] = ("SS", SIG_SS)
                atok[1] = ("SS", SIG_SS)

                @block.vector
                def _(v):
                    v.wait_ge(SS, SIG_SS)
                    v.tensor_scalar(sb_sm[:, smo + 6:smo + 7], sb_sm[:, smo + 5:smo + 6], 1.0 / D, EPS, ALU.mult, ALU.add)
                    chain(v)
                    v.reciprocal(sb_sm[:, smo + 7:smo + 8], sb_sm[:, smo + 6:smo + 7])
                    chain(v)
                    v.tensor_scalar_mul(sb_gaten[:], sb_gate[:], sb_sm[:, smo + 7:smo + 8])
                    mark(v, "VV")
                GN_VV = cnt["VV"]

                def emit_proj(t, nw0, dsts):
                    for j, dst in enumerate(dsts):
                        nw = nw0 + j
                        for kc in range(NKC):
                            t.matmul(dst, zf[:, kc * 128:(kc + 1) * 128],
                                     sb_wqkv[:, kc * 3 * D + nw * 512: kc * 3 * D + (nw + 1) * 512],
                                     start=(kc == 0), stop=False)
                        t.matmul(dst, sb_ones[:, 0:128], sb_rq[:, nw * 512:(nw + 1) * 512],
                                 start=False, stop=True)

                @block.tensor
                def _(t):
                    if c == 0:
                        t.wait_ge(SS, WSC_SS)
                    wtok(t, atok[2]); wtok(t, atok[3])
                    emit_proj(t, 0, [psA[:, 1024:1536], psA[:, 1536:2048]])
                    mark(t, "TT")
                Q_TT = cnt["TT"]

                @block.tensor
                def _(t):
                    wtok(t, atok[0]); wtok(t, atok[1])
                    emit_proj(t, 2, [psA[:, 0:512], psA[:, 512:1024]])
                    mark(t, "TT")
                K_TT = cnt["TT"]
                s2state[c] = (GN_VV, Q_TT, K_TT, emit_proj)

            def emit_s2b(c):
                par = c % 2
                zf = sb_ztok[:, par * D:(par + 1) * D]
                q4s = sb_q4st[:, 0:D]
                tp = psT[:, 0:1024]
                GN_VV, Q_TT, K_TT, emit_proj = s2state[c]

                @block.scalar
                def _(s):
                    s.wait_ge(TT, Q_TT)
                    if c > 0:
                        s.wait_ge(VV, q4k4_vv[0])
                    s.activation(sb_tmp1[:], psA[:, 1024:2048], ACT.Relu)
                    mark(s, "SS")
                QR_SS = cnt["SS"]
                atok[2] = ("SS", QR_SS)
                atok[3] = ("SS", QR_SS)

                @block.vector
                def _(v):
                    v.wait_ge(SS, QR_SS)
                    v.wait_ge(VV, GN_VV)
                    v.tensor_mul(sb_tmp1[:], sb_tmp1[:], sb_gaten[:])
                    mark(v, "VV")
                QG_VV = cnt["VV"]

                @block.scalar
                def _(s):
                    s.wait_ge(TT, K_TT)
                    s.activation(sb_tmp2[:], psA[:, 0:1024], ACT.Relu)
                    mark(s, "SS")
                KR_SS = cnt["SS"]
                atok[0] = ("SS", KR_SS)
                atok[1] = ("SS", KR_SS)

                @block.vector
                def _(v):
                    v.wait_ge(SS, KR_SS)
                    v.tensor_mul(sb_tmp2[:], sb_tmp2[:], sb_gaten[:])
                    mark(v, "VV")
                QKG_VV = cnt["VV"]
                gn_vv[c] = QKG_VV

                @block.scalar
                def _(s):
                    s.wait_ge(VV, QG_VV)
                    s.activation(sb_tmp1[:], sb_tmp1[:], ACT.Square)
                    mark(s, "SS")
                QS_SS = cnt["SS"]

                @block.vector
                def _(v):
                    v.wait_ge(SS, QS_SS)
                    if c >= 2:
                        v.wait_ge(TT, q4t_tt_by[c - 2])
                    v.tensor_scalar(sb_q4tok[:, (c % 2) * D:(c % 2 + 1) * D], sb_tmp1[:], EPS, None, ALU.add)
                    mark(v, "VV")
                Q4_VV = cnt["VV"]
                q4_vv_by[c] = Q4_VV

                @block.scalar
                def _(s):
                    s.wait_ge(VV, QKG_VV)
                    s.activation(sb_tmp2[:], sb_tmp2[:], ACT.Square)
                    mark(s, "SS")
                KS_SS = cnt["SS"]

                @block.vector
                def _(v):
                    v.wait_ge(SS, KS_SS)
                    if c >= 2:
                        v.wait_ge(TT, state_tt_by[c - 2])
                    v.tensor_scalar(sb_k4tok[:, (c % 2) * D:(c % 2 + 1) * D], sb_tmp2[:], EPS, None, ALU.add)
                    mark(v, "VV")
                K4_VV = cnt["VV"]
                k4_vv_by[c] = K4_VV
                q4k4_vv[0] = cnt["VV"]

                @block.tensor
                def _(t):
                    wtok(t, atok[2]); wtok(t, atok[3])
                    emit_proj(t, 4, [psA[:, 1024:1536], psA[:, 1536:2048]])
                    mark(t, "TT")
                V_TT = cnt["TT"]
                vproj_tt[c] = V_TT
                s2tail[c] = True

                @block.scalar
                def _(s):
                    s.wait_ge(TT, V_TT)
                    s.wait_ge(sems["GG"], SETUP_GG)
                    dst = sb_vtok[:, c * HEADS * 65:(c + 1) * HEADS * 65].rearrange("p (h w) -> p h w", w=65)[:, :, 0:64]
                    s.activation(dst[:, 0:8, :], psA[:, 1024:1536].rearrange("p (h w) -> p h w", w=64), ACT.Copy)
                    s.activation(dst[:, 8:16, :], psA[:, 1536:2048].rearrange("p (h w) -> p h w", w=64), ACT.Copy)
                    mark(s, "SS")
                VE_SS = cnt["SS"]
                ve_ss_by[c] = VE_SS
                atok[2] = ("SS", VE_SS)
                atok[3] = ("SS", VE_SS)

            def emit_s2c(c):
                par = c % 2
                q4s = sb_q4st[:, 0:D]
                tp = psT[:, 0:1024]
                q4slot = sb_q4tok[:, par * D:(par + 1) * D]
                k4slot = sb_k4tok[:, par * D:(par + 1) * D]

                @block.tensor
                def _(t):
                    t.wait_ge(VV, q4_vv_by[c])
                    wtok(t, atok["t0"])
                    for kc in range(NKC):
                        t.transpose(tp[:, kc * 128:(kc + 1) * 128], q4slot[:, kc * 128:(kc + 1) * 128], sb_ident[:])
                    q4t_tt_by[c] = mark(t, "TT")
                Q4T_TT = cnt["TT"]

                @block.vector
                def _(v):
                    v.wait_ge(TT, Q4T_TT)
                    if c >= 1:
                        v.wait_ge(sems["DSP"], spill_ds[c - 1])
                    v.tensor_copy(q4s, tp[:, 0:1024])
                    mark(v, "VV")
                Q4F_VV = cnt["VV"]
                atok["t0"] = ("VV", Q4F_VV)

                @block.tensor
                def _(t):
                    t.wait_ge(VV, k4_vv_by[c])
                    wtok(t, atok["t0"])
                    for kc in range(NKC):
                        t.transpose(tp[:, kc * 128:(kc + 1) * 128], k4slot[:, kc * 128:(kc + 1) * 128], sb_ident[:])
                    mark(t, "TT")
                K4T_TT = cnt["TT"]

                @block.vector
                def _(v):
                    v.wait_ge(TT, K4T_TT)
                    dst = sb_k4fm[:].rearrange("p (kc tok) -> p kc tok", tok=TC)[:, :, c * 128:(c + 1) * 128]
                    v.tensor_copy(dst, tp[:, 0:1024].rearrange("p (kc w) -> p kc w", w=128))
                    mark(v, "VV")
                K4F_VV = cnt["VV"]
                atok["t0"] = ("VV", K4F_VV)

                @block.sync
                def _(s):
                    s.wait_ge(VV, Q4F_VV)
                    s.dma_start(out=d_q4spill.ap()[c], in_=q4s).then_inc(sems["DSP"], 16)
                    bump("DSP", 16)
                spill_ds[c] = cnt["DSP"]

                @block.tensor
                def _(t):
                    t.wait_ge(SS, ve_ss_by[c])
                    if c > 0:
                        t.wait_ge(VV, snap_vv[c - 1])
                    for h in range(HEADS):
                        p, half = h // 2, h % 2
                        if p < 7:
                            dst = psSA[half * 64:(half + 1) * 64, p * 65:(p + 1) * 65]
                        else:
                            dst = psSB[half * 64:(half + 1) * 64, 0:65]
                        st = (c == 0) and (h in (0, 1, 14, 15))
                        t.matmul(dst, k4slot[:, h * 64:(h + 1) * 64],
                                 sb_vtok[:, c * HEADS * 65 + h * 65: c * HEADS * 65 + (h + 1) * 65],
                                 start=st, stop=(c == NCH - 1 and h >= 14), skip_group_check=True)
                    state_tt_by[c] = mark(t, "TT")
                state_tt[c] = cnt["TT"]

                @block.vector
                def _(v):
                    v.wait_ge(TT, state_tt[c])
                    if c < NCH - 1:
                        slot = sb_snaps[:, (c + 1) * 520:(c + 2) * 520]
                        v.tensor_copy(slot[:, 0:455], psSA[:, 0:455])
                        v.tensor_copy(slot[:, 455:520], psSB[:, 0:65])
                    else:
                        v.tensor_copy(sb_exch[:, 0:455], psSA[:, 0:455])
                        v.tensor_copy(sb_exch[:, 455:520], psSB[:, 0:65])
                    mark(v, "VV")
                snap_vv[c] = cnt["VV"]

            s2state = {}
            zf_vv = {}
            vproj_tt = {}
            s2tail = {}
            q4t_tt_by = {}
            state_tt_by = {}
            q4_vv_by = {}
            k4_vv_by = {}
            ve_ss_by = {}
            emit_s1(0)
            emit_s1b(0)
            for c in range(NCH):
                if c + 1 < NCH:
                    emit_s1(c + 1)
                emit_s2a(c)
                if c + 1 < NCH:
                    emit_s1b(c + 1)
                emit_s2b(c)
                if c >= 1:
                    emit_s2c(c - 1)
            emit_s2c(NCH - 1)

            A_SNAP = snap_vv[NCH - 1]
            ALL_SPILLS = cnt["DSP"]

        with ExitStack() as dd:
            ed = dd.enter_context
            sb_wp = ed(nc.sbuf_tensor("wp_bf", [128, NKC * D], BF16))
            sb_u1 = ed(nc.sbuf_tensor("u1_bf", [128, NKC * RANK], BF16))
            sb_v1 = ed(nc.sbuf_tensor("v1_bf", [128, 2 * DFF], BF16))
            sb_u2 = ed(nc.sbuf_tensor("u2_bf", [128, 32 * RANK], BF16))
            sb_v2 = ed(nc.sbuf_tensor("v2_bf", [128, 2 * D], BF16))
            sb_q4pf = ed(nc.sbuf_tensor("q4pf", [128, 2 * D], BF16))
            sb_am = ed(nc.sbuf_tensor("am", [128, HEADS * 128], BF16))
            sb_den = ed(nc.sbuf_tensor("den", [128, 16], F32))
            sb_rec = ed(nc.sbuf_tensor("recip", [128, 16], F32))
            sb_osb = ed(nc.sbuf_tensor("osb", [128, D], BF16))
            sb_ofm = ed(nc.sbuf_tensor("ofm", [128, 2 * NKC * 256], BF16))
            sb_x2 = ed(nc.sbuf_tensor("x2", [128, 2 * D], F32))
            sb_z3 = ed(nc.sbuf_tensor("z3", [128, D], BF16))
            sb_z3fm = ed(nc.sbuf_tensor("z3fm", [128, NKC * 256], BF16))
            sb_psb = ed(nc.sbuf_tensor("psb", [128, 512], BF16))
            sb_hsb = ed(nc.sbuf_tensor("hsb", [128, 32 * 256], BF16))
            sb_qsb = ed(nc.sbuf_tensor("qsb", [128, 512], BF16))
            sb_x3 = ed(nc.sbuf_tensor("x3", [128, 2 * D], F32))
            sb_st2 = ed(nc.sbuf_tensor("st2", [128, 12], F32))
            sb_sm2 = ed(nc.sbuf_tensor("sm2", [128, 6], F32))
            sb_warm = ed(nc.sbuf_tensor("warm", [128, 1], F32))
            psC = ed(nc.psum_tensor("psC", [128, 4096], F32))

            btok = {b_: ("VV", 0) for b_ in range(8)}

            def bw(t, *banks):
                seen = set()
                for b_ in banks:
                    if btok[b_] not in seen:
                        seen.add(btok[b_])
                        wtok(t, btok[b_])

            def bs(tokval, *banks):
                for b_ in banks:
                    btok[b_] = tokval

            @block.scalar
            def _(s):
                s.wait_ge(VV, A_SNAP)
                s.dma_start(out=d_ccin.ap(), in_=sb_exch[:]).then_inc(sems["DGE"], 16)
                bump("DGE", 16)
            CCIN_DGE = cnt["DGE"]

            @block.gpsimd
            def _(g):
                g.wait_ge(sems["DGE"], CCIN_DGE)
                g.collective_compute("AllGather", ALU.bypass,
                                     replica_groups=[[0, 1], [2, 3], [4, 5], [6, 7]],
                                     ins=[d_ccin.ap().opt()],
                                     outs=[d_ccout.ap().opt()]).then_inc(CC, 1)
                bump("CC", 1)
            EXCH_CC = cnt["CC"]

            @block.scalar
            def _(s):
                s.wait_ge(CC, EXCH_CC)
                s.dma_start(out=sb_exch[:], in_=d_ccout.ap()[0]).then_inc(sems["DGE"], 16)
                bump("DGE", 16)
            EXCH_DG = cnt["DGE"]

            snapfix_vv = {}

            @block.vector
            def _(v):
                v.wait_ge(sems["DGE"], EXCH_DG)
                v.tensor_scalar_mul(sb_exch[:], sb_exch[:], sb_wprev[:])
                chain(v)
                v.tensor_copy(sb_snaps[:, 0:520], sb_exch[:])
                snapfix_vv[0] = mark(v, "VV")
                for c in range(1, NCH):
                    slot = sb_snaps[:, c * 520:(c + 1) * 520]
                    v.tensor_add(slot, slot, sb_exch[:])
                    snapfix_vv[c] = mark(v, "VV")
            SNAPFIX_VV = cnt["VV"]

            @block.gpsimd
            def _(g):
                g.wait_ge(VV, A_SNAP)
                g.wait_ge(sems["DGE"], 32)
                for kc in range(NKC):
                    g.dma_start(out=sb_wp[:, kc * D:(kc + 1) * D],
                                in_=d_wp[kc * 128:(kc + 1) * 128, :]).then_inc(DG, 16)
                    g.dma_start(out=sb_u1[:, kc * RANK:(kc + 1) * RANK],
                                in_=d_u1[kc * 128:(kc + 1) * 128, :]).then_inc(DG, 16)
                    bump("DG", 32)
                for rc in range(2):
                    g.dma_start(out=sb_v1[:, rc * DFF:(rc + 1) * DFF],
                                in_=d_v1[rc * 128:(rc + 1) * 128, :],
                                max_dma_last_dim=8000).then_inc(DG, 16)
                    g.dma_start(out=sb_v2[:, rc * D:(rc + 1) * D],
                                in_=d_v2[rc * 128:(rc + 1) * 128, :]).then_inc(DG, 16)
                    bump("DG", 32)
            DW_DG = cnt["DG"]

            @block.gpsimd
            def _(g):
                for fc in range(32):
                    g.dma_start(out=sb_u2[:, fc * RANK:(fc + 1) * RANK],
                                in_=d_u2[fc * 128:(fc + 1) * 128, :]).then_inc(sems["DG2"], 16)
                    bump("DG2", 16)
            DW_DG2 = cnt["DG2"]


            if DEBUG_TAPS:
                @block.sync
                def _(s):
                    s.wait_ge(VV, SNAPFIX_VV)
                    s.dma_start(out=taps["t_snap"], in_=sb_snaps[:, 520:1040]).then_inc(ST, 16)
                    bump("ST", 16)

            q4use_tt = {}
            ofm_ss = {}
            proj_done = [0]
            ot_tt = {}
            am_vv = {}
            div_vv = {}

            def c_a(c):
                par = c % 2
                q4p = sb_q4pf[:, par * D:(par + 1) * D]
                dsq = "DSQ0" if par == 0 else "DSQ1"

                @block.sync
                def _(s):
                    if c >= 2:
                        s.wait_ge(TT, q4use_tt[c - 2])
                    else:
                        s.wait_ge(sems["DSP"], ALL_SPILLS)
                    s.dma_start(out=q4p, in_=d_q4spill.ap()[c]).then_inc(sems[dsq], 16)
                    bump(dsq, 16)
                Q4P_DS = cnt[dsq]

                k4 = sb_k4fm[:].rearrange("p (kc tok) -> p kc tok", tok=TC)
                q4 = q4p.rearrange("p (kc w) -> p kc w", w=128)

                @block.tensor
                def _(t):
                    t.wait_ge(sems[dsq], Q4P_DS)
                    bw(t, 0, 1, 2, 3)
                    for p in range(NPAIR):
                        t.matmul(psC[:, p * 128:(p + 1) * 128],
                                 k4[0:64, p, c * 128:(c + 1) * 128], q4[0:64, p, :],
                                 start=True, stop=True, tile_position=(0, 0))
                        t.matmul(psC[:, 1024 + p * 128:1024 + (p + 1) * 128],
                                 k4[64:128, p, c * 128:(c + 1) * 128], q4[64:128, p, :],
                                 start=True, stop=True, tile_position=(64, 0))
                    mark(t, "TT")
                ATT_TT = cnt["TT"]

                @block.vector
                def _(v):
                    v.wait_ge(TT, ATT_TT)
                    amv = sb_am[:].rearrange("p (h w) -> p h w", w=128)
                    mb = sb_mask[:].unsqueeze(1).to_broadcast([128, 8, 128])
                    v.tensor_tensor(amv[:, 0::2, :], psC[:, 0:1024].rearrange("p (h w) -> p h w", w=128), mb, ALU.mult)
                    v.tensor_tensor(amv[:, 1::2, :], psC[:, 1024:2048].rearrange("p (h w) -> p h w", w=128), mb, ALU.mult)
                    mark(v, "VV")
                am_vv[c] = cnt["VV"]
                bs(("VV", am_vv[c]), 0, 1, 2, 3)

            def c_b(c):
                par = c % 2
                q4p = sb_q4pf[:, par * D:(par + 1) * D]
                q4 = q4p.rearrange("p (kc w) -> p kc w", w=128)

                @block.tensor
                def _(t):
                    t.wait_ge(VV, am_vv[c])
                    for h in range(HEADS):
                        i = h // 2
                        slot = (h % 2) * 1024 + i * 128
                        t.matmul(psC[:, slot:slot + 65], sb_am[:, h * 128:(h + 1) * 128],
                                 sb_vtok[:, c * HEADS * 65 + h * 65: c * HEADS * 65 + (h + 1) * 65],
                                 start=(h in (0, 1, 8, 9)), stop=False, skip_group_check=True)
                    t.wait_ge(VV, snapfix_vv[c])
                    for p in range(NPAIR):
                        off = c * 520 + (p * 65 if p < 7 else 455)
                        t.matmul(psC[:, p * 128:p * 128 + 65],
                                 q4[0:64, p, :], sb_snaps[0:64, off:off + 65],
                                 start=False, stop=True, tile_position=(0, 0), skip_group_check=True)
                        t.matmul(psC[:, 1024 + p * 128:1024 + p * 128 + 65],
                                 q4[64:128, p, :], sb_snaps[64:128, off:off + 65],
                                 start=False, stop=True, tile_position=(64, 0), skip_group_check=True)
                    mark(t, "TT")
                NUMM_TT = cnt["TT"]
                q4use_tt[c] = NUMM_TT

                @block.vector
                def _(v):
                    v.wait_ge(TT, NUMM_TT)
                    if c >= 1:
                        v.wait_ge(TT, ot_tt[c - 1])
                    dd_ = psC[:, 0:2048].rearrange("p (sl w) -> p sl w", w=128)[:, :, 64:65]
                    v.tensor_scalar(sb_den[:].unsqueeze(2), dd_, EPS, None, ALU.add)
                    chain(v)
                    v.reciprocal(sb_rec[:], sb_den[:])
                    chain(v)
                    ov = sb_osb[:].rearrange("p (h w) -> p h w", w=64)
                    ne = psC[:, 0:1024].rearrange("p (sl w) -> p sl w", w=128)[:, :, 0:64]
                    no = psC[:, 1024:2048].rearrange("p (sl w) -> p sl w", w=128)[:, :, 0:64]
                    v.tensor_tensor(ov[:, 0::2, :], ne,
                                    sb_rec[:, 0:8].unsqueeze(2).to_broadcast([128, 8, 64]), ALU.mult)
                    v.tensor_tensor(ov[:, 1::2, :], no,
                                    sb_rec[:, 8:16].unsqueeze(2).to_broadcast([128, 8, 64]), ALU.mult)
                    mark(v, "VV")
                div_vv[c] = cnt["VV"]
                bs(("VV", div_vv[c]), 0, 1, 2, 3)

            def c_c(c):
                @block.tensor
                def _(t):
                    t.wait_ge(VV, div_vv[c])
                    bw(t, 0)
                    dst = psC[:, 0:512].bitcast(BF16)
                    for kc in range(NKC):
                        t.transpose(dst[:, kc * 128:(kc + 1) * 128], sb_osb[:, kc * 128:(kc + 1) * 128], sb_ident[:])
                    mark(t, "TT")
                OT_TT = cnt["TT"]
                ot_tt[c] = OT_TT

            def c_c_v(c):
                OT_TT = ot_tt[c]

                @block.vector
                def _(v):
                    v.wait_ge(TT, OT_TT)
                    v.wait_ge(TT, proj_done[0])
                    src = psC[:, 0:512].bitcast(BF16).rearrange("p (kc w) -> p kc w", w=128)
                    hpar = (c // 2) % 2
                    dst = sb_ofm[:, hpar * NKC * 256:(hpar + 1) * NKC * 256]
                    dst = dst.rearrange("p (kc w) -> p kc w", w=256)[:, :, (c % 2) * 128:(c % 2) * 128 + 128]
                    v.tensor_copy(dst, src)
                    mark(v, "VV")
                bs(("VV", cnt["VV"]), 0)
                ofm_ss[c] = cnt["VV"]

            def d_proj(hs, j):
                c = hs * 2 + j
                hpar = hs % 2
                x2sl = sb_x2[:, j * D:(j + 1) * D]

                @block.sync
                def _(s):
                    if hs > 0:
                        s.wait_ge(VV, x2_free[j])
                    else:
                        s.wait_ge(sems["DSP"], ALL_SPILLS)
                    s.dma_start(out=x2sl, in_=d_x[c * 128:(c + 1) * 128, :]).then_inc(sems["DSXD"], 16)
                    bump("DSXD", 16)
                XD_DS = cnt["DSXD"]

                proj_nw = []

                @block.tensor
                def _(t):
                    t.wait_ge(VV, ofm_ss[hs * 2 + 1])
                    t.wait_ge(DG, DW_DG)
                    ofmv = sb_ofm[:, hpar * NKC * 256:(hpar + 1) * NKC * 256]
                    for nw in range(2):
                        bw(t, 4 + nw)
                        dst = psC[:, 2048 + nw * 512:2048 + (nw + 1) * 512]
                        for kc in range(NKC):
                            t.matmul(dst, ofmv[:, kc * 256 + j * 128: kc * 256 + (j + 1) * 128],
                                     sb_wp[:, kc * D + nw * 512:kc * D + (nw + 1) * 512],
                                     start=(kc == 0), stop=False)
                        t.matmul(dst, sb_ones[:, 0:128], sb_rp[:, nw * 512:(nw + 1) * 512],
                                 start=False, stop=True)
                        proj_nw.append(mark(t, "TT"))
                PROJ_TT = proj_nw[1]
                proj_done[0] = PROJ_TT

                add_m = []

                @block.vector
                def _(v):
                    v.wait_ge(TT, proj_nw[0])
                    v.wait_ge(sems["DSXD"], XD_DS)
                    v.tensor_add(x2sl[:, 0:512], psC[:, 2048:2560], x2sl[:, 0:512])
                    chain(v)
                    add_m.append(cnt["VV"])
                    v.bn_stats(sb_st2[:, 0:6], x2sl[:, 0:512])
                    v.wait_ge(TT, PROJ_TT)
                    v.tensor_add(x2sl[:, 512:1024], psC[:, 2560:3072], x2sl[:, 512:1024])
                    chain(v)
                    add_m.append(cnt["VV"])
                    v.bn_stats(sb_st2[:, 6:12], x2sl[:, 512:1024])
                    chain(v)
                    v.bn_aggr(sb_sm2[:, 0:2], sb_st2[:])
                    chain(v)
                    v.tensor_scalar(sb_sm2[:, 2:3], sb_sm2[:, 1:2], EPS, None, ALU.add)
                    chain(v)
                    v.reciprocal(sb_sm2[:, 3:4], sb_sm2[:, 2:3])
                    mark(v, "VV")
                LN3_VV = cnt["VV"]
                bs(("VV", add_m[0]), 4)
                bs(("VV", add_m[1]), 5)

                @block.scalar
                def _(s):
                    if j == 0:
                        s.activation(sb_warm[:, 0:1], sb_eps[:, 0:1], ACT.Sqrt)
                    s.wait_ge(VV, LN3_VV)
                    s.activation(sb_sm2[:, 4:5], sb_sm2[:, 3:4], ACT.Sqrt)
                    mark(s, "SS")
                d_state[(hs, j)] = (cnt["SS"], x2sl)

            d_z3h = {}

            def d_z3_v(hs, j):
                R3_SS, x2sl = d_state[(hs, j)]

                z3h = []
                d_z3h[(hs, j)] = z3h

                @block.vector
                def _(v):
                    v.wait_ge(SS, R3_SS)
                    v.tensor_scalar(sb_sm2[:, 5:6], sb_sm2[:, 0:1], sb_sm2[:, 4:5], -1.0, ALU.mult, ALU.mult)
                    chain(v)
                    if hs + j > 0:
                        v.wait_ge(TT, z3t_used[0])
                    v.tensor_scalar(sb_z3[:, 0:512], x2sl[:, 0:512], sb_sm2[:, 4:5], sb_sm2[:, 5:6], ALU.mult, ALU.add)
                    z3h.append(mark(v, "VV"))
                    v.tensor_scalar(sb_z3[:, 512:1024], x2sl[:, 512:1024], sb_sm2[:, 4:5], sb_sm2[:, 5:6], ALU.mult, ALU.add)
                    z3h.append(mark(v, "VV"))

            def d_z3_t(hs, j):
                z3h = d_z3h[(hs, j)]

                @block.tensor
                def _(t):
                    t.wait_ge(VV, z3h[0])
                    bw(t, 6)
                    dst = psC[:, 3072:3584].bitcast(BF16)
                    for kc in range(4):
                        t.transpose(dst[:, kc * 128:(kc + 1) * 128], sb_z3[:, kc * 128:(kc + 1) * 128], sb_ident[:])
                    t.wait_ge(VV, z3h[1])
                    for kc in range(4, NKC):
                        t.transpose(dst[:, kc * 128:(kc + 1) * 128], sb_z3[:, kc * 128:(kc + 1) * 128], sb_ident[:])
                    z3t_used[0] = mark(t, "TT")
                Z3T_TT = cnt["TT"]

                @block.vector
                def _(v):
                    v.wait_ge(TT, Z3T_TT)
                    src = psC[:, 3072:3584].bitcast(BF16).rearrange("p (kc w) -> p kc w", w=128)
                    dst = sb_z3fm[:].rearrange("p (kc w) -> p kc w", w=256)[:, :, j * 128:(j + 1) * 128]
                    v.tensor_copy(dst, src)
                    mark(v, "VV")
                bs(("VV", cnt["VV"]), 6)
                z3f_vv[j] = cnt["VV"]

            def d_ffn(hs):
                TW = 256

                @block.tensor
                def _(t):
                    t.wait_ge(VV, z3f_vv[1])
                    bw(t, 7)
                    for mc in range(2):
                        dst = psC[:, 3584 + mc * TW:3584 + (mc + 1) * TW]
                        for kc in range(NKC):
                            t.matmul(dst, sb_u1[:, kc * RANK + mc * 128:kc * RANK + (mc + 1) * 128],
                                     sb_z3fm[:, kc * TW:(kc + 1) * TW],
                                     start=(mc == 0 and kc == 0), stop=(kc == NKC - 1), skip_group_check=True)
                    mark(t, "TT")
                U1_TT = cnt["TT"]

                @block.vector
                def _(v):
                    v.wait_ge(TT, U1_TT)
                    v.tensor_copy(sb_psb[:], psC[:, 3584:4096])
                    mark(v, "VV")
                P_VV = cnt["VV"]
                bs(("VV", P_VV), 7)

                ss_base = cnt["SS"]
                v1_tt = []

                @block.tensor
                def _(t):
                    t.wait_ge(VV, P_VV)
                    t.wait_ge(sems["DG2"], DW_DG2)
                    bw(t, 4, 5, 6)
                    for pr in range(16):
                        if pr >= 2:
                            t.wait_ge(SS, ss_base + pr - 1)
                        for half in range(2):
                            fc = pr * 2 + half
                            dst = psC[:, 2048 + (pr % 2) * 512 + half * TW:
                                      2048 + (pr % 2) * 512 + (half + 1) * TW]
                            for rc in range(2):
                                t.matmul(dst, sb_v1[:, rc * DFF + fc * 128:rc * DFF + (fc + 1) * 128],
                                         sb_psb[:, rc * TW:(rc + 1) * TW],
                                         start=(rc == 0), stop=(rc == 1))
                        v1_tt.append(mark(t, "TT"))
                        if pr >= 2:
                            for half in range(2):
                                fc = (pr - 2) * 2 + half
                                for mc in range(2):
                                    t.matmul(psC[:, 3072 + mc * TW:3072 + (mc + 1) * TW],
                                             sb_u2[:, fc * RANK + mc * 128:fc * RANK + (mc + 1) * 128],
                                             sb_hsb[:, fc * TW:(fc + 1) * TW],
                                             start=(fc == 0 and mc == 0), stop=False, skip_group_check=True)
                    for prl in (14, 15):
                        t.wait_ge(SS, ss_base + prl + 1)
                        for half in range(2):
                            fc = prl * 2 + half
                            for mc in range(2):
                                t.matmul(psC[:, 3072 + mc * TW:3072 + (mc + 1) * TW],
                                         sb_u2[:, fc * RANK + mc * 128:fc * RANK + (mc + 1) * 128],
                                         sb_hsb[:, fc * TW:(fc + 1) * TW],
                                         start=False, stop=(fc == 31 and mc == 1), skip_group_check=True)
                    mark(t, "TT")
                U2_TT = cnt["TT"]

                @block.scalar
                def _(s):
                    s.activation(sb_warm[:, 0:1], sb_eps[:, 0:1], ACT.Gelu_apprx_tanh)
                    for pr in range(16):
                        s.wait_ge(TT, v1_tt[pr])
                        src = psC[:, 2048 + (pr % 2) * 512:2048 + (pr % 2) * 512 + 512]
                        s.activation(sb_hsb[:, pr * 2 * TW:pr * 2 * TW + TW], src[:, 0:TW],
                                     ACT.Gelu_apprx_tanh, bias=sb_b1c[:, 2 * pr:2 * pr + 1])
                        s.activation(sb_hsb[:, pr * 2 * TW + TW:(pr + 1) * 2 * TW], src[:, TW:2 * TW],
                                     ACT.Gelu_apprx_tanh, bias=sb_b1c[:, 2 * pr + 1:2 * pr + 2])
                        assert mark(s, "SS") == ss_base + pr + 1
                bs(("SS", cnt["SS"]), 4, 5)

                @block.vector
                def _(v):
                    v.wait_ge(TT, U2_TT)
                    v.tensor_copy(sb_qsb[:], psC[:, 3072:3584])
                    mark(v, "VV")
                QQ_VV = cnt["VV"]
                bs(("VV", QQ_VV), 6)
                return QQ_VV

            def d_out(hs, QQ_VV):
                TW = 256

                for j in range(2):
                    c = hs * 2 + j
                    xpar = c % 2
                    x3b = sb_x3[:, xpar * D:(xpar + 1) * D]
                    dso = "DSO" if xpar == 0 else "DSO1"
                    _, x2sl = d_state[(hs, j)]

                    v2_nw = []

                    @block.tensor
                    def _(t):
                        t.wait_ge(VV, QQ_VV)
                        for nw in range(2):
                            bw(t, 4 + nw)
                            dst = psC[:, 2048 + nw * 512:2048 + (nw + 1) * 512]
                            for rc in range(2):
                                t.matmul(dst, sb_qsb[:, rc * TW + j * 128:rc * TW + (j + 1) * 128],
                                         sb_v2[:, rc * D + nw * 512:rc * D + (nw + 1) * 512],
                                         start=(rc == 0), stop=False)
                            t.matmul(dst, sb_ones[:, 0:128], sb_rb2[:, nw * 512:(nw + 1) * 512],
                                     start=False, stop=True)
                            v2_nw.append(mark(t, "TT"))

                    x3_m = []

                    @block.vector
                    def _(v):
                        v.wait_ge(TT, v2_nw[0])
                        if c >= 2:
                            v.wait_ge(sems[dso], x3_ds[xpar])
                        v.tensor_add(x3b[:, 0:512], psC[:, 2048:2560], x2sl[:, 0:512])
                        mark(v, "VV")
                        x3_m.append(cnt["VV"])
                        v.wait_ge(TT, v2_nw[1])
                        v.tensor_add(x3b[:, 512:1024], psC[:, 2560:3072], x2sl[:, 512:1024])
                        mark(v, "VV")
                        x3_m.append(cnt["VV"])
                    X3_VV = x3_m[1]
                    bs(("VV", x3_m[0]), 4)
                    bs(("VV", x3_m[1]), 5)
                    x2_free[j] = X3_VV

                    @block.scalar
                    def _(s):
                        s.wait_ge(VV, X3_VV)
                        s.dma_start(out=d_out_ap[c * 128:(c + 1) * 128, :], in_=x3b).then_inc(sems[dso], 16)
                        x3_ds[xpar] = bump(dso, 16)

            d_state = {}
            x2_free = {0: 0, 1: 0}
            z3t_used = [0]
            z3f_vv = {}
            x3_ds = {0: 0, 1: 0}
            d_out_ap = d_out_dram
            for c0 in (0, 1):
                c_a(c0); c_b(c0); c_c(c0); c_c_v(c0)
            qq_prev = None
            for hs in range(8):
                if hs < 7:
                    c_a(2 * hs + 2)
                if hs > 0:
                    d_out(hs - 1, qq_prev)
                if hs < 7:
                    c_b(2 * hs + 2)
                d_proj(hs, 0)
                if hs < 7:
                    c_c(2 * hs + 2)
                    c_c_v(2 * hs + 2)
                    c_a(2 * hs + 3)
                    d_z3_v(hs, 0)
                    c_b(2 * hs + 3)
                    d_z3_t(hs, 0)
                else:
                    d_z3_v(hs, 0)
                    d_z3_t(hs, 0)
                d_proj(hs, 1)
                if hs < 7:
                    c_c(2 * hs + 3)
                d_z3_v(hs, 1)
                d_z3_t(hs, 1)
                if hs < 7:
                    c_c_v(2 * hs + 3)
                qq_prev = d_ffn(hs)
            d_out(7, qq_prev)

            @block.sync
            def _(s):
                for nm in ["DS", "DSX0", "DSX1", "DSP", "DSQ0", "DSQ1", "DSXD", "DSO", "DSO1", "DG", "DGE", "DG2", "ST"]:
                    if cnt[nm]:
                        s.wait_ge(sems[nm], cnt[nm])

    return nc


def kernel(L0_attn_x, L0_attn_ln_g, L0_attn_ln_b, L0_attn_W_qkv, L0_attn_b_qkv,
           L0_attn_W_gate, L0_attn_b_gate, L0_attn_W_proj, L0_attn_b_proj,
           L0_ffn_U1, L0_ffn_V1, L0_ffn_b1, L0_ffn_U2, L0_ffn_V2, L0_ffn_b2):
    x = np.ascontiguousarray(np.asarray(L0_attn_x, dtype=np.float32))
    nc = build_program()
    ident16 = np.eye(128, dtype=BF)
    ident32 = np.eye(128, dtype=np.float32)
    mask = np.ascontiguousarray(np.triu(np.ones((128, 128), np.float32)))
    common = {
        "wqkv": np.ascontiguousarray(np.asarray(L0_attn_W_qkv, np.float32)),
        "wg": np.ascontiguousarray(np.asarray(L0_attn_W_gate, np.float32)),
        "wp": np.ascontiguousarray(np.asarray(L0_attn_W_proj, np.float32)),
        "u1": np.ascontiguousarray(np.asarray(L0_ffn_U1, np.float32)),
        "v1": np.ascontiguousarray(np.asarray(L0_ffn_V1, np.float32)),
        "u2": np.ascontiguousarray(np.asarray(L0_ffn_U2, np.float32)),
        "v2": np.ascontiguousarray(np.asarray(L0_ffn_V2, np.float32)),
        "lng": np.ascontiguousarray(np.asarray(L0_attn_ln_g, np.float32).reshape(8, 128).T),
        "lnb": np.ascontiguousarray(np.asarray(L0_attn_ln_b, np.float32).reshape(8, 128).T),
        "bqkv": np.ascontiguousarray(np.asarray(L0_attn_b_qkv, np.float32).reshape(1, 3 * D)),
        "bg": np.ascontiguousarray(np.asarray(L0_attn_b_gate, np.float32).reshape(1, D)),
        "bp": np.ascontiguousarray(np.asarray(L0_attn_b_proj, np.float32).reshape(1, D)),
        "b1c": np.ascontiguousarray(np.asarray(L0_ffn_b1, np.float32).reshape(32, 128).T),
        "b2": np.ascontiguousarray(np.asarray(L0_ffn_b2, np.float32).reshape(1, D)),
        "id16": ident16,
        "id32": ident32,
        "mask": mask,
    }
    in_maps = []
    for core in range(8):
        b, h = core // 2, core % 2
        m = dict(common)
        m["x"] = np.ascontiguousarray(x[b, h * TC:(h + 1) * TC, :])
        m["wprev"] = np.full((128, 1), float(h), np.float32)
        in_maps.append(m)
    res = run_bass_kernel_spmd(nc, in_maps, core_ids=list(range(8)))
    out = np.empty((B, T, D), np.float32)
    for core in range(8):
        b, h = core // 2, core % 2
        out[b, h * TC:(h + 1) * TC, :] = res.results[core]["out"]
    return out


# version 28
# speedup vs baseline: 1.0639x; 1.0248x over previous
import numpy as np
import ml_dtypes
from contextlib import ExitStack

import concourse.bass as bass
import concourse.mybir as mybir
from concourse.bass_core_rust import BassInstruction
from concourse.bass_utils import run_bass_kernel_spmd

F32 = mybir.dt.float32
BF16 = mybir.dt.bfloat16
ACT = mybir.ActivationFunctionType
ALU = mybir.AluOpType
AXX = mybir.AxisListType.X

DEBUG_TAPS = False

B, T, D = 4, 4096, 1024
HEADS, DH, NPAIR = 16, 64, 8
DFF, RANK = 4096, 256
EPS = 1e-5
TC, NCH, CH = 2048, 16, 128
NSC, SCT = 4, 512
NKC = 8
BF = ml_dtypes.bfloat16


def build_program():
    nc = bass.Bass("TRN2", target_bir_lowering=False, detect_race_conditions=False)

    d_x = nc.dram_tensor("x", [TC, D], F32, kind="ExternalInput").ap()
    d_wqkv = nc.dram_tensor("wqkv", [D, 3 * D], F32, kind="ExternalInput").ap()
    d_wg = nc.dram_tensor("wg", [D, D], F32, kind="ExternalInput").ap()
    d_wp = nc.dram_tensor("wp", [D, D], F32, kind="ExternalInput").ap()
    d_u1 = nc.dram_tensor("u1", [D, RANK], F32, kind="ExternalInput").ap()
    d_v1 = nc.dram_tensor("v1", [RANK, DFF], F32, kind="ExternalInput").ap()
    d_u2 = nc.dram_tensor("u2", [DFF, RANK], F32, kind="ExternalInput").ap()
    d_v2 = nc.dram_tensor("v2", [RANK, D], F32, kind="ExternalInput").ap()
    d_lng = nc.dram_tensor("lng", [128, 8], F32, kind="ExternalInput").ap()
    d_lnb = nc.dram_tensor("lnb", [128, 8], F32, kind="ExternalInput").ap()
    d_bqkv = nc.dram_tensor("bqkv", [1, 3 * D], F32, kind="ExternalInput").ap()
    d_bg = nc.dram_tensor("bg", [1, D], F32, kind="ExternalInput").ap()
    d_bp = nc.dram_tensor("bp", [1, D], F32, kind="ExternalInput").ap()
    d_b1 = nc.dram_tensor("b1c", [128, 32], F32, kind="ExternalInput").ap()
    d_b2 = nc.dram_tensor("b2", [1, D], F32, kind="ExternalInput").ap()
    d_id16 = nc.dram_tensor("id16", [128, 128], BF16, kind="ExternalInput").ap()
    d_id32 = nc.dram_tensor("id32", [128, 128], F32, kind="ExternalInput").ap()
    d_mask = nc.dram_tensor("mask", [128, 128], F32, kind="ExternalInput").ap()
    d_wprev = nc.dram_tensor("wprev", [128, 1], F32, kind="ExternalInput").ap()
    d_out_dram = nc.dram_tensor("out", [TC, D], F32, kind="ExternalOutput").ap()

    d_q4spill = nc.dram_tensor("q4spill", [NCH, 128, D], BF16)
    d_ccin = nc.dram_tensor("ccin", [128, 520], F32)
    d_ccout = nc.dram_tensor("ccout", [2, 128, 520], F32)

    taps = {}
    if DEBUG_TAPS:
        for nm, shp, dt in [("t_z", [128, D], BF16), ("t_gate", [128, D], F32),
                            ("t_q4", [128, D], BF16), ("t_k4", [128, D], BF16),
                            ("t_v", [128, HEADS * 65], BF16), ("t_snap", [128, 520], BF16),
                            ("t_o", [128, D], BF16), ("t_x2", [128, D], F32),
                            ("t_ofm", [128, 256], BF16)]:
            taps[nm] = nc.dram_tensor(nm, shp, dt, kind="ExternalOutput").ap()

    last_inst = {}
    nc.push_inst_callback(lambda ins: last_inst.__setitem__(ins.engine, ins))
    cnt = {"DS": 0, "DG": 0, "TT": 0, "VV": 0, "SS": 0, "CC": 0, "ST": 0,
       "DSX0": 0, "DSX1": 0, "DSP": 0, "DSQ0": 0, "DSQ1": 0, "DSXD": 0, "DSO": 0, "GG": 0,
       "DGE": 0, "DG2": 0, "DSO1": 0}

    with ExitStack() as top:
        e = top.enter_context
        sems = {n: e(nc.semaphore(n)) for n in cnt}
        DS, DG, TT, VV, SS, CC, ST = (sems[n] for n in ["DS", "DG", "TT", "VV", "SS", "CC", "ST"])

        def bump(name, amt=1):
            cnt[name] += amt
            return cnt[name]

        def mark(eng, name):
            BassInstruction(last_inst[eng.engine]).then_inc(sems[name], 1)
            return bump(name)

        def wtok(eng, tok):
            eng.wait_ge(sems[tok[0]], tok[1])

        def chain(eng, name="VV"):
            BassInstruction(last_inst[eng.engine]).then_inc(sems[name], 1)
            bump(name)
            eng.wait_ge(sems[name], cnt[name])

        sb_k4fm = e(nc.sbuf_tensor("k4fm", [128, NKC * TC], BF16))
        sb_vtok = e(nc.sbuf_tensor("vtok", [128, NCH * HEADS * 65], BF16))
        sb_snaps = e(nc.sbuf_tensor("snaps", [128, NCH * 520], BF16))
        sb_ident = e(nc.sbuf_tensor("sident", [128, 128], BF16))
        sb_id32 = e(nc.sbuf_tensor("sid32", [128, 128], F32))
        sb_mask = e(nc.sbuf_tensor("smask", [128, 128], F32))
        sb_lng = e(nc.sbuf_tensor("slng", [128, 8], F32))
        sb_lnb = e(nc.sbuf_tensor("lnbv", [128, 8], BF16))
        sb_b1c = e(nc.sbuf_tensor("sb1c", [128, 32], F32))
        sb_eps = e(nc.sbuf_tensor("epsc", [128, 1], F32))
        sb_eps2 = e(nc.sbuf_tensor("eps2c", [128, 1], F32))
        sb_wprev = e(nc.sbuf_tensor("swprev", [128, 1], F32))
        sb_ones = e(nc.sbuf_tensor("ones_row", [1, 128], BF16))
        sb_rp = e(nc.sbuf_tensor("srp", [1, D], BF16))
        sb_rb2 = e(nc.sbuf_tensor("srb2", [1, D], BF16))
        sb_exch = e(nc.sbuf_tensor("sexch", [128, 520], F32))

        block = e(nc.Block())

        @block.sync
        def _(s):
            s.dma_start(out=sb_ident[:], in_=d_id16).then_inc(DS, 16)
            s.dma_start(out=sb_id32[:], in_=d_id32).then_inc(DS, 16)
            s.dma_start(out=sb_mask[:], in_=d_mask).then_inc(DS, 16)
            s.dma_start(out=sb_lng[:], in_=d_lng).then_inc(DS, 16)
            s.dma_start(out=sb_b1c[:], in_=d_b1).then_inc(DS, 16)
            s.dma_start(out=sb_wprev[:], in_=d_wprev).then_inc(DS, 16)
            bump("DS", 6 * 16)
        SETUP_DS = cnt["DS"]

        @block.gpsimd
        def _(g):
            g.dma_start(out=sb_lnb[:], in_=d_lnb).then_inc(DG, 16)
            g.dma_start(out=sb_rp[:], in_=d_bp).then_inc(DG, 16)
            g.dma_start(out=sb_rb2[:], in_=d_b2).then_inc(DG, 16)
            bump("DG", 3 * 16)
            g.memset(sb_ones[:], 1.0).then_inc(sems["GG"], 1)
            bump("GG", 1)
        SETUP_GG = cnt["GG"]

        @block.vector
        def _(v):
            v.memset(sb_eps[:], EPS)
            v.memset(sb_eps2[:], EPS * EPS)
            v.memset(sb_vtok[:], 1.0)
            mark(v, "VV")
        EPS_VV = cnt["VV"]

        with ExitStack() as aa:
            ea = aa.enter_context
            sb_wqkv = ea(nc.sbuf_tensor("wqkv_bf", [128, NKC * 3 * D], BF16))
            sb_wgate = ea(nc.sbuf_tensor("wg_bf", [128, NKC * D], BF16))
            sb_rq = ea(nc.sbuf_tensor("srq", [1, 3 * D], BF16))
            sb_rg = ea(nc.sbuf_tensor("srg", [1, D], BF16))
            sb_r2row = ea(nc.sbuf_tensor("r2row", [1, 256], BF16))
            sb_x = ea(nc.sbuf_tensor("xa", [128, 2 * D], F32))
            sb_ztok = ea(nc.sbuf_tensor("ztok", [128, 2 * D], BF16))
            sb_gate = ea(nc.sbuf_tensor("gate", [128, D], F32))
            sb_gaten = ea(nc.sbuf_tensor("gaten", [128, D], F32))
            sb_tmp1 = ea(nc.sbuf_tensor("tmp1", [128, D], F32))
            sb_tmp2 = ea(nc.sbuf_tensor("tmp2", [128, D], F32))
            sb_q4tok = ea(nc.sbuf_tensor("q4tok", [128, 2 * D], BF16))
            sb_k4tok = ea(nc.sbuf_tensor("k4tok", [128, 2 * D], BF16))
            sb_q4st = ea(nc.sbuf_tensor("q4stage", [128, D], BF16))
            sb_stats = ea(nc.sbuf_tensor("stats", [128, 12], F32))
            sb_sm = ea(nc.sbuf_tensor("smalls", [128, 36], F32))
            psA = ea(nc.psum_tensor("psA", [128, 2048], F32))
            psT = ea(nc.psum_tensor("psT", [128, 2 * 1024], BF16))
            psSA = ea(nc.psum_tensor("psSA", [128, 512], F32))
            psSB = ea(nc.psum_tensor("psSB", [128, 512], F32))

            atok = {0: ("VV", 0), 1: ("VV", 0), 2: ("VV", 0), 3: ("VV", 0),
                    "t0": ("VV", 0), "r2": ("VV", 0)}

            @block.gpsimd
            def _(g):
                for kc in range(NKC):
                    g.dma_start(out=sb_wgate[:, kc * D:(kc + 1) * D],
                                in_=d_wg[kc * 128:(kc + 1) * 128, :]).then_inc(DG, 16)
                g.dma_start(out=sb_rg[:], in_=d_bg).then_inc(DG, 16)
                bump("DG", 9 * 16)
                for kc in range(NKC):
                    g.dma_start(out=sb_wqkv[:, kc * 3 * D:(kc + 1) * 3 * D],
                                in_=d_wqkv[kc * 128:(kc + 1) * 128, :],
                                max_dma_last_dim=8000).then_inc(DG, 16)
                bump("DG", 8 * 16)
            WQKV_DG = cnt["DG"]

            if True:
                bq_rows = [sb_gate, sb_gaten, sb_tmp1]

                @block.sync
                def _(s):
                    for i in range(3):
                        s.dma_start(out=bq_rows[i][0:1, 0:1024],
                                    in_=d_bqkv[:, i * 1024:(i + 1) * 1024]).then_inc(DS, 16)
                    bump("DS", 48)
                BQ_DS = cnt["DS"]

                rq_tt = []
                rq_vv = []
                rq_vv_base = cnt["VV"]

                @block.tensor
                def _(t):
                    t.wait_ge(DG, WQKV_DG)
                    for nw in range(6):
                        if nw >= 3:
                            t.wait_ge(VV, rq_vv_base + (nw - 3) + 1)
                        dst = psA[:1, (nw % 3) * 512:(nw % 3) * 512 + 512]
                        for kc in range(NKC):
                            t.matmul(dst, sb_lnb[:, kc:kc + 1],
                                     sb_wqkv[:, kc * 3 * D + nw * 512: kc * 3 * D + (nw + 1) * 512],
                                     start=(kc == 0), stop=(kc == NKC - 1))
                        rq_tt.append(mark(t, "TT"))

                @block.vector
                def _(v):
                    v.wait_ge(DS, BQ_DS)
                    for nw in range(6):
                        v.wait_ge(TT, rq_tt[nw])
                        v.tensor_add(sb_rq[:, nw * 512:(nw + 1) * 512],
                                     psA[:1, (nw % 3) * 512:(nw % 3) * 512 + 512],
                                     bq_rows[nw // 2][0:1, (nw % 2) * 512:(nw % 2) * 512 + 512])
                        rq_vv.append(mark(v, "VV"))
                        assert rq_vv[-1] == rq_vv_base + len(rq_vv)
                RQ_VV = cnt["VV"]
                for b_ in range(4):
                    atok[b_] = ("VV", RQ_VV)

            snap_vv = {}
            state_tt = {}
            spill_ds = {}
            z_vv = {}
            stats_vv = {}
            sm_ss = {}
            zt_tt = {}
            gn_vv = {}
            q4k4_vv = [0]
            q4t_tt_prev = [0]
            k4s_tt_prev = [0]

            def emit_s1(c):
                par = c % 2
                smo = (c % 3) * 12
                xb = sb_x[:, par * D:(par + 1) * D]
                zt = sb_ztok[:, par * D:(par + 1) * D]
                dsx = "DSX0" if par == 0 else "DSX1"

                @block.sync
                def _(s):
                    if c >= 2:
                        s.wait_ge(VV, z_vv[c - 2])
                    s.dma_start(out=xb, in_=d_x[c * 128:(c + 1) * 128, :]).then_inc(sems[dsx], 16)
                    bump(dsx, 16)
                X_DS = cnt[dsx]

                @block.vector
                def _(v):
                    v.wait_ge(sems[dsx], X_DS)
                    if c >= 3:
                        v.wait_ge(TT, zt_tt[c - 3])
                    v.bn_stats(sb_stats[:, 0:6], xb[:, 0:512])
                    v.bn_stats(sb_stats[:, 6:12], xb[:, 512:1024])
                    chain(v)
                    v.bn_aggr(sb_sm[:, smo + 0:smo + 2], sb_stats[:, 0:12])
                    mark(v, "VV")
                stats_vv[c] = cnt["VV"]

                @block.scalar
                def _(s):
                    s.wait_ge(VV, stats_vv[c])
                    s.wait_ge(VV, EPS_VV)
                    s.activation(sb_sm[:, smo + 2:smo + 3], sb_sm[:, smo + 1:smo + 2], ACT.Sqrt,
                                 scale=float(1.0 + EPS), bias=sb_eps2[:, 0:1])
                    s.activation(sb_sm[:, smo + 3:smo + 4], sb_sm[:, smo + 1:smo + 2], ACT.Sqrt,
                                 bias=sb_eps[:, 0:1])
                    mark(s, "SS")
                sm_ss[c] = cnt["SS"]

                @block.vector
                def _(v):
                    v.wait_ge(SS, sm_ss[c])
                    v.reciprocal(sb_sm[:, smo + 8:smo + 9], sb_sm[:, smo + 2:smo + 3])
                    v.reciprocal(sb_sm[:, smo + 4:smo + 5], sb_sm[:, smo + 3:smo + 4])
                    chain(v)
                    v.tensor_mul(sb_sm[:, smo + 10:smo + 11], sb_sm[:, smo + 3:smo + 4], sb_sm[:, smo + 8:smo + 9])
                    v.tensor_mul(sb_sm[:, smo + 9:smo + 10], sb_sm[:, smo + 2:smo + 3], sb_sm[:, smo + 4:smo + 5])
                    v.tensor_scalar(sb_sm[:, smo + 11:smo + 12], sb_sm[:, smo + 0:smo + 1],
                                    sb_sm[:, smo + 8:smo + 9], -1.0, ALU.mult, ALU.mult)
                    chain(v)
                    if c >= 2:
                        v.wait_ge(TT, vproj_tt[c - 2])
                    v.tensor_scalar(zt, xb, sb_sm[:, smo + 8:smo + 9],
                                    sb_sm[:, smo + 11:smo + 12], ALU.mult, ALU.add)
                    mark(v, "VV")
                z_vv[c] = cnt["VV"]

            def emit_s1b(c):
                par = c % 2
                smo = (c % 3) * 12
                zt = sb_ztok[:, par * D:(par + 1) * D]
                tp = psT[:, 0:1024]
                r2ps = psT[:1, 1024:1280].bitcast(F32)
                r2sl = sb_r2row[:, par * 128:(par + 1) * 128]

                @block.tensor
                def _(t):
                    t.wait_ge(VV, z_vv[c])
                    wtok(t, atok["t0"])
                    for kc in range(NKC):
                        t.transpose(tp[:, kc * 128:(kc + 1) * 128], zt[:, kc * 128:(kc + 1) * 128], sb_ident[:])
                    wtok(t, atok["r2"])
                    t.transpose(r2ps, sb_sm[:, smo + 10:smo + 11], sb_id32[:])
                    mark(t, "TT")
                zt_tt[c] = cnt["TT"]

                @block.vector
                def _(v):
                    v.wait_ge(TT, zt_tt[c])
                    v.tensor_copy(zt, tp[:, 0:1024])
                    v.tensor_copy(r2sl, r2ps)
                    mark(v, "VV")
                zf_vv[c] = cnt["VV"]
                atok["t0"] = ("VV", cnt["VV"])
                atok["r2"] = ("VV", cnt["VV"])

            def emit_s2a(c):
                par = c % 2
                smo = (c % 3) * 12
                zf = sb_ztok[:, par * D:(par + 1) * D]
                r2sl = sb_r2row[:, par * 128:(par + 1) * 128]
                ZF_VV = zf_vv[c]

                @block.tensor
                def _(t):
                    t.wait_ge(VV, ZF_VV)
                    wtok(t, atok[0]); wtok(t, atok[1])
                    for nw in range(2):
                        dst = psA[:, nw * 512:(nw + 1) * 512]
                        for kc in range(NKC):
                            t.matmul(dst, zf[:, kc * 128:(kc + 1) * 128],
                                     sb_wgate[:, kc * D + nw * 512: kc * D + (nw + 1) * 512],
                                     start=(kc == 0), stop=False)
                        t.matmul(dst, r2sl, sb_rg[:, nw * 512:(nw + 1) * 512], start=False, stop=True)
                    mark(t, "TT")
                GATE_TT = cnt["TT"]

                @block.scalar
                def _(s):
                    s.wait_ge(TT, GATE_TT)
                    if c > 0:
                        s.wait_ge(VV, gn_vv[c - 1])
                    s.activation(sb_gate[:], psA[:, 0:1024], ACT.Sigmoid, scale=sb_sm[:, smo + 9:smo + 10],
                                 accum_out=sb_sm[:, smo + 5:smo + 6])
                    mark(s, "SS")
                SIG_SS = cnt["SS"]
                atok[0] = ("SS", SIG_SS)
                atok[1] = ("SS", SIG_SS)

                @block.vector
                def _(v):
                    v.wait_ge(SS, SIG_SS)
                    v.tensor_scalar(sb_sm[:, smo + 6:smo + 7], sb_sm[:, smo + 5:smo + 6], 1.0 / D, EPS, ALU.mult, ALU.add)
                    chain(v)
                    v.reciprocal(sb_sm[:, smo + 7:smo + 8], sb_sm[:, smo + 6:smo + 7])
                    chain(v)
                    v.tensor_scalar_mul(sb_gaten[:], sb_gate[:], sb_sm[:, smo + 7:smo + 8])
                    mark(v, "VV")
                GN_VV = cnt["VV"]

                def emit_proj(t, nw0, dsts):
                    for j, dst in enumerate(dsts):
                        nw = nw0 + j
                        for kc in range(NKC):
                            t.matmul(dst, zf[:, kc * 128:(kc + 1) * 128],
                                     sb_wqkv[:, kc * 3 * D + nw * 512: kc * 3 * D + (nw + 1) * 512],
                                     start=(kc == 0), stop=False)
                        t.matmul(dst, sb_ones[:, 0:128], sb_rq[:, nw * 512:(nw + 1) * 512],
                                 start=False, stop=True)

                @block.tensor
                def _(t):
                    if c == 0:
                        t.wait_ge(SS, WSC_SS)
                    wtok(t, atok[2]); wtok(t, atok[3])
                    emit_proj(t, 0, [psA[:, 1024:1536], psA[:, 1536:2048]])
                    mark(t, "TT")
                Q_TT = cnt["TT"]

                @block.tensor
                def _(t):
                    wtok(t, atok[0]); wtok(t, atok[1])
                    emit_proj(t, 2, [psA[:, 0:512], psA[:, 512:1024]])
                    mark(t, "TT")
                K_TT = cnt["TT"]
                s2state[c] = (GN_VV, Q_TT, K_TT, emit_proj)

            def emit_s2b(c):
                par = c % 2
                zf = sb_ztok[:, par * D:(par + 1) * D]
                q4s = sb_q4st[:, 0:D]
                tp = psT[:, 0:1024]
                GN_VV, Q_TT, K_TT, emit_proj = s2state[c]

                @block.scalar
                def _(s):
                    s.wait_ge(TT, Q_TT)
                    if c > 0:
                        s.wait_ge(VV, q4k4_vv[0])
                    s.activation(sb_tmp1[:], psA[:, 1024:2048], ACT.Relu)
                    mark(s, "SS")
                QR_SS = cnt["SS"]
                atok[2] = ("SS", QR_SS)
                atok[3] = ("SS", QR_SS)

                @block.vector
                def _(v):
                    v.wait_ge(SS, QR_SS)
                    v.wait_ge(VV, GN_VV)
                    v.tensor_mul(sb_tmp1[:], sb_tmp1[:], sb_gaten[:])
                    mark(v, "VV")
                QG_VV = cnt["VV"]

                @block.scalar
                def _(s):
                    s.wait_ge(TT, K_TT)
                    s.activation(sb_tmp2[:], psA[:, 0:1024], ACT.Relu)
                    mark(s, "SS")
                KR_SS = cnt["SS"]
                atok[0] = ("SS", KR_SS)
                atok[1] = ("SS", KR_SS)

                @block.vector
                def _(v):
                    v.wait_ge(SS, KR_SS)
                    v.tensor_mul(sb_tmp2[:], sb_tmp2[:], sb_gaten[:])
                    mark(v, "VV")
                QKG_VV = cnt["VV"]
                gn_vv[c] = QKG_VV

                @block.scalar
                def _(s):
                    s.wait_ge(VV, QG_VV)
                    s.activation(sb_tmp1[:], sb_tmp1[:], ACT.Square)
                    mark(s, "SS")
                QS_SS = cnt["SS"]

                @block.vector
                def _(v):
                    v.wait_ge(SS, QS_SS)
                    if c >= 2:
                        v.wait_ge(TT, q4t_tt_by[c - 2])
                    v.tensor_scalar(sb_q4tok[:, (c % 2) * D:(c % 2 + 1) * D], sb_tmp1[:], EPS, None, ALU.add)
                    mark(v, "VV")
                Q4_VV = cnt["VV"]
                q4_vv_by[c] = Q4_VV

                @block.scalar
                def _(s):
                    s.wait_ge(VV, QKG_VV)
                    s.activation(sb_tmp2[:], sb_tmp2[:], ACT.Square)
                    mark(s, "SS")
                KS_SS = cnt["SS"]

                @block.vector
                def _(v):
                    v.wait_ge(SS, KS_SS)
                    if c >= 2:
                        v.wait_ge(TT, state_tt_by[c - 2])
                    v.tensor_scalar(sb_k4tok[:, (c % 2) * D:(c % 2 + 1) * D], sb_tmp2[:], EPS, None, ALU.add)
                    mark(v, "VV")
                K4_VV = cnt["VV"]
                k4_vv_by[c] = K4_VV
                q4k4_vv[0] = cnt["VV"]

                @block.tensor
                def _(t):
                    wtok(t, atok[2]); wtok(t, atok[3])
                    emit_proj(t, 4, [psA[:, 1024:1536], psA[:, 1536:2048]])
                    mark(t, "TT")
                V_TT = cnt["TT"]
                vproj_tt[c] = V_TT
                s2tail[c] = True

                @block.scalar
                def _(s):
                    s.wait_ge(TT, V_TT)
                    s.wait_ge(VV, EPS_VV)
                    dst = sb_vtok[:, c * HEADS * 65:(c + 1) * HEADS * 65].rearrange("p (h w) -> p h w", w=65)[:, :, 0:64]
                    s.activation(dst[:, 0:8, :], psA[:, 1024:1536].rearrange("p (h w) -> p h w", w=64), ACT.Copy)
                    s.activation(dst[:, 8:16, :], psA[:, 1536:2048].rearrange("p (h w) -> p h w", w=64), ACT.Copy)
                    mark(s, "SS")
                VE_SS = cnt["SS"]
                ve_ss_by[c] = VE_SS
                atok[2] = ("SS", VE_SS)
                atok[3] = ("SS", VE_SS)

            def emit_s2c(c):
                par = c % 2
                q4s = sb_q4st[:, 0:D]
                tp = psT[:, 0:1024]
                q4slot = sb_q4tok[:, par * D:(par + 1) * D]
                k4slot = sb_k4tok[:, par * D:(par + 1) * D]

                @block.tensor
                def _(t):
                    t.wait_ge(VV, q4_vv_by[c])
                    wtok(t, atok["t0"])
                    for kc in range(NKC):
                        t.transpose(tp[:, kc * 128:(kc + 1) * 128], q4slot[:, kc * 128:(kc + 1) * 128], sb_ident[:])
                    q4t_tt_by[c] = mark(t, "TT")
                Q4T_TT = cnt["TT"]

                @block.vector
                def _(v):
                    v.wait_ge(TT, Q4T_TT)
                    if c >= 1:
                        v.wait_ge(sems["DSP"], spill_ds[c - 1])
                    v.tensor_copy(q4s, tp[:, 0:1024])
                    mark(v, "VV")
                Q4F_VV = cnt["VV"]
                atok["t0"] = ("VV", Q4F_VV)

                @block.tensor
                def _(t):
                    t.wait_ge(VV, k4_vv_by[c])
                    wtok(t, atok["t0"])
                    for kc in range(NKC):
                        t.transpose(tp[:, kc * 128:(kc + 1) * 128], k4slot[:, kc * 128:(kc + 1) * 128], sb_ident[:])
                    mark(t, "TT")
                K4T_TT = cnt["TT"]

                @block.vector
                def _(v):
                    v.wait_ge(TT, K4T_TT)
                    dst = sb_k4fm[:].rearrange("p (kc tok) -> p kc tok", tok=TC)[:, :, c * 128:(c + 1) * 128]
                    v.tensor_copy(dst, tp[:, 0:1024].rearrange("p (kc w) -> p kc w", w=128))
                    mark(v, "VV")
                K4F_VV = cnt["VV"]
                atok["t0"] = ("VV", K4F_VV)

                @block.sync
                def _(s):
                    s.wait_ge(VV, Q4F_VV)
                    s.dma_start(out=d_q4spill.ap()[c], in_=q4s).then_inc(sems["DSP"], 16)
                    bump("DSP", 16)
                spill_ds[c] = cnt["DSP"]

                @block.tensor
                def _(t):
                    t.wait_ge(SS, ve_ss_by[c])
                    if c > 0:
                        t.wait_ge(VV, snap_vv[c - 1])
                    for h in range(HEADS):
                        p, half = h // 2, h % 2
                        if p < 7:
                            dst = psSA[half * 64:(half + 1) * 64, p * 65:(p + 1) * 65]
                        else:
                            dst = psSB[half * 64:(half + 1) * 64, 0:65]
                        st = (c == 0) and (h in (0, 1, 14, 15))
                        t.matmul(dst, k4slot[:, h * 64:(h + 1) * 64],
                                 sb_vtok[:, c * HEADS * 65 + h * 65: c * HEADS * 65 + (h + 1) * 65],
                                 start=st, stop=(c == NCH - 1 and h >= 14), skip_group_check=True)
                    state_tt_by[c] = mark(t, "TT")
                state_tt[c] = cnt["TT"]

                @block.vector
                def _(v):
                    v.wait_ge(TT, state_tt[c])
                    if c < NCH - 1:
                        slot = sb_snaps[:, (c + 1) * 520:(c + 2) * 520]
                        v.tensor_copy(slot[:, 0:455], psSA[:, 0:455])
                        v.tensor_copy(slot[:, 455:520], psSB[:, 0:65])
                    else:
                        v.tensor_copy(sb_exch[:, 0:455], psSA[:, 0:455])
                        v.tensor_copy(sb_exch[:, 455:520], psSB[:, 0:65])
                    mark(v, "VV")
                snap_vv[c] = cnt["VV"]

            s2state = {}
            zf_vv = {}
            vproj_tt = {}
            s2tail = {}
            q4t_tt_by = {}
            state_tt_by = {}
            q4_vv_by = {}
            k4_vv_by = {}
            ve_ss_by = {}
            emit_s1(0)
            emit_s1b(0)
            @block.scalar
            def _(s):
                s.wait_ge(TT, rq_tt[-1])
                s.wait_ge(DS, BQ_DS)
                for kc in range(NKC):
                    w = sb_wqkv[:, kc * 3 * D:(kc + 1) * 3 * D]
                    s.activation(w, w, ACT.Copy, scale=sb_lng[:, kc:kc + 1])
                mark(s, "SS")
            WSC_SS = cnt["SS"]

            for c in range(NCH):
                if c + 1 < NCH:
                    emit_s1(c + 1)
                emit_s2a(c)
                if c + 1 < NCH:
                    emit_s1b(c + 1)
                emit_s2b(c)
                if c >= 1:
                    emit_s2c(c - 1)
            emit_s2c(NCH - 1)

            A_SNAP = snap_vv[NCH - 1]
            ALL_SPILLS = cnt["DSP"]

        with ExitStack() as dd:
            ed = dd.enter_context
            sb_wp = ed(nc.sbuf_tensor("wp_bf", [128, NKC * D], BF16))
            sb_u1 = ed(nc.sbuf_tensor("u1_bf", [128, NKC * RANK], BF16))
            sb_v1 = ed(nc.sbuf_tensor("v1_bf", [128, 2 * DFF], BF16))
            sb_u2 = ed(nc.sbuf_tensor("u2_bf", [128, 32 * RANK], BF16))
            sb_v2 = ed(nc.sbuf_tensor("v2_bf", [128, 2 * D], BF16))
            sb_q4pf = ed(nc.sbuf_tensor("q4pf", [128, 2 * D], BF16))
            sb_am = ed(nc.sbuf_tensor("am", [128, HEADS * 128], BF16))
            sb_den = ed(nc.sbuf_tensor("den", [128, 16], F32))
            sb_rec = ed(nc.sbuf_tensor("recip", [128, 16], F32))
            sb_osb = ed(nc.sbuf_tensor("osb", [128, D], BF16))
            sb_ofm = ed(nc.sbuf_tensor("ofm", [128, 2 * NKC * 256], BF16))
            sb_x2 = ed(nc.sbuf_tensor("x2", [128, 2 * D], F32))
            sb_z3 = ed(nc.sbuf_tensor("z3", [128, D], BF16))
            sb_z3fm = ed(nc.sbuf_tensor("z3fm", [128, NKC * 256], BF16))
            sb_psb = ed(nc.sbuf_tensor("psb", [128, 512], BF16))
            sb_hsb = ed(nc.sbuf_tensor("hsb", [128, 32 * 256], BF16))
            sb_qsb = ed(nc.sbuf_tensor("qsb", [128, 512], BF16))
            sb_x3 = ed(nc.sbuf_tensor("x3", [128, 2 * D], F32))
            sb_st2 = ed(nc.sbuf_tensor("st2", [128, 12], F32))
            sb_sm2 = ed(nc.sbuf_tensor("sm2", [128, 6], F32))
            sb_warm = ed(nc.sbuf_tensor("warm", [128, 1], F32))
            psC = ed(nc.psum_tensor("psC", [128, 4096], F32))

            btok = {b_: ("VV", 0) for b_ in range(8)}

            def bw(t, *banks):
                seen = set()
                for b_ in banks:
                    if btok[b_] not in seen:
                        seen.add(btok[b_])
                        wtok(t, btok[b_])

            def bs(tokval, *banks):
                for b_ in banks:
                    btok[b_] = tokval

            @block.scalar
            def _(s):
                s.wait_ge(VV, A_SNAP)
                s.dma_start(out=d_ccin.ap(), in_=sb_exch[:]).then_inc(sems["DGE"], 16)
                bump("DGE", 16)
            CCIN_DGE = cnt["DGE"]

            @block.gpsimd
            def _(g):
                g.wait_ge(sems["DGE"], CCIN_DGE)
                g.collective_compute("AllGather", ALU.bypass,
                                     replica_groups=[[0, 1], [2, 3], [4, 5], [6, 7]],
                                     ins=[d_ccin.ap().opt()],
                                     outs=[d_ccout.ap().opt()]).then_inc(CC, 1)
                bump("CC", 1)
            EXCH_CC = cnt["CC"]

            @block.scalar
            def _(s):
                s.wait_ge(CC, EXCH_CC)
                s.dma_start(out=sb_exch[:], in_=d_ccout.ap()[0]).then_inc(sems["DGE"], 16)
                bump("DGE", 16)
            EXCH_DG = cnt["DGE"]

            snapfix_vv = {}

            @block.vector
            def _(v):
                v.wait_ge(sems["DGE"], EXCH_DG)
                v.tensor_scalar_mul(sb_exch[:], sb_exch[:], sb_wprev[:])
                chain(v)
                v.tensor_copy(sb_snaps[:, 0:520], sb_exch[:])
                snapfix_vv[0] = mark(v, "VV")
                for c in range(1, NCH):
                    slot = sb_snaps[:, c * 520:(c + 1) * 520]
                    v.tensor_add(slot, slot, sb_exch[:])
                    snapfix_vv[c] = mark(v, "VV")
            SNAPFIX_VV = cnt["VV"]

            @block.gpsimd
            def _(g):
                g.wait_ge(VV, A_SNAP)
                g.wait_ge(sems["DGE"], 32)
                for kc in range(NKC):
                    g.dma_start(out=sb_wp[:, kc * D:(kc + 1) * D],
                                in_=d_wp[kc * 128:(kc + 1) * 128, :]).then_inc(DG, 16)
                    g.dma_start(out=sb_u1[:, kc * RANK:(kc + 1) * RANK],
                                in_=d_u1[kc * 128:(kc + 1) * 128, :]).then_inc(DG, 16)
                    bump("DG", 32)
                for rc in range(2):
                    g.dma_start(out=sb_v1[:, rc * DFF:(rc + 1) * DFF],
                                in_=d_v1[rc * 128:(rc + 1) * 128, :],
                                max_dma_last_dim=8000).then_inc(DG, 16)
                    g.dma_start(out=sb_v2[:, rc * D:(rc + 1) * D],
                                in_=d_v2[rc * 128:(rc + 1) * 128, :]).then_inc(DG, 16)
                    bump("DG", 32)
            DW_DG = cnt["DG"]

            @block.gpsimd
            def _(g):
                for fc in range(32):
                    g.dma_start(out=sb_u2[:, fc * RANK:(fc + 1) * RANK],
                                in_=d_u2[fc * 128:(fc + 1) * 128, :]).then_inc(sems["DG2"], 16)
                    bump("DG2", 16)
            DW_DG2 = cnt["DG2"]


            if DEBUG_TAPS:
                @block.sync
                def _(s):
                    s.wait_ge(VV, SNAPFIX_VV)
                    s.dma_start(out=taps["t_snap"], in_=sb_snaps[:, 520:1040]).then_inc(ST, 16)
                    bump("ST", 16)

            q4use_tt = {}
            ofm_ss = {}
            proj_done = [0]
            ot_tt = {}
            am_vv = {}
            div_vv = {}

            def c_a(c):
                par = c % 2
                q4p = sb_q4pf[:, par * D:(par + 1) * D]
                dsq = "DSQ0" if par == 0 else "DSQ1"

                @block.sync
                def _(s):
                    if c >= 2:
                        s.wait_ge(TT, q4use_tt[c - 2])
                    else:
                        s.wait_ge(sems["DSP"], ALL_SPILLS)
                    s.dma_start(out=q4p, in_=d_q4spill.ap()[c]).then_inc(sems[dsq], 16)
                    bump(dsq, 16)
                Q4P_DS = cnt[dsq]

                k4 = sb_k4fm[:].rearrange("p (kc tok) -> p kc tok", tok=TC)
                q4 = q4p.rearrange("p (kc w) -> p kc w", w=128)

                @block.tensor
                def _(t):
                    t.wait_ge(sems[dsq], Q4P_DS)
                    bw(t, 0, 1, 2, 3)
                    for p in range(NPAIR):
                        t.matmul(psC[:, p * 128:(p + 1) * 128],
                                 k4[0:64, p, c * 128:(c + 1) * 128], q4[0:64, p, :],
                                 start=True, stop=True, tile_position=(0, 0))
                        t.matmul(psC[:, 1024 + p * 128:1024 + (p + 1) * 128],
                                 k4[64:128, p, c * 128:(c + 1) * 128], q4[64:128, p, :],
                                 start=True, stop=True, tile_position=(64, 0))
                    mark(t, "TT")
                ATT_TT = cnt["TT"]

                @block.vector
                def _(v):
                    v.wait_ge(TT, ATT_TT)
                    amv = sb_am[:].rearrange("p (h w) -> p h w", w=128)
                    mb = sb_mask[:].unsqueeze(1).to_broadcast([128, 8, 128])
                    v.tensor_tensor(amv[:, 0::2, :], psC[:, 0:1024].rearrange("p (h w) -> p h w", w=128), mb, ALU.mult)
                    v.tensor_tensor(amv[:, 1::2, :], psC[:, 1024:2048].rearrange("p (h w) -> p h w", w=128), mb, ALU.mult)
                    mark(v, "VV")
                am_vv[c] = cnt["VV"]
                bs(("VV", am_vv[c]), 0, 1, 2, 3)

            def c_b(c):
                par = c % 2
                q4p = sb_q4pf[:, par * D:(par + 1) * D]
                q4 = q4p.rearrange("p (kc w) -> p kc w", w=128)

                @block.tensor
                def _(t):
                    t.wait_ge(VV, am_vv[c])
                    for h in range(HEADS):
                        i = h // 2
                        slot = (h % 2) * 1024 + i * 128
                        t.matmul(psC[:, slot:slot + 65], sb_am[:, h * 128:(h + 1) * 128],
                                 sb_vtok[:, c * HEADS * 65 + h * 65: c * HEADS * 65 + (h + 1) * 65],
                                 start=(h in (0, 1, 8, 9)), stop=False, skip_group_check=True)
                    t.wait_ge(VV, snapfix_vv[c])
                    for p in range(NPAIR):
                        off = c * 520 + (p * 65 if p < 7 else 455)
                        t.matmul(psC[:, p * 128:p * 128 + 65],
                                 q4[0:64, p, :], sb_snaps[0:64, off:off + 65],
                                 start=False, stop=True, tile_position=(0, 0), skip_group_check=True)
                        t.matmul(psC[:, 1024 + p * 128:1024 + p * 128 + 65],
                                 q4[64:128, p, :], sb_snaps[64:128, off:off + 65],
                                 start=False, stop=True, tile_position=(64, 0), skip_group_check=True)
                    mark(t, "TT")
                NUMM_TT = cnt["TT"]
                q4use_tt[c] = NUMM_TT

                @block.vector
                def _(v):
                    v.wait_ge(TT, NUMM_TT)
                    if c >= 1:
                        v.wait_ge(TT, ot_tt[c - 1])
                    dd_ = psC[:, 0:2048].rearrange("p (sl w) -> p sl w", w=128)[:, :, 64:65]
                    v.tensor_scalar(sb_den[:].unsqueeze(2), dd_, EPS, None, ALU.add)
                    chain(v)
                    v.reciprocal(sb_rec[:], sb_den[:])
                    chain(v)
                    ov = sb_osb[:].rearrange("p (h w) -> p h w", w=64)
                    ne = psC[:, 0:1024].rearrange("p (sl w) -> p sl w", w=128)[:, :, 0:64]
                    no = psC[:, 1024:2048].rearrange("p (sl w) -> p sl w", w=128)[:, :, 0:64]
                    v.tensor_tensor(ov[:, 0::2, :], ne,
                                    sb_rec[:, 0:8].unsqueeze(2).to_broadcast([128, 8, 64]), ALU.mult)
                    v.tensor_tensor(ov[:, 1::2, :], no,
                                    sb_rec[:, 8:16].unsqueeze(2).to_broadcast([128, 8, 64]), ALU.mult)
                    mark(v, "VV")
                div_vv[c] = cnt["VV"]
                bs(("VV", div_vv[c]), 0, 1, 2, 3)

            def c_c(c):
                @block.tensor
                def _(t):
                    t.wait_ge(VV, div_vv[c])
                    bw(t, 0)
                    dst = psC[:, 0:512].bitcast(BF16)
                    for kc in range(NKC):
                        t.transpose(dst[:, kc * 128:(kc + 1) * 128], sb_osb[:, kc * 128:(kc + 1) * 128], sb_ident[:])
                    mark(t, "TT")
                OT_TT = cnt["TT"]
                ot_tt[c] = OT_TT

            def c_c_v(c):
                OT_TT = ot_tt[c]

                @block.vector
                def _(v):
                    v.wait_ge(TT, OT_TT)
                    v.wait_ge(TT, proj_done[0])
                    src = psC[:, 0:512].bitcast(BF16).rearrange("p (kc w) -> p kc w", w=128)
                    hpar = (c // 2) % 2
                    dst = sb_ofm[:, hpar * NKC * 256:(hpar + 1) * NKC * 256]
                    dst = dst.rearrange("p (kc w) -> p kc w", w=256)[:, :, (c % 2) * 128:(c % 2) * 128 + 128]
                    v.tensor_copy(dst, src)
                    mark(v, "VV")
                bs(("VV", cnt["VV"]), 0)
                ofm_ss[c] = cnt["VV"]

            def d_proj(hs, j):
                c = hs * 2 + j
                hpar = hs % 2
                x2sl = sb_x2[:, j * D:(j + 1) * D]

                @block.sync
                def _(s):
                    if hs > 0:
                        s.wait_ge(VV, x2_free[j])
                    else:
                        s.wait_ge(sems["DSP"], ALL_SPILLS)
                    s.dma_start(out=x2sl, in_=d_x[c * 128:(c + 1) * 128, :]).then_inc(sems["DSXD"], 16)
                    bump("DSXD", 16)
                XD_DS = cnt["DSXD"]

                proj_nw = []

                @block.tensor
                def _(t):
                    t.wait_ge(VV, ofm_ss[hs * 2 + 1])
                    t.wait_ge(DG, DW_DG)
                    ofmv = sb_ofm[:, hpar * NKC * 256:(hpar + 1) * NKC * 256]
                    for nw in range(2):
                        bw(t, 4 + nw)
                        dst = psC[:, 2048 + nw * 512:2048 + (nw + 1) * 512]
                        for kc in range(NKC):
                            t.matmul(dst, ofmv[:, kc * 256 + j * 128: kc * 256 + (j + 1) * 128],
                                     sb_wp[:, kc * D + nw * 512:kc * D + (nw + 1) * 512],
                                     start=(kc == 0), stop=False)
                        t.matmul(dst, sb_ones[:, 0:128], sb_rp[:, nw * 512:(nw + 1) * 512],
                                 start=False, stop=True)
                        proj_nw.append(mark(t, "TT"))
                PROJ_TT = proj_nw[1]
                proj_done[0] = PROJ_TT

                add_m = []

                @block.vector
                def _(v):
                    v.wait_ge(TT, proj_nw[0])
                    v.wait_ge(sems["DSXD"], XD_DS)
                    v.tensor_add(x2sl[:, 0:512], psC[:, 2048:2560], x2sl[:, 0:512])
                    chain(v)
                    add_m.append(cnt["VV"])
                    v.bn_stats(sb_st2[:, 0:6], x2sl[:, 0:512])
                    v.wait_ge(TT, PROJ_TT)
                    v.tensor_add(x2sl[:, 512:1024], psC[:, 2560:3072], x2sl[:, 512:1024])
                    chain(v)
                    add_m.append(cnt["VV"])
                    v.bn_stats(sb_st2[:, 6:12], x2sl[:, 512:1024])
                    chain(v)
                    v.bn_aggr(sb_sm2[:, 0:2], sb_st2[:])
                    chain(v)
                    v.tensor_scalar(sb_sm2[:, 2:3], sb_sm2[:, 1:2], EPS, None, ALU.add)
                    chain(v)
                    v.reciprocal(sb_sm2[:, 3:4], sb_sm2[:, 2:3])
                    mark(v, "VV")
                LN3_VV = cnt["VV"]
                bs(("VV", add_m[0]), 4)
                bs(("VV", add_m[1]), 5)

                @block.scalar
                def _(s):
                    if j == 0:
                        s.activation(sb_warm[:, 0:1], sb_eps[:, 0:1], ACT.Sqrt)
                    s.wait_ge(VV, LN3_VV)
                    s.activation(sb_sm2[:, 4:5], sb_sm2[:, 3:4], ACT.Sqrt)
                    mark(s, "SS")
                d_state[(hs, j)] = (cnt["SS"], x2sl)

            def d_z3(hs, j):
                R3_SS, x2sl = d_state[(hs, j)]

                z3h = []

                @block.vector
                def _(v):
                    v.wait_ge(SS, R3_SS)
                    v.tensor_scalar(sb_sm2[:, 5:6], sb_sm2[:, 0:1], sb_sm2[:, 4:5], -1.0, ALU.mult, ALU.mult)
                    chain(v)
                    if hs + j > 0:
                        v.wait_ge(TT, z3t_used[0])
                    v.tensor_scalar(sb_z3[:, 0:512], x2sl[:, 0:512], sb_sm2[:, 4:5], sb_sm2[:, 5:6], ALU.mult, ALU.add)
                    z3h.append(mark(v, "VV"))
                    v.tensor_scalar(sb_z3[:, 512:1024], x2sl[:, 512:1024], sb_sm2[:, 4:5], sb_sm2[:, 5:6], ALU.mult, ALU.add)
                    z3h.append(mark(v, "VV"))

                @block.tensor
                def _(t):
                    t.wait_ge(VV, z3h[0])
                    bw(t, 6)
                    dst = psC[:, 3072:3584].bitcast(BF16)
                    for kc in range(4):
                        t.transpose(dst[:, kc * 128:(kc + 1) * 128], sb_z3[:, kc * 128:(kc + 1) * 128], sb_ident[:])
                    t.wait_ge(VV, z3h[1])
                    for kc in range(4, NKC):
                        t.transpose(dst[:, kc * 128:(kc + 1) * 128], sb_z3[:, kc * 128:(kc + 1) * 128], sb_ident[:])
                    z3t_used[0] = mark(t, "TT")
                Z3T_TT = cnt["TT"]

                @block.vector
                def _(v):
                    v.wait_ge(TT, Z3T_TT)
                    src = psC[:, 3072:3584].bitcast(BF16).rearrange("p (kc w) -> p kc w", w=128)
                    dst = sb_z3fm[:].rearrange("p (kc w) -> p kc w", w=256)[:, :, j * 128:(j + 1) * 128]
                    v.tensor_copy(dst, src)
                    mark(v, "VV")
                bs(("VV", cnt["VV"]), 6)
                z3f_vv[j] = cnt["VV"]

            def d_ffn(hs):
                TW = 256

                @block.tensor
                def _(t):
                    t.wait_ge(VV, z3f_vv[1])
                    bw(t, 7)
                    for mc in range(2):
                        dst = psC[:, 3584 + mc * TW:3584 + (mc + 1) * TW]
                        for kc in range(NKC):
                            t.matmul(dst, sb_u1[:, kc * RANK + mc * 128:kc * RANK + (mc + 1) * 128],
                                     sb_z3fm[:, kc * TW:(kc + 1) * TW],
                                     start=(mc == 0 and kc == 0), stop=(kc == NKC - 1), skip_group_check=True)
                    mark(t, "TT")
                U1_TT = cnt["TT"]

                @block.vector
                def _(v):
                    v.wait_ge(TT, U1_TT)
                    v.tensor_copy(sb_psb[:], psC[:, 3584:4096])
                    mark(v, "VV")
                P_VV = cnt["VV"]
                bs(("VV", P_VV), 7)

                ss_base = cnt["SS"]
                v1_tt = []

                @block.tensor
                def _(t):
                    t.wait_ge(VV, P_VV)
                    t.wait_ge(sems["DG2"], DW_DG2)
                    bw(t, 4, 5, 6)
                    for pr in range(16):
                        if pr >= 2:
                            t.wait_ge(SS, ss_base + pr - 1)
                        for half in range(2):
                            fc = pr * 2 + half
                            dst = psC[:, 2048 + (pr % 2) * 512 + half * TW:
                                      2048 + (pr % 2) * 512 + (half + 1) * TW]
                            for rc in range(2):
                                t.matmul(dst, sb_v1[:, rc * DFF + fc * 128:rc * DFF + (fc + 1) * 128],
                                         sb_psb[:, rc * TW:(rc + 1) * TW],
                                         start=(rc == 0), stop=(rc == 1))
                        v1_tt.append(mark(t, "TT"))
                        if pr >= 2:
                            for half in range(2):
                                fc = (pr - 2) * 2 + half
                                for mc in range(2):
                                    t.matmul(psC[:, 3072 + mc * TW:3072 + (mc + 1) * TW],
                                             sb_u2[:, fc * RANK + mc * 128:fc * RANK + (mc + 1) * 128],
                                             sb_hsb[:, fc * TW:(fc + 1) * TW],
                                             start=(fc == 0 and mc == 0), stop=False, skip_group_check=True)
                    for prl in (14, 15):
                        t.wait_ge(SS, ss_base + prl + 1)
                        for half in range(2):
                            fc = prl * 2 + half
                            for mc in range(2):
                                t.matmul(psC[:, 3072 + mc * TW:3072 + (mc + 1) * TW],
                                         sb_u2[:, fc * RANK + mc * 128:fc * RANK + (mc + 1) * 128],
                                         sb_hsb[:, fc * TW:(fc + 1) * TW],
                                         start=False, stop=(fc == 31 and mc == 1), skip_group_check=True)
                    mark(t, "TT")
                U2_TT = cnt["TT"]

                @block.scalar
                def _(s):
                    s.activation(sb_warm[:, 0:1], sb_eps[:, 0:1], ACT.Gelu_apprx_tanh)
                    for pr in range(16):
                        s.wait_ge(TT, v1_tt[pr])
                        src = psC[:, 2048 + (pr % 2) * 512:2048 + (pr % 2) * 512 + 512]
                        s.activation(sb_hsb[:, pr * 2 * TW:pr * 2 * TW + TW], src[:, 0:TW],
                                     ACT.Gelu_apprx_tanh, bias=sb_b1c[:, 2 * pr:2 * pr + 1])
                        s.activation(sb_hsb[:, pr * 2 * TW + TW:(pr + 1) * 2 * TW], src[:, TW:2 * TW],
                                     ACT.Gelu_apprx_tanh, bias=sb_b1c[:, 2 * pr + 1:2 * pr + 2])
                        assert mark(s, "SS") == ss_base + pr + 1
                bs(("SS", cnt["SS"]), 4, 5)
                return U2_TT

            def d_out(hs, U2_TT):
                TW = 256

                @block.vector
                def _(v):
                    v.wait_ge(TT, U2_TT)
                    v.tensor_copy(sb_qsb[:], psC[:, 3072:3584])
                    mark(v, "VV")
                QQ_VV = cnt["VV"]
                bs(("VV", QQ_VV), 6)

                for j in range(2):
                    c = hs * 2 + j
                    xpar = c % 2
                    x3b = sb_x3[:, xpar * D:(xpar + 1) * D]
                    dso = "DSO" if xpar == 0 else "DSO1"
                    _, x2sl = d_state[(hs, j)]

                    v2_nw = []

                    @block.tensor
                    def _(t):
                        t.wait_ge(VV, QQ_VV)
                        for nw in range(2):
                            bw(t, 4 + nw)
                            dst = psC[:, 2048 + nw * 512:2048 + (nw + 1) * 512]
                            for rc in range(2):
                                t.matmul(dst, sb_qsb[:, rc * TW + j * 128:rc * TW + (j + 1) * 128],
                                         sb_v2[:, rc * D + nw * 512:rc * D + (nw + 1) * 512],
                                         start=(rc == 0), stop=False)
                            t.matmul(dst, sb_ones[:, 0:128], sb_rb2[:, nw * 512:(nw + 1) * 512],
                                     start=False, stop=True)
                            v2_nw.append(mark(t, "TT"))

                    x3_m = []

                    @block.vector
                    def _(v):
                        v.wait_ge(TT, v2_nw[0])
                        if c >= 2:
                            v.wait_ge(sems[dso], x3_ds[xpar])
                        v.tensor_add(x3b[:, 0:512], psC[:, 2048:2560], x2sl[:, 0:512])
                        mark(v, "VV")
                        x3_m.append(cnt["VV"])
                        v.wait_ge(TT, v2_nw[1])
                        v.tensor_add(x3b[:, 512:1024], psC[:, 2560:3072], x2sl[:, 512:1024])
                        mark(v, "VV")
                        x3_m.append(cnt["VV"])
                    X3_VV = x3_m[1]
                    bs(("VV", x3_m[0]), 4)
                    bs(("VV", x3_m[1]), 5)
                    x2_free[j] = X3_VV

                    @block.scalar
                    def _(s):
                        s.wait_ge(VV, X3_VV)
                        s.dma_start(out=d_out_ap[c * 128:(c + 1) * 128, :], in_=x3b).then_inc(sems[dso], 16)
                        x3_ds[xpar] = bump(dso, 16)

            d_state = {}
            x2_free = {0: 0, 1: 0}
            z3t_used = [0]
            z3f_vv = {}
            x3_ds = {0: 0, 1: 0}
            d_out_ap = d_out_dram
            for c0 in (0, 1):
                c_a(c0); c_b(c0); c_c(c0); c_c_v(c0)
            for hs in range(8):
                if hs < 7:
                    c_a(2 * hs + 2)
                d_proj(hs, 0)
                if hs < 7:
                    c_b(2 * hs + 2)
                    c_c(2 * hs + 2)
                d_z3(hs, 0)
                if hs < 7:
                    c_c_v(2 * hs + 2)
                    c_a(2 * hs + 3)
                d_proj(hs, 1)
                if hs < 7:
                    c_b(2 * hs + 3)
                    c_c(2 * hs + 3)
                d_z3(hs, 1)
                if hs < 7:
                    c_c_v(2 * hs + 3)
                u2tt = d_ffn(hs)
                d_out(hs, u2tt)

            @block.sync
            def _(s):
                for nm in ["DS", "DSX0", "DSX1", "DSP", "DSQ0", "DSQ1", "DSXD", "DSO", "DSO1", "DG", "DGE", "DG2", "ST"]:
                    if cnt[nm]:
                        s.wait_ge(sems[nm], cnt[nm])

    return nc


def kernel(L0_attn_x, L0_attn_ln_g, L0_attn_ln_b, L0_attn_W_qkv, L0_attn_b_qkv,
           L0_attn_W_gate, L0_attn_b_gate, L0_attn_W_proj, L0_attn_b_proj,
           L0_ffn_U1, L0_ffn_V1, L0_ffn_b1, L0_ffn_U2, L0_ffn_V2, L0_ffn_b2):
    x = np.ascontiguousarray(np.asarray(L0_attn_x, dtype=np.float32))
    nc = build_program()
    ident16 = np.eye(128, dtype=BF)
    ident32 = np.eye(128, dtype=np.float32)
    mask = np.ascontiguousarray(np.triu(np.ones((128, 128), np.float32)))
    common = {
        "wqkv": np.ascontiguousarray(np.asarray(L0_attn_W_qkv, np.float32)),
        "wg": np.ascontiguousarray(np.asarray(L0_attn_W_gate, np.float32)),
        "wp": np.ascontiguousarray(np.asarray(L0_attn_W_proj, np.float32)),
        "u1": np.ascontiguousarray(np.asarray(L0_ffn_U1, np.float32)),
        "v1": np.ascontiguousarray(np.asarray(L0_ffn_V1, np.float32)),
        "u2": np.ascontiguousarray(np.asarray(L0_ffn_U2, np.float32)),
        "v2": np.ascontiguousarray(np.asarray(L0_ffn_V2, np.float32)),
        "lng": np.ascontiguousarray(np.asarray(L0_attn_ln_g, np.float32).reshape(8, 128).T),
        "lnb": np.ascontiguousarray(np.asarray(L0_attn_ln_b, np.float32).reshape(8, 128).T),
        "bqkv": np.ascontiguousarray(np.asarray(L0_attn_b_qkv, np.float32).reshape(1, 3 * D)),
        "bg": np.ascontiguousarray(np.asarray(L0_attn_b_gate, np.float32).reshape(1, D)),
        "bp": np.ascontiguousarray(np.asarray(L0_attn_b_proj, np.float32).reshape(1, D)),
        "b1c": np.ascontiguousarray(np.asarray(L0_ffn_b1, np.float32).reshape(32, 128).T),
        "b2": np.ascontiguousarray(np.asarray(L0_ffn_b2, np.float32).reshape(1, D)),
        "id16": ident16,
        "id32": ident32,
        "mask": mask,
    }
    in_maps = []
    for core in range(8):
        b, h = core // 2, core % 2
        m = dict(common)
        m["x"] = np.ascontiguousarray(x[b, h * TC:(h + 1) * TC, :])
        m["wprev"] = np.full((128, 1), float(h), np.float32)
        in_maps.append(m)
    res = run_bass_kernel_spmd(nc, in_maps, core_ids=list(range(8)))
    out = np.empty((B, T, D), np.float32)
    for core in range(8):
        b, h = core // 2, core % 2
        out[b, h * TC:(h + 1) * TC, :] = res.results[core]["out"]
    return out
